# version 17
# baseline (speedup 1.0000x reference)
import contextlib
import numpy as np
import concourse.bass as bass
import concourse.mybir as mybir
from concourse.bass_utils import run_bass_kernel_spmd

F32 = mybir.dt.float32
BF16 = mybir.dt.bfloat16
AF = mybir.ActivationFunctionType
ALU = mybir.AluOpType
AX = mybir.AxisListType
AP = bass.AP

D = 1024
T = 2048
NB = 2
NCH = 8
EPS = 1e-5
BIGNEG = -3.0e38

L1F_LAYERS = 6
L1F_SKIP_FFN = False
L1F_SKIP_ATTN = False
L1F_SKIP_CORE = False

ENGINES = ("sync", "tensor", "scalar", "vector", "gpsimd")
DMA_ENGINES = ("sync", "scalar", "gpsimd")


class Slot:
    __slots__ = ("name", "writer", "readers")

    def __init__(self, name):
        self.name = name
        self.writer = None
        self.readers = []

    def __repr__(self):
        return f"<{self.name}>"


class _Op:
    __slots__ = ("fn", "deps", "is_dma", "desc", "qkey")

    def __init__(self, fn, deps, is_dma, desc):
        self.fn = fn
        self.deps = deps
        self.is_dma = is_dma
        self.desc = desc


class Prog:
    self_sync = True

    def __init__(self):
        self.ops = {e: [] for e in ENGINES}
        self.marks = {e: set() for e in ENGINES}
        self.qcum = {}
        self.qfinal = {}
        self._slots = {}
        self._barrier = {}

    def slot(self, name):
        s = self._slots.get(name)
        if s is None:
            s = self._slots[name] = Slot(name)
        return s

    def barrier(self):
        for e in ENGINES:
            last = getattr(self, "_last_compute", {}).get(e)
            if last is not None:
                self._barrier[("c", e)] = last
        for k, v in self.qcum.items():
            self._barrier[("q", k)] = v

    def do(self, eng, fn, reads=(), writes=(), dma=False, qkey=None, desc=""):
        assert eng in ENGINES, eng
        assert not dma or (eng in DMA_ENGINES and qkey is not None)
        idx = len(self.ops[eng])
        deps = {}

        def add_dep(kind, e2, v):
            if kind == "c" and e2 == eng and not self.self_sync:
                return
            key = (kind, e2)
            if deps.get(key, -1) < v:
                deps[key] = v

        for (kind, e2), v in self._barrier.items():
            add_dep(kind, e2, v)
        for s in reads:
            if s.writer:
                add_dep(*s.writer)
        for s in writes:
            if s.writer:
                add_dep(*s.writer)
            for r in s.readers:
                add_dep(*r)
        op = _Op(fn, deps, dma, desc)
        op.qkey = qkey
        self.ops[eng].append(op)
        if not hasattr(self, "_last_compute"):
            self._last_compute = {}
        if not dma:
            self._last_compute[eng] = idx
        if dma:
            self.qcum[qkey] = self.qcum.get(qkey, 0) + 1
            tag = ("q", qkey, self.qcum[qkey])
        else:
            tag = ("c", eng, idx)
        for key, v in deps.items():
            if key[0] == "c":
                self.marks[key[1]].add(v)
        for s in reads:
            s.readers.append(tag)
        for s in writes:
            s.writer = tag
            s.readers = []
        return idx

    def replay(self, nc, block, sems):
        markcum = {}
        for e in ENGINES:
            marked = sorted(self.marks[e])
            cum, c, j = [], 0, 0
            for i in range(len(self.ops[e])):
                if j < len(marked) and marked[j] == i:
                    c += 1
                    j += 1
                cum.append(c)
            markcum[e] = cum
        csem = {e: sems["c_" + e] for e in ENGINES}
        qsem = {k: sems["q_" + k] for k in self.qcum}

        def make_body(e):
            ops, marks = self.ops[e], self.marks[e]

            def body(eng):
                ensured = {}
                for i, op in enumerate(ops):
                    for key, v in sorted(op.deps.items(), key=str):
                        if key[0] == "c":
                            sem, val = csem[key[1]], markcum[key[1]][v]
                        else:
                            k = key[1]
                            sem = qsem[k]
                            val = 16 * (self.qcum[k] if self.qfinal.get(k) else v)
                        if ensured.get(sem.name, -1) < val:
                            eng.wait_ge(sem, val)
                            ensured[sem.name] = val
                    ins = op.fn(eng)
                    if op.is_dma:
                        ins.then_inc(qsem[op.qkey], 16)
                    elif i in marks:
                        ins.then_inc(csem[e], 1)
            return body

        for e in ENGINES:
            if self.ops[e]:
                getattr(block, e)(make_body(e))

    def nops(self):
        return {e: len(self.ops[e]) for e in ENGINES}


def _f32(x):
    return np.ascontiguousarray(x, dtype=np.float32)


def col128(v, pad_chunks=None):
    v = np.asarray(v, np.float32).ravel()
    n = v.shape[0]
    nch = (n + 127) // 128 if pad_chunks is None else pad_chunks
    out = np.zeros((nch * 128,), np.float32)
    out[:n] = v
    return _f32(out.reshape(nch, 128).T)


def host_constants():
    c = {}
    c["c_eye"] = _f32(np.eye(128))
    psw = np.zeros((128, 128), np.float32)
    for p in range(128):
        psw[p, p ^ 1] = 1.0
    c["c_pswap_l0"] = psw
    prh = np.zeros((128, 128), np.float32)
    for blk in range(2):
        b0 = blk * 64
        for i in range(32):
            prh[b0 + i, b0 + 32 + i] = 1.0
            prh[b0 + 32 + i, b0 + i] = 1.0
    c["c_pswap_l1"] = prh
    sel2 = np.zeros((128, 2), np.float32)
    sel2[0:64, 0] = 1.0
    sel2[64:128, 1] = 1.0
    c["c_sel2"] = sel2
    sel2T = np.zeros((2, 128), np.float32)
    sel2T[0, 0:64] = 1.0
    sel2T[1, 64:128] = 1.0
    c["c_sel2T"] = sel2T
    mcur = np.where(np.arange(128)[:, None] > np.arange(128)[None, :], BIGNEG, 0.0)
    c["c_mask_curT"] = _f32(mcur.T)
    mprev = np.where(np.arange(128)[:, None] < np.arange(128)[None, :], BIGNEG, 0.0)
    c["c_mask_prevT"] = _f32(mprev.T)
    c["c_ones_col"] = _f32(np.ones((128, 1)))
    c["c_ones_row"] = _f32(np.ones((128, 128)))
    c["c_ones512"] = _f32(np.ones((1, 512)))
    c["c_onesbf"] = _f32(np.ones((1, 128)))
    c["c_bigrow"] = _f32(np.full((1, 128), BIGNEG))
    e64 = np.zeros((128, 64), np.float32)
    e64[64, :] = 1.0
    c["c_e64"] = e64
    c["c_sel2T32"] = c["c_sel2T"]
    c["c_sel232"] = c["c_sel2"]
    fc = np.zeros((128, 16), np.float32)
    for i, v in enumerate(FCONST_VALS):
        fc[:, i] = v
    c["c_fconsts"] = fc
    return c


FCONST_VALS = [1e-5, 1.0, 2048.0, 10.0, -10.0, 0.125, 220.0, 1e-30]

CONST_SHAPES = {
    "c_fconsts": ([128, 16], "f32"),
    "c_eye": ([128, 128], "bf"), "c_pswap_l0": ([128, 128], "bf"),
    "c_pswap_l1": ([128, 128], "bf"), "c_sel2": ([128, 2], "bf"),
    "c_sel2T": ([2, 128], "bf"), "c_mask_curT": ([128, 128], "bf"),
    "c_mask_prevT": ([128, 128], "bf"), "c_ones_col": ([128, 1], "bf"),
    "c_ones_row": ([128, 128], "f32"),
    "c_ones512": ([1, 512], "bf"),
    "c_onesbf": ([1, 128], "bf"),
    "c_bigrow": ([1, 128], "bf"),
    "c_e64": ([128, 64], "f32"),
    "c_sel2T32": ([2, 128], "f32"),
    "c_sel232": ([128, 2], "f32"),
}


def host_l0_rope_consts():
    import jax
    import jax.numpy as jnp
    cpu = jax.devices("cpu")[0]
    with jax.default_device(cpu):
        return _host_l0_rope_consts_impl(jnp)


def _host_l0_rope_consts_impl(jnp):
    top = 2595.0 * jnp.log10(jnp.asarray(1.0 + 4000.0 / 200.0, dtype=jnp.float32))
    mel = jnp.power(10.0, jnp.linspace(0.0, top, 32, dtype=jnp.float32) / 2595.0) - 1.0
    fb = np.asarray(200.0 * mel / 1000.0, dtype=np.float32)
    col = np.zeros((128, 1), np.float32)
    sign = np.zeros((128, 1), np.float32)
    for p in range(128):
        col[p, 0] = fb[(p % 64) // 2]
        sign[p, 0] = -1.0 if (p % 2) == 0 else 1.0
    return col, sign


def host_l1f_rotary():
    import jax
    import jax.numpy as jnp
    cpu = jax.devices("cpu")[0]
    with jax.default_device(cpu):
        return _host_l1f_rotary_impl(jnp)


def _host_l1f_rotary_impl(jnp):
    DIM_HEAD, WINDOW, n = 64, 256, T
    half = jnp.arange(0, DIM_HEAD, 2, dtype=jnp.float32)
    inv_freq = 1.0 / 10000.0 ** (half / DIM_HEAD)
    t = jnp.arange(n, dtype=jnp.float32)
    freqs = t[:, None] * inv_freq[None, :]
    freqs = jnp.concatenate([freqs, freqs], -1)
    base_scale = (half + 0.4 * DIM_HEAD) / (1.4 * DIM_HEAD)
    power = (t - n // 2) / (WINDOW // 2)
    scale = base_scale[None, :] ** power[:, None]
    scale = jnp.concatenate([scale, scale], -1)
    cos, sin = jnp.cos(freqs), jnp.sin(freqs)
    inv = 1.0 / scale
    sgn = np.ones((64, 1), np.float32)
    sgn[0:32] = -1.0
    tabs = []
    for tbl in (np.asarray(cos * scale).T, np.asarray(sin * scale).T * sgn,
                np.asarray(cos * inv).T, np.asarray(sin * inv).T * sgn):
        tabs.append(_f32(np.concatenate([tbl, tbl], axis=0)))
    return tabs


def _cw_consts():
    c_rem = 2 * np.pi
    cs = []
    for _ in range(6):
        ci = np.float32(c_rem)
        m, e = np.frexp(ci)
        ci = np.float32(np.ldexp(np.round(float(m) * 64) / 64.0, int(e)))
        cs.append(float(ci))
        c_rem -= float(ci)
    return cs


CW = _cw_consts()
INV_2PI = float(np.float32(1.0 / (2 * np.pi)))
MAGIC = 12582912.0


def x_to_fm(x):
    out = []
    for core in range(8):
        xb = x[core * NB:(core + 1) * NB]
        out.append(_f32(np.transpose(xb, (0, 2, 1))))
    return out


def fm_to_x(parts):
    return _f32(np.concatenate([np.transpose(np.asarray(p, np.float32), (0, 2, 1))
                                for p in parts], axis=0))


class Ctx:
    SBUF_BASE = 16512
    SBUF_LIMIT = 212832

    def __init__(self, nc, P):
        self.nc = nc
        self.P = P
        self.dram = {}
        self.t = {}
        self._top = 0
        self._ptop = 0
        self._pofs = {}
        self._uid = 0

    def _alloc(self, name, shape, dtype, off):
        self._uid += 1
        h = self.nc.alloc_sbuf_tensor_at(f"{name}__{self._uid}", list(shape), dtype,
                                         offset=off + self.SBUF_BASE)
        self.t[name] = h
        return h

    @staticmethod
    def _sz(shape, dtype):
        sz = int(np.prod(shape[1:])) * (4 if dtype == F32 else 2)
        return (sz + 31) & ~31

    def galloc(self, name, shape, dtype):
        h = self._alloc(name, shape, dtype, self._top)
        self._top += self._sz(shape, dtype)
        assert self._top <= self.SBUF_LIMIT, (name, self._top)
        return h

    def phase_begin(self):
        self._ptop = (self._top + 255) & ~255
        self._pofs = {}

    def palloc(self, name, shape, dtype, at=None, off2=0):
        if at is not None:
            off = self._pofs[at] + off2
            assert off2 + self._sz(shape, dtype) <= self._pofs_sz[at], (name, at)
        else:
            off = self._ptop
            self._ptop += self._sz(shape, dtype)
            assert self._ptop <= self.SBUF_LIMIT, (name, self._ptop)
        if not hasattr(self, "_pofs_sz"):
            self._pofs_sz = {}
        self._pofs[name] = off
        self._pofs_sz[name] = self._sz(shape, dtype)
        return self._alloc(name, shape, dtype, off)

    def ps(self, bank, off=0, width=512, parts=(0, 128)):
        return self.PS[parts[0]:parts[1], bank * 512 + off: bank * 512 + off + width]

    def psl(self, *banks):
        return [self.P.slot(f"psum_b{b}") for b in banks]

    def row(self, i):
        return self.t["rows"].ap()[0:1, i * T:(i + 1) * T]

    def row5(self, i):
        return self.t["rows"].ap()[0:1, i * 512:(i + 1) * 512]

    def rowsl(self, i):
        return self.P.slot(f"row{i}")


def build_common(nc, P):
    cx = Ctx(nc, P)
    cx.PSt = nc.alloc_psum_tensor("PS", [128, 4096], F32)
    cx.PS = cx.PSt.ap()
    for nm, (shape, dt) in CONST_SHAPES.items():
        cx.galloc(nm, shape, BF16 if dt == "bf" else F32)
    cx.csl = {nm: P.slot("const_" + nm) for nm in CONST_SHAPES}
    cx.sl_const = P.slot("const")

    def fcol(v, p0=0, n=1):
        i = FCONST_VALS.index(v)
        return cx.t["c_fconsts"].ap()[p0:p0 + n, i:i + 1]
    cx.fcol = fcol
    return cx


def emit_load_consts(cx):
    P = cx.P
    P.qfinal["kconst"] = True
    for nm in CONST_SHAPES:
        P.do("gpsimd", lambda e, nm=nm: e.dma_start(
            out=cx.t[nm].ap(), in_=cx.dram[nm]),
            reads=(), writes=[cx.csl[nm]], dma=True, qkey="kconst", desc=f"ld {nm}")


def emit_stats_rows(cx, pr, src_t, src_slots, sq_t, sq_slots, i_mean, i_msq,
                    nch=NCH, sq_engine="scalar"):
    P = cx.P
    for c in range(nch):
        P.do(sq_engine, lambda e, c=c: e.activation(
            sq_t[:, c % sq_t.shape[1], :] if False else sq_t[:, c % sq_t.shape[1], :],
            src_t[:, c, :], AF.Square),
            reads=[src_slots[c]], writes=[sq_slots[c % len(sq_slots)]],
            desc=f"{pr} sq{c}")
        for ti in range(4):
            P.do("tensor", lambda e, c=c, ti=ti: e.matmul(
                cx.ps(ti, 0, 512, (0, 1)), cx.t["c_ones_col"].ap(),
                src_t[:, c, ti * 512: ti * 512 + 512],
                start=(c == 0), stop=(c == nch - 1)),
                reads=[src_slots[c], cx.csl["c_ones_col"]], writes=cx.psl(ti),
                desc=f"{pr} sum{c}.{ti}")
            P.do("tensor", lambda e, c=c, ti=ti: e.matmul(
                cx.ps(4 + ti, 0, 512, (0, 1)), cx.t["c_ones_col"].ap(),
                sq_t[:, c % sq_t.shape[1], ti * 512: ti * 512 + 512],
                start=(c == 0), stop=(c == nch - 1)),
                reads=[sq_slots[c % len(sq_slots)], cx.csl["c_ones_col"]],
                writes=cx.psl(4 + ti), desc=f"{pr} ssq{c}.{ti}")
    r_mean, r_msq = cx.row(i_mean), cx.row(i_msq)
    for ti in range(4):
        P.do("vector", lambda e, ti=ti, r_mean=r_mean: e.tensor_scalar_mul(
            r_mean[0:1, ti * 512: ti * 512 + 512],
            cx.ps(ti, 0, 512, (0, 1)), 1.0 / D),
            reads=cx.psl(ti), writes=[cx.rowsl(i_mean)], desc=f"{pr} mean{ti}")
        P.do("vector", lambda e, ti=ti, r_msq=r_msq: e.tensor_scalar_mul(
            r_msq[0:1, ti * 512: ti * 512 + 512],
            cx.ps(4 + ti, 0, 512, (0, 1)), 1.0 / D),
            reads=cx.psl(4 + ti), writes=[cx.rowsl(i_msq)], desc=f"{pr} msq{ti}")


def emit_rstd_row(cx, pr, i_mean, i_msq, i_rstd, i_tmp, eps=EPS):
    P = cx.P
    rt, rm, rq, rr = cx.row(i_tmp), cx.row(i_mean), cx.row(i_msq), cx.row(i_rstd)
    P.do("vector", lambda e: e.tensor_mul(rt, rm, rm),
         reads=[cx.rowsl(i_mean)], writes=[cx.rowsl(i_tmp)], desc=f"{pr} m2")
    P.do("vector", lambda e: e.tensor_sub(rt, rq, rt),
         reads=[cx.rowsl(i_msq), cx.rowsl(i_tmp)], writes=[cx.rowsl(i_tmp)],
         desc=f"{pr} var")
    P.do("scalar", lambda e: e.activation(rt, rt, AF.Ln, bias=cx.fcol(eps, 0)),
         reads=[cx.rowsl(i_tmp), cx.csl["c_fconsts"]], writes=[cx.rowsl(i_tmp)],
         desc=f"{pr} ln")
    P.do("scalar", lambda e: e.activation(rr, rt, AF.Exp, scale=-0.5),
         reads=[cx.rowsl(i_tmp)], writes=[cx.rowsl(i_rstd)], desc=f"{pr} exp")


def emit_bcast(cx, pr, i_row, banks):
    P = cx.P
    ones = cx.t["c_ones_row"].ap()[0:1, 0:128]
    rowap = cx.row(i_row)
    out = []
    for ti, bk in enumerate(banks):
        pap = cx.ps(bk)
        P.do("tensor", lambda e, pap=pap, ti=ti, ones=ones, rowap=rowap: e.matmul(
            pap, ones, rowap[0:1, ti * 512: ti * 512 + 512],
            start=True, stop=True),
            reads=[cx.rowsl(i_row), cx.csl["c_ones_row"]], writes=cx.psl(bk),
            desc=f"{pr} bc{ti}")
        out.append(pap)
    return out


def emit_l0f(cx, b, xin, xout):
    nc, P = cx.nc, cx.P
    sl = P.slot
    pr = f"l0f{b}"
    R_MEAN, R_MSQ, R_RSTD, R_TMP = 0, 1, 2, 3
    R_MEANU, R_MSQU, R_RSTDU = 0, 1, 2

    cx.phase_begin()
    cx.palloc("rows", [1, 4 * T], F32)
    tgtb = cx.palloc("tgtb", [128, NCH, T], BF16).ap()
    h1 = cx.palloc("h1", [128, 32, 1024], BF16).ap()
    ubf = cx.palloc("ubf", [128, NCH, T], BF16).ap()
    xst = cx.palloc("xst", [128, 1, T], F32).ap()
    fsc = cx.palloc("fsc", [128, 2, 1024], F32).ap()
    xsq = cx.palloc("xsq", [128, 1, T], BF16).ap()
    w1s = cx.palloc("w1s", [128, 2, 8, 128], BF16).ap()
    w2s = cx.palloc("w2s", [128, 2, 32, 128], BF16).ap()
    tfh = cx.dram["tgtf_hbm"]

    tb_sl = [sl(f"{pr}_tb{c}") for c in range(NCH)]
    sq_sl = [sl(f"{pr}_sq0")]
    u_sl = [sl(f"{pr}_u{c}") for c in range(NCH)]
    h1_sl = [sl(f"{pr}_h1_{j}") for j in range(32)]
    _xs = sl(f"{pr}_xst0")
    xst_sl = [_xs, _xs]
    fsc_sl = [sl(f"{pr}_fsc0"), sl(f"{pr}_fsc1")]
    tfh_sl = [sl(f"{pr}_tfh{c}") for c in range(NCH)]

    for c in range(NCH):
        db = c % 2
        P.do("sync", lambda e, c=c, db=db: e.dma_start(
            out=xst[:, 0, :], in_=xin[b, c * 128:(c + 1) * 128, :]),
            reads=(), writes=[xst_sl[db]], dma=True, qkey="l0f_xst", desc=f"{pr} ldx{c}")
        P.do("vector", lambda e, c=c, db=db: e.tensor_copy(tgtb[:, c, :], xst[:, 0, :]),
             reads=[xst_sl[db]], writes=[tb_sl[c]], desc=f"{pr} cast{c}")
    emit_stats_rows(cx, pr, tgtb, tb_sl, xsq, sq_sl, R_MEAN, R_MSQ)
    emit_rstd_row(cx, pr, R_MEAN, R_MSQ, R_RSTD, R_TMP)
    mb = emit_bcast(cx, pr + "m", R_MEAN, [0, 1, 2, 3])
    rb = emit_bcast(cx, pr + "r", R_RSTD, [4, 5, 6, 7])

    for c in range(NCH):
        db = c % 2
        P.do("scalar", lambda e, c=c, db=db: e.dma_start(
            out=xst[:, 0, :], in_=xin[b, c * 128:(c + 1) * 128, :]),
            reads=(), writes=[xst_sl[db]], dma=True, qkey="l0f_xst", desc=f"{pr} ldx2_{c}")
        for ti in range(4):
            P.do("vector", lambda e, c=c, db=db, ti=ti: e.tensor_sub(
                xst[:, 0, ti * 512: ti * 512 + 512],
                xst[:, 0, ti * 512: ti * 512 + 512], mb[ti]),
                reads=[xst_sl[db]] + cx.psl(ti), writes=[xst_sl[db]],
                desc=f"{pr} sub{c}.{ti}")
            P.do("vector", lambda e, c=c, db=db, ti=ti: e.tensor_mul(
                xst[:, 0, ti * 512: ti * 512 + 512],
                xst[:, 0, ti * 512: ti * 512 + 512], rb[ti]),
                reads=[xst_sl[db]] + cx.psl(4 + ti), writes=[xst_sl[db]],
                desc=f"{pr} mul{c}.{ti}")
        P.do("vector", lambda e, c=c, db=db: e.tensor_copy(tgtb[:, c, :], xst[:, 0, :]),
             reads=[xst_sl[db]], writes=[tb_sl[c]], desc=f"{pr} tcast{c}")
        P.do("sync", lambda e, c=c, db=db: e.dma_start(
            out=tfh[b, c * 128:(c + 1) * 128, :], in_=xst[:, 0, :]),
            reads=[xst_sl[db]], writes=[tfh_sl[c]], dma=True, qkey=f"l0f_tfh{c}",
            desc=f"{pr} sttf{c}")

    W1 = cx.dram["L0_ffn_w1"]
    W2 = cx.dram["L0_ffn_w2"]
    b1c = cx.t["l0f_b1"].ap()
    b2c = cx.t["l0f_b2"].ap()
    w1s_sl = [sl(f"{pr}_w1s0"), sl(f"{pr}_w1s1")]
    w2s_sl = [sl(f"{pr}_w2s0"), sl(f"{pr}_w2s1")]
    for th in range(2):
        t0 = th * 1024
        for j in range(32):
            db = j % 2
            pb = 0 if db == 0 else 2
            P.do("gpsimd", lambda e, j=j, db=db: e.dma_start(
                out=w1s[:, db, :, :],
                in_=W1[0, :, j * 128:(j + 1) * 128].rearrange("(c p) f -> p c f", p=128)),
                reads=(), writes=[w1s_sl[db]], dma=True, qkey=f"l0f_w1s{db}",
                desc=f"{pr} ldw1 {th}.{j}")
            for tt in range(2):
                for dd in range(NCH):
                    P.do("tensor", lambda e, j=j, db=db, tt=tt, dd=dd, pb=pb, t0=t0:
                         e.matmul(cx.ps(pb + tt), w1s[:, db, dd, :],
                                  tgtb[:, dd, t0 + tt * 512: t0 + tt * 512 + 512],
                                  start=(dd == 0), stop=(dd == NCH - 1)),
                         reads=[w1s_sl[db], tb_sl[dd]], writes=cx.psl(pb + tt),
                         desc=f"{pr} w1mm {th}.{j}.{tt}.{dd}")
            if db == 0:
                P.do("scalar", lambda e, j=j, pb=pb: e.activation(
                    h1[:, j, :], cx.ps(pb, 0, 1024), AF.Relu, bias=b1c[:, j:j + 1]),
                    reads=cx.psl(pb, pb + 1) + [cx.csl["l0f_b1"]],
                    writes=[h1_sl[j]], desc=f"{pr} relu {th}.{j}")
            else:
                P.do("vector", lambda e, j=j, pb=pb: e.tensor_scalar(
                    h1[:, j, :], cx.ps(pb, 0, 1024), b1c[:, j:j + 1], 0.0,
                    ALU.add, ALU.max),
                    reads=cx.psl(pb, pb + 1) + [cx.csl["l0f_b1"]],
                    writes=[h1_sl[j]], desc=f"{pr} reluv {th}.{j}")
        for do_ in range(NCH):
            db = do_ % 2
            pb = 4 if db == 0 else 6
            P.do("gpsimd", lambda e, do_=do_, db=db: e.dma_start(
                out=w2s[:, db, :, :],
                in_=W2[0, :, do_ * 128:(do_ + 1) * 128].rearrange("(c p) f -> p c f", p=128)),
                reads=(), writes=[w2s_sl[db]], dma=True, qkey=f"l0f_w2s{db}",
                desc=f"{pr} ldw2 {th}.{do_}")
            P.do("scalar", lambda e, do_=do_, db=db, t0=t0: e.dma_start(
                out=fsc[:, db, :], in_=tfh[b, do_ * 128:(do_ + 1) * 128, t0:t0 + 1024]),
                reads=[tfh_sl[do_]], writes=[fsc_sl[db]], dma=True,
                qkey=f"l0f_fsc{db}", desc=f"{pr} ldtf {th}.{do_}")
            for j in range(32):
                for tt in range(2):
                    P.do("tensor", lambda e, do_=do_, db=db, j=j, tt=tt, pb=pb:
                         e.matmul(cx.ps(pb + tt), w2s[:, db, j, :],
                                  h1[:, j, tt * 512: tt * 512 + 512],
                                  start=(j == 0), stop=(j == 31)),
                         reads=[w2s_sl[db], h1_sl[j]], writes=cx.psl(pb + tt),
                         desc=f"{pr} w2mm {th}.{do_}.{j}.{tt}")
            P.do("vector", lambda e, do_=do_, db=db, pb=pb, t0=t0: e.scalar_tensor_tensor(
                ubf[:, do_, t0: t0 + 1024], cx.ps(pb, 0, 1024), b2c[:, do_:do_ + 1],
                fsc[:, db, :], ALU.add, ALU.add),
                reads=cx.psl(pb, pb + 1) + [fsc_sl[db], cx.csl["l0f_b2"]],
                writes=[u_sl[do_]], desc=f"{pr} uev {th}.{do_}")

    emit_stats_rows(cx, pr + "u", ubf, u_sl, xsq, sq_sl, R_MEANU, R_MSQU)
    emit_rstd_row(cx, pr + "u", R_MEANU, R_MSQU, R_RSTDU, R_TMP)
    mbU = emit_bcast(cx, pr + "mu", R_MEANU, [0, 1, 2, 3])
    rbU = emit_bcast(cx, pr + "ru", R_RSTDU, [4, 5, 6, 7])
    gam = cx.t["l0f_gamma"].ap()
    bet = cx.t["l0f_beta"].ap()
    for c in range(NCH):
        db = c % 2
        P.do("sync", lambda e, c=c, db=db: e.dma_start(
            out=xst[:, 0, :], in_=xin[b, c * 128:(c + 1) * 128, :]),
            reads=(), writes=[xst_sl[db]], dma=True, qkey="l0f_xst", desc=f"{pr} ldx3_{c}")
        for hf in range(2):
            t0 = hf * 1024
            for ti in range(2):
                bi = hf * 2 + ti
                P.do("vector", lambda e, c=c, db=db, t0=t0, ti=ti, bi=bi: e.tensor_sub(
                    fsc[:, db, ti * 512: ti * 512 + 512],
                    ubf[:, c, t0 + ti * 512: t0 + ti * 512 + 512], mbU[bi]),
                    reads=[u_sl[c]] + cx.psl(bi), writes=[fsc_sl[db]],
                    desc=f"{pr} ap1 {c}.{bi}")
                P.do("vector", lambda e, c=c, db=db, ti=ti, bi=bi: e.tensor_mul(
                    fsc[:, db, ti * 512: ti * 512 + 512],
                    fsc[:, db, ti * 512: ti * 512 + 512], rbU[bi]),
                    reads=[fsc_sl[db]] + cx.psl(4 + bi), writes=[fsc_sl[db]],
                    desc=f"{pr} ap2 {c}.{bi}")
            P.do("vector", lambda e, c=c, db=db: e.tensor_scalar(
                fsc[:, db, :], fsc[:, db, :], gam[:, c:c + 1], bet[:, c:c + 1],
                ALU.mult, ALU.add),
                reads=[fsc_sl[db], cx.csl["l0f_gamma"], cx.csl["l0f_beta"]],
                writes=[fsc_sl[db]], desc=f"{pr} ap3 {c}.{hf}")
            P.do("vector", lambda e, c=c, db=db, t0=t0: e.tensor_add(
                xst[:, 0, t0:t0 + 1024], xst[:, 0, t0:t0 + 1024], fsc[:, db, :]),
                reads=[fsc_sl[db], xst_sl[db]], writes=[xst_sl[db]],
                desc=f"{pr} ap4 {c}.{hf}")
        P.do("sync", lambda e, c=c, db=db: e.dma_start(
            out=xout[b, c * 128:(c + 1) * 128, :], in_=xst[:, 0, :]),
            reads=[xst_sl[db]], writes=[sl(f"{pr}_xo{c}")], dma=True,
            qkey="l0f_xo", desc=f"{pr} stx{c}")
    P.barrier()



def emit_ln_spill(cx, pr, b, xin, hbf, hbf_sl, xst, xst_sl, xsq, sq_sl,
                  spill=None, spill_sl=None, spill_qk=None, xst_qk=None):
    P = cx.P
    R_MEAN, R_MSQ, R_RSTD, R_TMP = 0, 1, 2, 3
    for c in range(NCH):
        P.do("sync", lambda e, c=c: e.dma_start(
            out=xst[:, 0, :], in_=xin[b, c * 128:(c + 1) * 128, :]),
            reads=(), writes=[xst_sl[0]], dma=True, qkey=xst_qk, desc=f"{pr} ldx{c}")
        P.do("vector", lambda e, c=c: e.tensor_copy(hbf[:, c, :], xst[:, 0, :]),
             reads=[xst_sl[0]], writes=[hbf_sl[c]], desc=f"{pr} cast{c}")
    emit_stats_rows(cx, pr, hbf, hbf_sl, xsq, sq_sl, R_MEAN, R_MSQ)
    emit_rstd_row(cx, pr, R_MEAN, R_MSQ, R_RSTD, R_TMP)
    mb = emit_bcast(cx, pr + "m", R_MEAN, [0, 1, 2, 3])
    rb = emit_bcast(cx, pr + "r", R_RSTD, [4, 5, 6, 7])
    for c in range(NCH):
        P.do("scalar", lambda e, c=c: e.dma_start(
            out=xst[:, 0, :], in_=xin[b, c * 128:(c + 1) * 128, :]),
            reads=(), writes=[xst_sl[0]], dma=True, qkey=xst_qk, desc=f"{pr} ldx2_{c}")
        for ti in range(4):
            P.do("vector", lambda e, c=c, ti=ti: e.tensor_sub(
                xst[:, 0, ti * 512: ti * 512 + 512],
                xst[:, 0, ti * 512: ti * 512 + 512], mb[ti]),
                reads=[xst_sl[0]] + cx.psl(ti), writes=[xst_sl[0]],
                desc=f"{pr} sub{c}.{ti}")
            P.do("vector", lambda e, c=c, ti=ti: e.tensor_mul(
                xst[:, 0, ti * 512: ti * 512 + 512],
                xst[:, 0, ti * 512: ti * 512 + 512], rb[ti]),
                reads=[xst_sl[0]] + cx.psl(4 + ti), writes=[xst_sl[0]],
                desc=f"{pr} mul{c}.{ti}")
        P.do("vector", lambda e, c=c: e.tensor_copy(hbf[:, c, :], xst[:, 0, :]),
             reads=[xst_sl[0]], writes=[hbf_sl[c]], desc=f"{pr} hcast{c}")
        if spill is not None:
            P.do("sync", lambda e, c=c: e.dma_start(
                out=spill[b, c * 128:(c + 1) * 128, :], in_=xst[:, 0, :]),
                reads=[xst_sl[0]], writes=[spill_sl[c]], dma=True,
                qkey=spill_qk + str(c), desc=f"{pr} spill{c}")


def emit_l1a(cx, b, xin, xout):
    nc, P = cx.nc, cx.P
    sl = P.slot
    pr = f"l1a{b}"
    E, S = 2048, 16
    NE = 16

    cx.phase_begin()
    cx.palloc("rows", [1, 4 * T], F32)
    hbf = cx.palloc("hbf", [128, NCH, T], BF16).ap()
    xc = cx.palloc("xc", [128, NE, T], BF16).ap()
    xm = cx.palloc("xm", [128, T], F32).ap()
    cv = cx.palloc("cv", [128, T], F32).ap()
    spb = cx.palloc("spb", [128, 512], BF16).ap()
    dens = cx.palloc("dens", [16, 3, T], F32).ap()
    xst = cx.palloc("xst", [128, 1, T], F32).ap()
    xsq = cx.palloc("xsq", [128, 1, T], BF16, at="xc").ap()
    hfst = cx.palloc("hfst", [128, T], F32, at="xm").ap()
    iws = cx.palloc("iws", [128, 2, 8, 128], BF16).ap()
    dtw = cx.palloc("dtw", [128, 2, 16, 128], BF16).ap()
    bcd = cx.palloc("bcd", [128, 16, 48], BF16).ap()
    ows = cx.palloc("ows", [128, 2, 16, 128], BF16).ap()

    hbf_sl = [sl(f"{pr}_h{c}") for c in range(NCH)]
    xc_sl = [sl(f"{pr}_xc{c}") for c in range(NE)]
    sq_sl = [sl(f"{pr}_sq0")]
    xst_sl = [sl(f"{pr}_xst")]
    hf_sl = [sl(f"{pr}_hf{c}") for c in range(NCH)]
    xm_sl, cv_sl = sl(f"{pr}_xm"), sl(f"{pr}_cv")
    spb_sl = sl(f"{pr}_spb")
    hfst_sl = sl(f"{pr}_hfst")
    den_sl = [sl(f"{pr}_den{i}") for i in range(3)]
    iws_sl = [sl(f"{pr}_iw0"), sl(f"{pr}_iw1")]
    dtw_sl = [sl(f"{pr}_dtw0"), sl(f"{pr}_dtw1")]
    bcd_sl = sl(f"{pr}_bcd")
    ows_sl = [sl(f"{pr}_ow0"), sl(f"{pr}_ow1")]

    hfh = cx.dram["tgtf_hbm"]
    IW = cx.dram["L1_in_wT"]
    DTW = cx.dram["L1_dt_wT"]
    BCD = cx.dram["L1_bcd_wT"]
    OW = cx.dram["L1_out_wT"]
    pc = cx.t["l1a_pcols"].ap()
    bcdb = cx.t["l1a_bcdb"].ap()
    outb = cx.t["l1a_outb"].ap()

    emit_ln_spill(cx, pr, b, xin, hbf, hbf_sl, xst, xst_sl, xsq, sq_sl,
                  spill=hfh, spill_sl=hf_sl, spill_qk="l1a_hf", xst_qk="l1a_xst")

    for f in range(NE):
        db = f % 2
        pb = 0 if db == 0 else 4
        P.do("gpsimd", lambda e, f=f, db=db: e.dma_start(
            out=iws[:, db, :, :],
            in_=IW[:, f * 128:(f + 1) * 128].rearrange("(c p) o -> p c o", p=128)),
            reads=(), writes=[iws_sl[db]], dma=True, qkey=f"l1a_iw{db}",
            desc=f"{pr} ldiw{f}")
        for tt in range(4):
            for dd in range(NCH):
                P.do("tensor", lambda e, f=f, db=db, tt=tt, dd=dd, pb=pb: e.matmul(
                    cx.ps(pb + tt), iws[:, db, dd, :],
                    hbf[:, dd, tt * 512: tt * 512 + 512],
                    start=(dd == 0), stop=(dd == NCH - 1)),
                    reads=[iws_sl[db], hbf_sl[dd]], writes=cx.psl(pb + tt),
                    desc=f"{pr} iwmm{f}.{tt}.{dd}")
        P.do("vector", lambda e, f=f, pb=pb: e.tensor_scalar_add(
            xm[:, :], cx.ps(pb, 0, 2048), pc[:, f:f + 1]),
            reads=cx.psl(pb, pb + 1, pb + 2, pb + 3) + [cx.csl["l1a_pcols"]],
            writes=[xm_sl], desc=f"{pr} xmev{f}")
        P.do("vector", lambda e: e.memset(cv[:, 0:1], 0.0),
             reads=(), writes=[cv_sl], desc=f"{pr} cvz{f}")
        P.do("vector", lambda e, f=f: e.tensor_scalar_mul(
            cv[:, 1:T], xm[:, 0:T - 1], pc[:, 32 + f:32 + f + 1]),
            reads=[xm_sl, cx.csl["l1a_pcols"]], writes=[cv_sl], desc=f"{pr} cv0{f}")
        P.do("vector", lambda e, f=f: e.scalar_tensor_tensor(
            cv[:, :], xm[:, :], pc[:, 48 + f:48 + f + 1], cv[:, :],
            ALU.mult, ALU.add),
            reads=[xm_sl, cv_sl, cx.csl["l1a_pcols"]], writes=[cv_sl],
            desc=f"{pr} cv1{f}")
        P.do("vector", lambda e, f=f: e.scalar_tensor_tensor(
            cv[:, 0:T - 1], xm[:, 1:T], pc[:, 64 + f:64 + f + 1], cv[:, 0:T - 1],
            ALU.mult, ALU.add),
            reads=[xm_sl, cv_sl, cx.csl["l1a_pcols"]], writes=[cv_sl],
            desc=f"{pr} cv2{f}")
        P.do("scalar", lambda e, f=f: e.activation(
            xc[:, f, :], cv[:, :], AF.Silu, bias=pc[:, 80 + f:80 + f + 1]),
            reads=[cv_sl, cx.csl["l1a_pcols"]], writes=[xc_sl[f]],
            desc=f"{pr} silu{f}")

    mx = cv[:, 0:512]
    az = cv[:, 512:1024]
    for eo in range(NE):
        db = eo % 2
        P.do("gpsimd", lambda e, eo=eo, db=db: e.dma_start(
            out=dtw[:, db, :, :],
            in_=DTW[:, eo * 128:(eo + 1) * 128].rearrange("(c p) o -> p c o", p=128)),
            reads=(), writes=[dtw_sl[db]], dma=True, qkey=f"l1a_dtw{db}",
            desc=f"{pr} lddtw{eo}")
        for tt in range(4):
            pb = (tt % 2) + 2 * db
            for ee in range(NE):
                P.do("tensor", lambda e, eo=eo, db=db, tt=tt, ee=ee, pb=pb: e.matmul(
                    cx.ps(pb), dtw[:, db, ee, :],
                    xc[:, ee, tt * 512: tt * 512 + 512],
                    start=(ee == 0), stop=(ee == NE - 1)),
                    reads=[dtw_sl[db], xc_sl[ee]], writes=cx.psl(pb),
                    desc=f"{pr} dtmm{eo}.{tt}.{ee}")
            P.do("vector", lambda e, eo=eo, pb=pb: e.tensor_scalar_add(
                az[:, :], cx.ps(pb), pc[:, 96 + eo:96 + eo + 1]),
                reads=cx.psl(pb) + [cx.csl["l1a_pcols"]], writes=[cv_sl],
                desc=f"{pr} az{eo}.{tt}")
            P.do("vector", lambda e: e.tensor_scalar_max(mx[:, :], az[:, :], 0.0),
                 reads=[cv_sl], writes=[cv_sl], desc=f"{pr} mx{eo}.{tt}")
            P.do("vector", lambda e: e.scalar_tensor_tensor(
                az[:, :], mx[:, :], -2.0, az[:, :], ALU.mult, ALU.add),
                reads=[cv_sl], writes=[cv_sl], desc=f"{pr} zm{eo}.{tt}")
            P.do("scalar", lambda e: e.activation(az[:, :], az[:, :], AF.Exp),
                 reads=[cv_sl], writes=[cv_sl], desc=f"{pr} spe{eo}.{tt}")
            P.do("scalar", lambda e: e.activation(az[:, :], az[:, :], AF.Ln,
                                                  bias=cx.fcol(1.0, 0, 128)),
                 reads=[cv_sl, cx.csl["c_fconsts"]], writes=[cv_sl],
                 desc=f"{pr} spl{eo}.{tt}")
            P.do("vector", lambda e: e.tensor_add(spb[:, :], mx[:, :], az[:, :]),
                 reads=[cv_sl], writes=[spb_sl], desc=f"{pr} sp{eo}.{tt}")
            P.do("tensor", lambda e, eo=eo, tt=tt: e.matmul(
                cx.ps(4 + tt, 0, 512, (0, 1)), cx.t["c_ones_col"].ap(), spb[:, :],
                start=(eo == 0), stop=(eo == NE - 1)),
                reads=[spb_sl, cx.csl["c_ones_col"]], writes=cx.psl(4 + tt),
                desc=f"{pr} spsum{eo}.{tt}")
    R_DTS, R_DEC, R_Y = 0, 1, 2
    for tt in range(4):
        _rd = cx.row(R_DTS)
        P.do("vector", lambda e, tt=tt, _rd=_rd: e.tensor_scalar(
            _rd[0:1, tt * 512: tt * 512 + 512],
            cx.ps(4 + tt, 0, 512, (0, 1)), -1.0 / E, 10.0, ALU.mult, ALU.min),
            reads=cx.psl(4 + tt), writes=[cx.rowsl(R_DTS)], desc=f"{pr} dts{tt}")
    _rd, _re = cx.row(R_DTS), cx.row(R_DEC)
    P.do("vector", lambda e, _rd=_rd: e.tensor_scalar_max(_rd, _rd, -10.0),
         reads=[cx.rowsl(R_DTS)], writes=[cx.rowsl(R_DTS)], desc=f"{pr} clip")
    P.do("scalar", lambda e, _rd=_rd, _re=_re: e.activation(_re, _rd, AF.Exp),
         reads=[cx.rowsl(R_DTS)], writes=[cx.rowsl(R_DEC)], desc=f"{pr} dec")

    P.do("gpsimd", lambda e: e.dma_start(
        out=bcd[:, :, :], in_=BCD.rearrange("(c p) s -> p c s", p=128)),
        reads=(), writes=[bcd_sl], dma=True, qkey="l1a_bcd", desc=f"{pr} ldbcd")
    for k in range(3):
        for tt in range(4):
            for ee in range(NE):
                P.do("tensor", lambda e, k=k, tt=tt, ee=ee: e.matmul(
                    cx.ps(k, 0, 512, (0, 16)), bcd[:, ee, k * 16:(k + 1) * 16],
                    xc[:, ee, tt * 512: tt * 512 + 512],
                    start=(ee == 0), stop=(ee == NE - 1)),
                    reads=[bcd_sl, xc_sl[ee]], writes=cx.psl(k),
                    desc=f"{pr} bcd{k}.{tt}.{ee}")
            P.do("vector", lambda e, k=k, tt=tt: e.tensor_scalar_add(
                dens[0:16, min(k, 2), tt * 512: tt * 512 + 512],
                cx.ps(k, 0, 512, (0, 16)), bcdb[0:16, k:k + 1]),
                reads=cx.psl(k) + [cx.csl["l1a_bcdb"]], writes=[den_sl[min(k, 2)]],
                desc=f"{pr} bcdev{k}.{tt}")
    P.do("vector", lambda e: e.tensor_mul(dens[0:16, 0, :], dens[0:16, 0, :],
                                          dens[0:16, 2, :]),
         reads=[den_sl[0], den_sl[2]], writes=[den_sl[0]], desc=f"{pr} u")
    for tt in range(4):
        _re = cx.row(R_DEC)
        P.do("tensor", lambda e, tt=tt, _re=_re: e.matmul(
            cx.ps(3, 0, 512, (0, 16)), cx.t["c_ones_row"].ap()[0:1, 0:16],
            _re[0:1, tt * 512: tt * 512 + 512], start=True, stop=True),
            reads=[cx.rowsl(R_DEC), cx.csl["c_ones_row"]], writes=cx.psl(3),
            desc=f"{pr} decbc{tt}")
        P.do("vector", lambda e, tt=tt: e.tensor_copy(
            dens[0:16, 2, tt * 512: tt * 512 + 512], cx.ps(3, 0, 512, (0, 16))),
            reads=cx.psl(3), writes=[den_sl[2]], desc=f"{pr} decev{tt}")
    P.do("vector", lambda e: e.tensor_tensor_scan(
        dens[0:16, 0, :], dens[0:16, 2, :], dens[0:16, 0, :], 0.0,
        ALU.mult, ALU.add),
        reads=[den_sl[0], den_sl[2]], writes=[den_sl[0]], desc=f"{pr} scan")
    P.do("vector", lambda e: e.tensor_mul(dens[0:16, 1, :], dens[0:16, 1, :],
                                          dens[0:16, 0, :]),
         reads=[den_sl[0], den_sl[1]], writes=[den_sl[1]], desc=f"{pr} cs")
    for tt in range(4):
        P.do("tensor", lambda e, tt=tt: e.matmul(
            cx.ps(4, 0, 512, (0, 1)), cx.t["c_ones_row"].ap()[0:16, 0:1],
            dens[0:16, 1, tt * 512: tt * 512 + 512], start=True, stop=True),
            reads=[den_sl[1], cx.csl["c_ones_row"]], writes=cx.psl(4),
            desc=f"{pr} ymm{tt}")
        _ry = cx.row(R_Y)
        P.do("vector", lambda e, tt=tt, _ry=_ry: e.tensor_copy(
            _ry[0:1, tt * 512: tt * 512 + 512], cx.ps(4, 0, 512, (0, 1))),
            reads=cx.psl(4), writes=[cx.rowsl(R_Y)], desc=f"{pr} yev{tt}")

    ybc = emit_bcast(cx, pr + "y", R_Y, [4, 5, 6, 7])
    for f in range(NE):
        db = f % 2
        P.do("gpsimd", lambda e, f=f, db=db: e.dma_start(
            out=iws[:, db, :, :],
            in_=IW[:, 2048 + f * 128: 2048 + (f + 1) * 128].rearrange(
                "(c p) o -> p c o", p=128)),
            reads=(), writes=[iws_sl[db]], dma=True, qkey=f"l1a_iw{db}",
            desc=f"{pr} ldgw{f}")
        for tt in range(4):
            pb = (tt % 2) + 2 * db
            for dd in range(NCH):
                P.do("tensor", lambda e, f=f, db=db, tt=tt, dd=dd, pb=pb: e.matmul(
                    cx.ps(pb), iws[:, db, dd, :],
                    hbf[:, dd, tt * 512: tt * 512 + 512],
                    start=(dd == 0), stop=(dd == NCH - 1)),
                    reads=[iws_sl[db], hbf_sl[dd]], writes=cx.psl(pb),
                    desc=f"{pr} gwmm{f}.{tt}.{dd}")
            P.do("scalar", lambda e, f=f, tt=tt, pb=pb: e.activation(
                xm[:, tt * 512: tt * 512 + 512], cx.ps(pb), AF.Silu,
                bias=pc[:, 16 + f:16 + f + 1]),
                reads=cx.psl(pb) + [cx.csl["l1a_pcols"]], writes=[xm_sl],
                desc=f"{pr} gsilu{f}.{tt}")
            P.do("vector", lambda e, f=f, tt=tt: e.tensor_mul(
                xc[:, f, tt * 512: tt * 512 + 512],
                xm[:, tt * 512: tt * 512 + 512], ybc[tt]),
                reads=[xm_sl] + cx.psl(4 + tt), writes=[xc_sl[f]],
                desc=f"{pr} gmul{f}.{tt}")

    for do_ in range(NCH):
        db = do_ % 2
        pb = 0 if db == 0 else 4
        P.do("gpsimd", lambda e, do_=do_, db=db: e.dma_start(
            out=ows[:, db, :, :],
            in_=OW[:, do_ * 128:(do_ + 1) * 128].rearrange("(c p) o -> p c o", p=128)),
            reads=(), writes=[ows_sl[db]], dma=True, qkey=f"l1a_ow{db}",
            desc=f"{pr} ldow{do_}")
        P.do("scalar", lambda e, do_=do_: e.dma_start(
            out=hfst[:, :], in_=hfh[b, do_ * 128:(do_ + 1) * 128, :]),
            reads=[hf_sl[do_]], writes=[hfst_sl, xm_sl], dma=True, qkey="l1a_hfst",
            desc=f"{pr} ldhf{do_}")
        P.do("sync", lambda e, do_=do_: e.dma_start(
            out=xst[:, 0, :], in_=xin[b, do_ * 128:(do_ + 1) * 128, :]),
            reads=(), writes=[xst_sl[0]], dma=True, qkey="l1a_xst",
            desc=f"{pr} ldx4_{do_}")
        for tt in range(4):
            for ee in range(NE):
                P.do("tensor", lambda e, do_=do_, db=db, tt=tt, ee=ee, pb=pb: e.matmul(
                    cx.ps(pb + tt), ows[:, db, ee, :],
                    xc[:, ee, tt * 512: tt * 512 + 512],
                    start=(ee == 0), stop=(ee == NE - 1)),
                    reads=[ows_sl[db], xc_sl[ee]], writes=cx.psl(pb + tt),
                    desc=f"{pr} owmm{do_}.{tt}.{ee}")
        P.do("vector", lambda e, do_=do_, pb=pb: e.scalar_tensor_tensor(
            hfst[:, :], cx.ps(pb, 0, 2048), outb[:, do_:do_ + 1], hfst[:, :],
            ALU.add, ALU.add),
            reads=cx.psl(pb, pb + 1, pb + 2, pb + 3) + [hfst_sl, cx.csl["l1a_outb"]],
            writes=[hfst_sl], desc=f"{pr} oev{do_}")
        P.do("vector", lambda e, do_=do_: e.tensor_add(
            xst[:, 0, :], xst[:, 0, :], hfst[:, :]),
            reads=[hfst_sl, xst_sl[0]], writes=[xst_sl[0]], desc=f"{pr} oadd{do_}")
        P.do("sync", lambda e, do_=do_: e.dma_start(
            out=xout[b, do_ * 128:(do_ + 1) * 128, :], in_=xst[:, 0, :]),
            reads=[xst_sl[0]], writes=[sl(f"{pr}_xo{do_}")], dma=True,
            qkey="l1a_xo", desc=f"{pr} stx{do_}")
    P.barrier()


def l1a_dram(nc, cx):
    dr = {}
    dr["L1_in_wT"] = nc.dram_tensor("L1_in_wT", [D, 4096], F32,
                                    kind="ExternalInput").ap()
    dr["L1_dt_wT"] = nc.dram_tensor("L1_dt_wT", [2048, 2048], F32,
                                    kind="ExternalInput").ap()
    dr["L1_bcd_wT"] = nc.dram_tensor("L1_bcd_wT", [2048, 48], F32,
                                     kind="ExternalInput").ap()
    dr["L1_out_wT"] = nc.dram_tensor("L1_out_wT", [2048, D], F32,
                                     kind="ExternalInput").ap()
    cx.dram.update(dr)
    for nm, sh in [("l1a_pcols", [128, 112]), ("l1a_bcdb", [16, 3]),
                   ("l1a_outb", [128, 8])]:
        t = nc.dram_tensor(nm + "_hbm", sh, F32, kind="ExternalInput")
        cx.dram[nm + "_hbm"] = t.ap()
        cx.galloc(nm, sh, F32)
        cx.csl[nm] = cx.P.slot("const_" + nm)
        cx.P.qfinal["kparam"] = True
        cx.P.do("sync", lambda e, nm=nm: e.dma_start(
            out=cx.t[nm].ap(), in_=cx.dram[nm + "_hbm"]),
            reads=(), writes=[cx.csl[nm]], dma=True, qkey="kparam", desc=f"ld {nm}")


def l1a_host_inputs(inputs):
    in_w = np.asarray(inputs["L1_attn_in_w"], np.float32)
    dt_w = np.asarray(inputs["L1_attn_dt_w"], np.float32)
    conv_w = np.asarray(inputs["L1_attn_conv_w"], np.float32)
    pcols = np.zeros((128, 112), np.float32)
    pcols[:, 0:16] = col128(inputs["L1_attn_in_b"][:2048])
    pcols[:, 16:32] = col128(inputs["L1_attn_in_b"][2048:])
    pcols[:, 32:48] = col128(conv_w[:, 0, 0])
    pcols[:, 48:64] = col128(conv_w[:, 0, 1])
    pcols[:, 64:80] = col128(conv_w[:, 0, 2])
    pcols[:, 80:96] = col128(inputs["L1_attn_conv_b"])
    pcols[:, 96:112] = col128(inputs["L1_attn_dt_b"])
    bcdb = np.stack([np.asarray(inputs["L1_attn_B_b"], np.float32),
                     np.asarray(inputs["L1_attn_C_b"], np.float32),
                     np.asarray(inputs["L1_attn_D_b"], np.float32)], axis=1)
    bcdw = np.concatenate([np.asarray(inputs["L1_attn_B_w"], np.float32).T,
                           np.asarray(inputs["L1_attn_C_w"], np.float32).T,
                           np.asarray(inputs["L1_attn_D_w"], np.float32).T], axis=1)
    return {
        "L1_in_wT": _f32(in_w.T),
        "L1_dt_wT": _f32(dt_w.T),
        "L1_bcd_wT": _f32(bcdw),
        "L1_out_wT": _f32(np.asarray(inputs["L1_attn_out_w"], np.float32).T),
        "l1a_pcols_hbm": _f32(pcols),
        "l1a_bcdb_hbm": _f32(bcdb),
        "l1a_outb_hbm": col128(inputs["L1_attn_out_b"]),
    }


def build_program_l1a():
    nc = bass.Bass("TRN2", target_bir_lowering=False)
    P = Prog()
    cx = build_common(nc, P)
    cx.dram["x_in"] = nc.dram_tensor("x_in", [NB, D, T], F32,
                                     kind="ExternalInput").ap()
    cx.dram["x_out"] = nc.dram_tensor("x_out", [NB, D, T], F32,
                                      kind="ExternalOutput").ap()
    cx.dram["tgtf_hbm"] = nc.dram_tensor("tgtf_hbm", [NB, D, T], F32,
                                         kind="Internal").ap()
    for nm, (sh, _dt) in CONST_SHAPES.items():
        cx.dram[nm] = nc.dram_tensor(nm, sh, F32, kind="ExternalInput").ap()
    emit_load_consts(cx)
    l1a_dram(nc, cx)
    for b in range(NB):
        emit_l1a(cx, b, cx.dram["x_in"], cx.dram["x_out"])
    P.do("gpsimd", lambda e: e.nop(),
         reads=[P.slot(f"l1a{b}_xo{c}") for b in range(NB) for c in range(NCH)],
         desc="final wait")
    finish_program(nc, P)
    return nc, P



def emit_l0a(cx, b, xin, xout):
    nc, P = cx.nc, cx.P
    sl = P.slot
    pr = f"l0a{b}"
    HP = 8
    R_MEAN, R_MSQ, R_RSTD, R_TMP = 0, 1, 2, 3

    cx.phase_begin()
    cx.palloc("rows", [1, 4 * T], F32)
    hbf = cx.palloc("hbf", [128, NCH, T], BF16).ap()
    wvu = cx.palloc("wvu", [128, NCH, T], BF16).ap()
    vab = cx.palloc("vab", [128, 16, 16, 66], BF16).ap()
    tabC = cx.palloc("tabC", [128, T], F32).ap()
    tabS = cx.palloc("tabS", [128, T], F32).ap()
    rsc = cx.palloc("rsc", [128, 2, T], F32).ap()
    qcb = cx.palloc("qcb", [128, T], BF16).ap()
    qfm = cx.palloc("qfm", [128, T], BF16).ap()
    kfm = cx.palloc("kfm", [128, T], BF16).ap()
    tpA = cx.palloc("tpA", [128, 1, 1024], BF16).ap()
    tpB = cx.palloc("tpB", [128, 1, 1024], BF16).ap()
    asc = cx.palloc("asc", [128, 512], F32).ap()
    svc = cx.palloc("svc", [128, 1], BF16).ap()
    svr = cx.palloc("svr", [1, 16, 66], BF16).ap()
    wqs = cx.palloc("wqs", [128, 2, 8, 128], BF16).ap()
    wks = cx.palloc("wks", [128, 2, 8, 128], BF16).ap()
    wvs = cx.palloc("wvs", [128, 8, 1024], BF16, at="rsc").ap()
    xst = cx.palloc("xst", [128, 1, T], F32).ap()
    xsq = cx.palloc("xsq", [128, 1, T], BF16, at="wvu").ap()

    hbf_sl = [sl(f"{pr}_h{c}") for c in range(NCH)]
    wvu_sl = [sl(f"{pr}_wv{c}") for c in range(NCH)]
    vab_sl = [sl(f"{pr}_v{tc}") for tc in range(16)]
    sq_sl = [sl(f"{pr}_sq0")]
    xst_sl = [sl(f"{pr}_xst")]
    tab_sl = sl(f"{pr}_tab")
    rsc_sl = sl(f"{pr}_rsc")
    qcb_sl = sl(f"{pr}_qcb")
    qfm_sl, kfm_sl = sl(f"{pr}_qfm"), sl(f"{pr}_kfm")
    tpA_sl = [sl(f"{pr}_tpA0"), sl(f"{pr}_tpA1")]
    tpB_sl = [sl(f"{pr}_tpB0"), sl(f"{pr}_tpB1")]
    asc_sl = sl(f"{pr}_asc")
    sv_sl = sl(f"{pr}_sv")
    wqs_sl = [sl(f"{pr}_wq0"), sl(f"{pr}_wq1")]
    wks_sl = [sl(f"{pr}_wk0"), sl(f"{pr}_wk1")]
    wvs_sl = sl(f"{pr}_wvs")

    WQ = cx.dram["L0_attn_w_q"]
    WKV = cx.dram["L0_attn_w_kv"]
    WO = cx.dram["L0_attn_w_o"]
    lnwb = cx.t["l0a_lnwb"].ap()
    fbc = cx.t["l0a_fbsgn"].ap()
    theta_t = cx.t["l0a_theta"].ap()

    ramp = rsc[:, 0, :]
    P.do("gpsimd", lambda e: e.iota(ramp, [[1, T]], channel_multiplier=0,
                                    allow_small_or_imprecise_dtypes=True),
         reads=(), writes=[rsc_sl], desc=f"{pr} iota")
    def _r(i):
        return asc[0:1, 16 + i:17 + i]
    P.do("vector", lambda e: e.reciprocal(_r(0), cx.fcol(220.0, 0, 1)),
         reads=[cx.csl["c_fconsts"]], writes=[asc_sl], desc=f"{pr} r220")
    P.do("vector", lambda e: e.tensor_mul(_r(1), theta_t, _r(0)),
         reads=[cx.csl["l0a_theta"], sv_sl], writes=[asc_sl], desc=f"{pr} q0")
    P.do("vector", lambda e: e.tensor_scalar_mul(_r(2), _r(1), 4097.0),
         reads=[asc_sl], writes=[asc_sl], desc=f"{pr} vc")
    P.do("vector", lambda e: e.tensor_sub(_r(3), _r(2), _r(1)),
         reads=[asc_sl], writes=[asc_sl], desc=f"{pr} vd")
    P.do("vector", lambda e: e.tensor_sub(_r(3), _r(2), _r(3)),
         reads=[asc_sl], writes=[asc_sl], desc=f"{pr} qh")
    P.do("vector", lambda e: e.tensor_sub(_r(4), _r(1), _r(3)),
         reads=[asc_sl], writes=[asc_sl], desc=f"{pr} ql")
    P.do("vector", lambda e: e.scalar_tensor_tensor(
        _r(5), _r(3), -220.0, theta_t, ALU.mult, ALU.add),
        reads=[asc_sl, cx.csl["l0a_theta"]], writes=[asc_sl], desc=f"{pr} e1")
    P.do("vector", lambda e: e.scalar_tensor_tensor(
        _r(5), _r(4), -220.0, _r(5), ALU.mult, ALU.add),
        reads=[asc_sl], writes=[asc_sl], desc=f"{pr} e2")
    P.do("vector", lambda e: e.tensor_mul(_r(5), _r(5), _r(0)),
        reads=[asc_sl], writes=[asc_sl], desc=f"{pr} corr")
    P.do("vector", lambda e: e.tensor_add(asc[0:1, 0:1], _r(1), _r(5)),
        reads=[asc_sl], writes=[asc_sl], desc=f"{pr} th220")
    P.do("tensor", lambda e: e.matmul(
        cx.ps(0, 0, 1, (0, 128)), cx.t["c_ones_row"].ap()[0:1, 0:128],
        asc[0:1, 0:1], start=True, stop=True),
        reads=[asc_sl, cx.csl["c_ones_row"]], writes=cx.psl(0), desc=f"{pr} thbc")
    P.do("vector", lambda e: e.tensor_mul(asc[:, 0:1], fbc[:, 0:1],
                                          cx.ps(0, 0, 1, (0, 128))),
         reads=cx.psl(0) + [cx.csl["l0a_fbsgn"]], writes=[asc_sl],
         desc=f"{pr} freqs")
    P.do("vector", lambda e: e.tensor_scalar_mul(ramp, ramp, asc[:, 0:1]),
         reads=[asc_sl, rsc_sl], writes=[rsc_sl], desc=f"{pr} ang")
    kt = rsc[:, 1, :]
    P.do("vector", lambda e: e.tensor_scalar(kt, ramp, INV_2PI, MAGIC,
                                             ALU.mult, ALU.add),
         reads=[rsc_sl], writes=[rsc_sl], desc=f"{pr} kmag")
    P.do("vector", lambda e: e.tensor_scalar_add(kt, kt, -MAGIC),
         reads=[rsc_sl], writes=[rsc_sl], desc=f"{pr} kmag2")
    rt = tabS[:, :]
    P.do("vector", lambda e: e.scalar_tensor_tensor(rt, kt, -CW[0], ramp,
                                                    ALU.mult, ALU.add),
         reads=[rsc_sl], writes=[tab_sl], desc=f"{pr} cw0")
    for i in range(1, 6):
        P.do("vector", lambda e, i=i: e.scalar_tensor_tensor(
            rt, kt, -CW[i], rt, ALU.mult, ALU.add),
            reads=[rsc_sl, tab_sl], writes=[tab_sl], desc=f"{pr} cw{i}")
    mt0 = rsc[:, 0, :]
    P.do("vector", lambda e: e.tensor_single_scalar(mt0, rt, float(np.pi), ALU.is_gt),
         reads=[tab_sl], writes=[rsc_sl], desc=f"{pr} rgt")
    P.do("vector", lambda e: e.scalar_tensor_tensor(
        rt, mt0, float(-2 * np.pi), rt, ALU.mult, ALU.add),
        reads=[rsc_sl, tab_sl], writes=[tab_sl], desc=f"{pr} rwrap1")
    P.do("vector", lambda e: e.tensor_single_scalar(mt0, rt, float(-np.pi), ALU.is_lt),
         reads=[tab_sl], writes=[rsc_sl], desc=f"{pr} rlt")
    P.do("vector", lambda e: e.scalar_tensor_tensor(
        rt, mt0, float(2 * np.pi), rt, ALU.mult, ALU.add),
        reads=[rsc_sl, tab_sl], writes=[tab_sl], desc=f"{pr} rwrap2")
    P.do("vector", lambda e: e.tensor_scalar(
        rt, rt, float(np.pi), float(-np.pi), ALU.min, ALU.max),
        reads=[tab_sl], writes=[tab_sl], desc=f"{pr} rclamp")
    wt = tabC[:, :]
    P.do("vector", lambda e: e.tensor_scalar_add(wt, rt, float(np.pi / 2)),
         reads=[tab_sl], writes=[tab_sl], desc=f"{pr} wshift")
    mt = rsc[:, 0, :]
    P.do("vector", lambda e: e.tensor_single_scalar(mt, wt, float(np.pi), ALU.is_gt),
         reads=[tab_sl], writes=[rsc_sl], desc=f"{pr} wgt")
    P.do("vector", lambda e: e.scalar_tensor_tensor(
        wt, mt, float(-2 * np.pi), wt, ALU.mult, ALU.add),
        reads=[rsc_sl, tab_sl], writes=[tab_sl], desc=f"{pr} wrap")
    P.do("vector", lambda e: e.tensor_scalar(
        wt, wt, float(np.pi), float(-np.pi), ALU.min, ALU.max),
        reads=[tab_sl], writes=[tab_sl], desc=f"{pr} wclamp")
    P.do("scalar", lambda e: e.activation(tabC[:, :], wt, AF.Sin),
         reads=[tab_sl], writes=[tab_sl], desc=f"{pr} cos")
    P.do("scalar", lambda e: e.activation(tabS[:, :], rt, AF.Sin),
         reads=[tab_sl], writes=[tab_sl], desc=f"{pr} sin")
    P.do("vector", lambda e: e.tensor_scalar_mul(tabS[:, :], tabS[:, :], fbc[:, 1:2]),
         reads=[tab_sl, cx.csl["l0a_fbsgn"]], writes=[tab_sl], desc=f"{pr} sgn")

    for c in range(NCH):
        P.do("sync", lambda e, c=c: e.dma_start(
            out=xst[:, 0, :], in_=xin[b, c * 128:(c + 1) * 128, :]),
            reads=(), writes=[xst_sl[0]], dma=True, qkey="l0a_xst",
            desc=f"{pr} ldx{c}")
        P.do("vector", lambda e, c=c: e.tensor_copy(hbf[:, c, :], xst[:, 0, :]),
             reads=[xst_sl[0]], writes=[hbf_sl[c]], desc=f"{pr} cast{c}")
    emit_stats_rows(cx, pr, hbf, hbf_sl, xsq, sq_sl, R_MEAN, R_MSQ)
    _rm, _rq, _rr, _rt = (cx.row(R_MEAN), cx.row(R_MSQ), cx.row(R_RSTD),
                          cx.row(R_TMP))
    P.do("vector", lambda e: e.tensor_mul(_rt, _rm, _rm),
         reads=[cx.rowsl(R_MEAN)], writes=[cx.rowsl(R_TMP)], desc=f"{pr} m2")
    P.do("vector", lambda e: e.tensor_sub(_rq, _rq, _rt),
         reads=[cx.rowsl(R_MSQ), cx.rowsl(R_TMP)], writes=[cx.rowsl(R_MSQ)],
         desc=f"{pr} var")
    P.do("scalar", lambda e: e.activation(_rt, _rq, AF.Ln, bias=cx.fcol(EPS)),
         reads=[cx.rowsl(R_MSQ), cx.csl["c_fconsts"]], writes=[cx.rowsl(R_TMP)],
         desc=f"{pr} lnv")
    P.do("scalar", lambda e: e.activation(_rr, _rt, AF.Exp, scale=-0.5),
         reads=[cx.rowsl(R_TMP)], writes=[cx.rowsl(R_RSTD)], desc=f"{pr} rstd")
    P.do("vector", lambda e: e.tensor_mul(_rt, _rq, _rr),
         reads=[cx.rowsl(R_MSQ), cx.rowsl(R_RSTD)], writes=[cx.rowsl(R_TMP)],
         desc=f"{pr} vy1")
    P.do("vector", lambda e: e.tensor_mul(_rt, _rt, _rr),
         reads=[cx.rowsl(R_TMP), cx.rowsl(R_RSTD)], writes=[cx.rowsl(R_TMP)],
         desc=f"{pr} vy2")
    P.do("scalar", lambda e: e.activation(_rt, _rt, AF.Ln, bias=cx.fcol(EPS)),
         reads=[cx.rowsl(R_TMP), cx.csl["c_fconsts"]], writes=[cx.rowsl(R_TMP)],
         desc=f"{pr} lnvy")
    P.do("scalar", lambda e: e.activation(_rt, _rt, AF.Exp, scale=-0.5),
         reads=[cx.rowsl(R_TMP)], writes=[cx.rowsl(R_TMP)], desc=f"{pr} ri")
    P.do("vector", lambda e: e.tensor_mul(_rr, _rr, _rt),
         reads=[cx.rowsl(R_RSTD), cx.rowsl(R_TMP)], writes=[cx.rowsl(R_RSTD)],
         desc=f"{pr} a2")
    P.do("vector", lambda e: e.tensor_scalar_mul(_rm, _rm, -1.0),
         reads=[cx.rowsl(R_MEAN)], writes=[cx.rowsl(R_MEAN)], desc=f"{pr} negm")
    P.do("vector", lambda e: e.tensor_mul(_rm, _rm, _rr),
         reads=[cx.rowsl(R_MEAN), cx.rowsl(R_RSTD)], writes=[cx.rowsl(R_MEAN)],
         desc=f"{pr} m2a")
    ab = emit_bcast(cx, pr + "a", R_RSTD, [0, 1, 2, 3])
    mb = emit_bcast(cx, pr + "m", R_MEAN, [4, 5, 6, 7])
    for c in range(NCH):
        P.do("scalar", lambda e, c=c: e.dma_start(
            out=xst[:, 0, :], in_=xin[b, c * 128:(c + 1) * 128, :]),
            reads=(), writes=[xst_sl[0]], dma=True, qkey="l0a_xst",
            desc=f"{pr} ldx2_{c}")
        for ti in range(4):
            P.do("vector", lambda e, c=c, ti=ti: e.tensor_mul(
                xst[:, 0, ti * 512: ti * 512 + 512],
                xst[:, 0, ti * 512: ti * 512 + 512], ab[ti]),
                reads=[xst_sl[0]] + cx.psl(ti), writes=[xst_sl[0]],
                desc=f"{pr} ha{c}.{ti}")
            P.do("vector", lambda e, c=c, ti=ti: e.tensor_add(
                xst[:, 0, ti * 512: ti * 512 + 512],
                xst[:, 0, ti * 512: ti * 512 + 512], mb[ti]),
                reads=[xst_sl[0]] + cx.psl(4 + ti), writes=[xst_sl[0]],
                desc=f"{pr} hm{c}.{ti}")
        P.do("vector", lambda e, c=c: e.tensor_scalar(
            hbf[:, c, :], xst[:, 0, :], lnwb[:, c:c + 1], lnwb[:, 8 + c:9 + c],
            ALU.mult, ALU.add),
            reads=[xst_sl[0], cx.csl["l0a_lnwb"]], writes=[hbf_sl[c]],
            desc=f"{pr} hw{c}")

    P.do("gpsimd", lambda e: e.dma_start(
        out=wvs[:, :, :],
        in_=WKV[:, 1024:2048].rearrange("(c p) f -> p c f", p=128)),
        reads=(), writes=[wvs_sl, rsc_sl], dma=True, qkey="l0a_wvs",
        desc=f"{pr} ldwv")
    P.do("vector", lambda e: e.memset(vab[:, :, :, 64:65], 1.0),
         reads=(), writes=vab_sl, desc=f"{pr} vones")
    for tc in range(16):
        pb = (tc % 2) * 2
        for fh in range(2):
            for dd in range(NCH):
                P.do("tensor", lambda e, tc=tc, fh=fh, dd=dd, pb=pb: e.matmul(
                    cx.ps(pb + fh), hbf[:, dd, tc * 128:(tc + 1) * 128],
                    wvs[:, dd, fh * 512:(fh + 1) * 512],
                    start=(dd == 0), stop=(dd == NCH - 1)),
                    reads=[wvs_sl, rsc_sl, hbf_sl[dd]], writes=cx.psl(pb + fh),
                    desc=f"{pr} vmm{tc}.{fh}.{dd}")
            P.do("vector", lambda e, tc=tc, fh=fh, pb=pb: e.tensor_copy(
                vab[:, tc, fh * 8:(fh + 1) * 8, 0:64], cx.ps(pb + fh)),
                reads=cx.psl(pb + fh), writes=[vab_sl[tc]],
                desc=f"{pr} vev{tc}.{fh}")

    for c in range(HP):
        db = c % 2
        P.do("gpsimd", lambda e, c=c, db=db: e.dma_start(
            out=wqs[:, db, :, :],
            in_=WQ[:, c * 128:(c + 1) * 128].rearrange("(cc p) f -> p cc f", p=128)),
            reads=(), writes=[wqs_sl[db]], dma=True, qkey=f"l0a_wq{db}",
            desc=f"{pr} ldwq{c}")
        P.do("gpsimd", lambda e, c=c, db=db: e.dma_start(
            out=wks[:, db, :, :],
            in_=WKV[:, c * 128:(c + 1) * 128].rearrange("(cc p) f -> p cc f", p=128)),
            reads=(), writes=[wks_sl[db]], dma=True, qkey=f"l0a_wk{db}",
            desc=f"{pr} ldwk{c}")
        for which, wsl, wbuf, dst, dst_sl in (
                ("q", wqs_sl[db], wqs, qfm, qfm_sl),
                ("k", wks_sl[db], wks, kfm, kfm_sl)):
            for tt in range(4):
                for dd in range(NCH):
                    P.do("tensor", lambda e, c=c, db=db, tt=tt, dd=dd, wbuf=wbuf,
                         wsl=wsl: e.matmul(
                             cx.ps(tt), wbuf[:, db, dd, :],
                             hbf[:, dd, tt * 512: tt * 512 + 512],
                             start=(dd == 0), stop=(dd == NCH - 1)),
                         reads=[wsl, hbf_sl[dd]], writes=cx.psl(tt),
                         desc=f"{pr} {which}mm{c}.{tt}.{dd}")
            for hh2 in range(2):
                P.do("scalar", lambda e, hh2=hh2: e.activation(
                    rsc[:, 0, hh2 * 1024:(hh2 + 1) * 1024],
                    cx.ps(2 * hh2, 0, 1024), AF.Copy),
                    reads=cx.psl(2 * hh2, 2 * hh2 + 1), writes=[rsc_sl],
                    desc=f"{pr} {which}ef{c}.{hh2}")
            P.do("vector", lambda e: e.tensor_copy(qcb[:, :], cx.ps(0, 0, 2048)),
                 reads=cx.psl(0, 1, 2, 3), writes=[qcb_sl], desc=f"{pr} {which}eb{c}")
            for tt in range(4):
                P.do("tensor", lambda e, tt=tt: e.matmul(
                    cx.ps(4 + tt), cx.t["c_pswap_l0"].ap(),
                    qcb[:, tt * 512: tt * 512 + 512], start=True, stop=True),
                    reads=[qcb_sl, cx.csl["c_pswap_l0"]], writes=cx.psl(4 + tt),
                    desc=f"{pr} {which}pp{c}.{tt}")
            P.do("vector", lambda e: e.tensor_mul(rsc[:, 0, :], rsc[:, 0, :],
                                                  tabC[:, :]),
                 reads=[rsc_sl, tab_sl], writes=[rsc_sl], desc=f"{pr} {which}tc{c}")
            for tt in range(4):
                P.do("vector", lambda e, tt=tt: e.tensor_mul(
                    rsc[:, 1, tt * 512: tt * 512 + 512], cx.ps(4 + tt),
                    tabS[:, tt * 512: tt * 512 + 512]),
                    reads=cx.psl(4 + tt) + [tab_sl], writes=[rsc_sl],
                    desc=f"{pr} {which}ts{c}.{tt}")
            P.do("vector", lambda e, dst=dst: e.tensor_add(
                dst[:, :], rsc[:, 0, :], rsc[:, 1, :]),
                reads=[rsc_sl], writes=[dst_sl], desc=f"{pr} {which}rot{c}")

        for hh in range(2):
            h = 2 * c + hh
            tc16 = h
            for kc in range(16):
                P.do("tensor", lambda e, kc=kc, h=h: e.matmul(
                    cx.ps(6, 0, 1, (0, 65)), vab[:, kc, h, 0:65],
                    cx.t["c_ones_col"].ap(), start=(kc == 0), stop=(kc == 15)),
                    reads=[vab_sl[kc], cx.csl["c_ones_col"]], writes=cx.psl(6),
                    desc=f"{pr} sv{h}.{kc}")
            P.do("vector", lambda e: e.tensor_copy(svc[0:65, 0:1],
                                                   cx.ps(6, 0, 1, (0, 65))),
                 reads=cx.psl(6), writes=[sv_sl], desc=f"{pr} svc{h}")
            P.do("tensor", lambda e: e.matmul(
                cx.ps(7, 0, 65, (0, 1)), svc[0:65, 0:1],
                cx.t["c_eye"].ap()[0:65, 0:65], start=True, stop=True),
                reads=[sv_sl, cx.csl["c_eye"]], writes=cx.psl(7),
                desc=f"{pr} svt{h}")
            P.do("vector", lambda e, h=h: e.tensor_copy(
                svr[0:1, h, 0:65], cx.ps(7, 0, 65, (0, 1))),
                reads=cx.psl(7), writes=[sv_sl], desc=f"{pr} svr{h}")

        for qt in range(4):
            q0 = qt * 512
            hA, hB = 2 * c, 2 * c + 1
            P.do("tensor", lambda e, hA=hA: e.matmul(
                cx.ps(6, 0, 512, (0, 65)), svr[0:1, hA, 0:65],
                cx.t["c_ones512"].ap(), start=True, stop=False),
                reads=[sv_sl, cx.csl["c_ones512"]], writes=cx.psl(6),
                desc=f"{pr} wsvA{c}.{qt}")
            P.do("tensor", lambda e, hB=hB: e.matmul(
                cx.ps(7, 0, 512, (64, 128)), svr[0:1, hB, 0:64],
                cx.t["c_ones512"].ap(), start=True, stop=False,
                tile_position=(0, 64)),
                reads=[sv_sl, cx.csl["c_ones512"]], writes=cx.psl(7),
                desc=f"{pr} wsvB{c}.{qt}")
            P.do("tensor", lambda e, hB=hB: e.matmul(
                cx.ps(5, 0, 512, (0, 1)), svr[0:1, hB, 64:65],
                cx.t["c_ones512"].ap(), start=True, stop=False),
                reads=[sv_sl, cx.csl["c_ones512"]], writes=cx.psl(5),
                desc=f"{pr} wsvBd{c}.{qt}")
            for kc in range(16):
                kb = kc % 2
                P.do("tensor", lambda e, c=c, kc=kc, kb=kb, q0=q0: e.matmul(
                    cx.ps(kb), kfm[0:64, kc * 128:(kc + 1) * 128],
                    qfm[0:64, q0:q0 + 512], start=True, stop=True),
                    reads=[kfm_sl, qfm_sl], writes=cx.psl(kb),
                    desc=f"{pr} sA{c}.{qt}.{kc}")
                P.do("tensor", lambda e, c=c, kc=kc, kb=kb, q0=q0: e.matmul(
                    cx.ps(2 + kb), kfm[64:128, kc * 128:(kc + 1) * 128],
                    qfm[64:128, q0:q0 + 512], start=True, stop=True,
                    tile_position=(64, 0)),
                    reads=[kfm_sl, qfm_sl], writes=cx.psl(2 + kb),
                    desc=f"{pr} sB{c}.{qt}.{kc}")
                if kb == 1:
                    g = 0
                    P.do("scalar", lambda e, g=g: e.activation(
                        tpA[:, g, :], cx.ps(0, 0, 1024), AF.Square,
                        bias=cx.fcol(1.0, 0, 128), scale=0.125),
                        reads=cx.psl(0, 1) + [cx.csl["c_fconsts"]],
                        writes=[tpA_sl[g]], desc=f"{pr} tA{c}.{qt}.{kc}")
                    P.do("vector", lambda e, g=g: e.tensor_scalar(
                        rsc[:, 0, 0:1024], cx.ps(2, 0, 1024), 0.125, 1.0,
                        ALU.mult, ALU.add),
                        reads=cx.psl(2, 3), writes=[rsc_sl],
                        desc=f"{pr} tB1{c}.{qt}.{kc}")
                    P.do("vector", lambda e, g=g: e.tensor_mul(
                        tpB[:, g, :], rsc[:, 0, 0:1024], rsc[:, 0, 0:1024]),
                        reads=[rsc_sl], writes=[tpB_sl[g]],
                        desc=f"{pr} tB2{c}.{qt}.{kc}")
                    for k2 in (kc - 1, kc):
                        gg = (k2 % 2) * 512
                        P.do("tensor", lambda e, k2=k2, gg=gg, g=g, hA=hA: e.matmul(
                            cx.ps(6, 0, 512, (0, 65)), vab[:, k2, hA, 0:65],
                            tpA[:, g, gg:gg + 512], start=False,
                            stop=(k2 == 15)),
                            reads=[tpA_sl[g], vab_sl[k2]], writes=cx.psl(6),
                            desc=f"{pr} avA{c}.{qt}.{k2}")
                        P.do("tensor", lambda e, k2=k2, gg=gg, g=g, hB=hB: e.matmul(
                            cx.ps(7, 0, 512, (64, 128)), vab[:, k2, hB, 0:64],
                            tpB[:, g, gg:gg + 512], start=False,
                            stop=(k2 == 15), tile_position=(0, 64)),
                            reads=[tpB_sl[g], vab_sl[k2]], writes=cx.psl(7),
                            desc=f"{pr} avB{c}.{qt}.{k2}")
                        P.do("tensor", lambda e, k2=k2, gg=gg, g=g, hB=hB: e.matmul(
                            cx.ps(5, 0, 512, (0, 1)), vab[:, k2, hB, 64:65],
                            tpB[:, g, gg:gg + 512], start=False,
                            stop=(k2 == 15)),
                            reads=[tpB_sl[g], vab_sl[k2]], writes=cx.psl(5),
                            desc=f"{pr} avBd{c}.{qt}.{k2}")
            P.do("vector", lambda e: e.reciprocal(
                asc[64:65, 0:512], cx.ps(6, 0, 512, (64, 65))),
                reads=cx.psl(6), writes=[asc_sl], desc=f"{pr} rcA{c}.{qt}")
            P.do("tensor", lambda e: e.matmul(
                cx.ps(4, 0, 512, (0, 64)), cx.t["c_ones_row"].ap()[64:65, 0:64],
                asc[64:65, 0:512], start=True, stop=True),
                reads=[asc_sl, cx.csl["c_ones_row"]], writes=cx.psl(4),
                desc=f"{pr} rbA{c}.{qt}")
            P.do("vector", lambda e: e.tensor_copy(asc[0:64, 0:512],
                                                   cx.ps(4, 0, 512, (0, 64))),
                 reads=cx.psl(4), writes=[asc_sl], desc=f"{pr} reA{c}.{qt}")
            P.do("vector", lambda e, c=c, q0=q0: e.tensor_mul(
                wvu[0:64, c, q0:q0 + 512], cx.ps(6, 0, 512, (0, 64)),
                asc[0:64, 0:512]),
                reads=cx.psl(6) + [asc_sl], writes=[wvu_sl[c]],
                desc=f"{pr} wvA{c}.{qt}")
            P.do("vector", lambda e: e.reciprocal(
                asc[0:1, 0:512], cx.ps(5, 0, 512, (0, 1))),
                reads=cx.psl(5), writes=[asc_sl], desc=f"{pr} rcB{c}.{qt}")
            P.do("tensor", lambda e: e.matmul(
                cx.ps(4, 0, 512, (64, 128)), cx.t["c_ones_row"].ap()[0:1, 0:64],
                asc[0:1, 0:512], start=True, stop=True, tile_position=(0, 64)),
                reads=[asc_sl, cx.csl["c_ones_row"]], writes=cx.psl(4),
                desc=f"{pr} rbB{c}.{qt}")
            P.do("vector", lambda e: e.tensor_copy(asc[64:128, 0:512],
                                                   cx.ps(4, 0, 512, (64, 128))),
                 reads=cx.psl(4), writes=[asc_sl], desc=f"{pr} reB{c}.{qt}")
            P.do("vector", lambda e, c=c, q0=q0: e.tensor_mul(
                wvu[64:128, c, q0:q0 + 512], cx.ps(7, 0, 512, (64, 128)),
                asc[64:128, 0:512]),
                reads=cx.psl(7) + [asc_sl], writes=[wvu_sl[c]],
                desc=f"{pr} wvB{c}.{qt}")

    for f in range(NCH):
        db = f % 2
        pb = 0 if db == 0 else 4
        P.do("gpsimd", lambda e, f=f, db=db: e.dma_start(
            out=wqs[:, db, :, :],
            in_=WO[:, f * 128:(f + 1) * 128].rearrange("(cc p) o -> p cc o", p=128)),
            reads=(), writes=[wqs_sl[db]], dma=True, qkey=f"l0a_wq{db}",
            desc=f"{pr} ldwo{f}")
        P.do("sync", lambda e, f=f: e.dma_start(
            out=xst[:, 0, :], in_=xin[b, f * 128:(f + 1) * 128, :]),
            reads=(), writes=[xst_sl[0]], dma=True, qkey="l0a_xst",
            desc=f"{pr} ldx3_{f}")
        for tt in range(4):
            for dd in range(NCH):
                P.do("tensor", lambda e, f=f, db=db, tt=tt, dd=dd, pb=pb: e.matmul(
                    cx.ps(pb + tt), wqs[:, db, dd, :],
                    wvu[:, dd, tt * 512: tt * 512 + 512],
                    start=(dd == 0), stop=(dd == NCH - 1)),
                    reads=[wqs_sl[db], wvu_sl[dd]], writes=cx.psl(pb + tt),
                    desc=f"{pr} owmm{f}.{tt}.{dd}")
        P.do("vector", lambda e, f=f, pb=pb: e.tensor_add(
            xst[:, 0, :], xst[:, 0, :], cx.ps(pb, 0, 2048)),
            reads=cx.psl(pb, pb + 1, pb + 2, pb + 3) + [xst_sl[0]],
            writes=[xst_sl[0]], desc=f"{pr} oadd{f}")
        P.do("sync", lambda e, f=f: e.dma_start(
            out=xout[b, f * 128:(f + 1) * 128, :], in_=xst[:, 0, :]),
            reads=[xst_sl[0]], writes=[sl(f"{pr}_xo{f}")], dma=True,
            qkey="l0a_xo", desc=f"{pr} stx{f}")
    P.barrier()


def l0a_dram(nc, cx):
    dr = {}
    dr["L0_attn_w_q"] = nc.dram_tensor("L0_attn_w_q", [D, D], F32,
                                       kind="ExternalInput").ap()
    dr["L0_attn_w_kv"] = nc.dram_tensor("L0_attn_w_kv", [D, 2 * D], F32,
                                        kind="ExternalInput").ap()
    dr["L0_attn_w_o"] = nc.dram_tensor("L0_attn_w_o", [D, D], F32,
                                       kind="ExternalInput").ap()
    cx.dram.update(dr)
    for nm, sh in [("l0a_lnwb", [128, 16]), ("l0a_fbsgn", [128, 2]),
                   ("l0a_theta", [1, 1])]:
        t = nc.dram_tensor(nm + "_hbm", sh, F32, kind="ExternalInput")
        cx.dram[nm + "_hbm"] = t.ap()
        cx.galloc(nm, sh, F32)
        cx.csl[nm] = cx.P.slot("const_" + nm)
        cx.P.qfinal["kparam"] = True
        cx.P.do("sync", lambda e, nm=nm: e.dma_start(
            out=cx.t[nm].ap(), in_=cx.dram[nm + "_hbm"]),
            reads=(), writes=[cx.csl[nm]], dma=True, qkey="kparam", desc=f"ld {nm}")


def l0a_host_inputs(inputs):
    fb, sgn = host_l0_rope_consts()
    lnwb = np.zeros((128, 16), np.float32)
    lnwb[:, 0:8] = col128(inputs["L0_attn_ln_w"])
    lnwb[:, 8:16] = col128(inputs["L0_attn_ln_b"])
    fbsgn = np.concatenate([fb, sgn], axis=1)
    return {
        "L0_attn_w_q": _f32(inputs["L0_attn_w_q"]),
        "L0_attn_w_kv": _f32(inputs["L0_attn_w_kv"]),
        "L0_attn_w_o": _f32(inputs["L0_attn_w_o"]),
        "l0a_lnwb_hbm": lnwb,
        "l0a_fbsgn_hbm": _f32(fbsgn),
        "l0a_theta_hbm": _f32(np.asarray(inputs["L0_attn_theta"]).reshape(1, 1)),
    }


def build_program_l0a():
    nc = bass.Bass("TRN2", target_bir_lowering=False)
    P = Prog()
    cx = build_common(nc, P)
    cx.dram["x_in"] = nc.dram_tensor("x_in", [NB, D, T], F32,
                                     kind="ExternalInput").ap()
    cx.dram["x_out"] = nc.dram_tensor("x_out", [NB, D, T], F32,
                                      kind="ExternalOutput").ap()
    for nm, (sh, _dt) in CONST_SHAPES.items():
        cx.dram[nm] = nc.dram_tensor(nm, sh, F32, kind="ExternalInput").ap()
    emit_load_consts(cx)
    l0a_dram(nc, cx)
    for b in range(NB):
        emit_l0a(cx, b, cx.dram["x_in"], cx.dram["x_out"])
    P.do("gpsimd", lambda e: e.nop(),
         reads=[P.slot(f"l0a{b}_xo{c}") for b in range(NB) for c in range(NCH)],
         desc="final wait")
    finish_program(nc, P)
    return nc, P



def emit_l1f_entry(cx, b, xin, ybuf):
    cx.phase_begin()
    cx.palloc("rows", [1, 4 * T], F32)
    hbf = cx.palloc("e_hbf", [128, NCH, T], BF16).ap()
    xst = cx.palloc("e_xst", [128, 1, T], F32).ap()
    xsq = cx.palloc("e_xsq", [128, 1, T], BF16).ap()
    P = cx.P
    pr = f"l1fe{b}"
    hbf_sl = [P.slot(f"{pr}_h{c}") for c in range(NCH)]
    sq_sl = [P.slot(f"{pr}_sq0")]
    xst_sl = [P.slot(f"{pr}_xst")]
    y_sl = [P.slot(f"yb{b}_{c}") for c in range(NCH)]
    emit_ln_spill(cx, pr, b, xin, hbf, hbf_sl, xst, xst_sl, xsq, sq_sl,
                  spill=ybuf, spill_sl=y_sl, spill_qk=f"l1f_y{b}_",
                  xst_qk="l1f_xst")
    P.barrier()


def emit_l1f_exit(cx, b, xin, ybuf, xout):
    cx.phase_begin()
    xst = cx.palloc("x_xst", [128, 1, T], F32).ap()
    yst = cx.palloc("x_yst", [128, 1, T], F32).ap()
    P = cx.P
    pr = f"l1fx{b}"
    xst_sl = [P.slot(f"{pr}_xst")]
    yst_sl = [P.slot(f"{pr}_yst")]
    for c in range(NCH):
        P.do("sync", lambda e, c=c: e.dma_start(
            out=xst[:, 0, :], in_=xin[b, c * 128:(c + 1) * 128, :]),
            reads=(), writes=[xst_sl[0]], dma=True, qkey="l1f_xst",
            desc=f"{pr} ldx{c}")
        P.do("scalar", lambda e, c=c: e.dma_start(
            out=yst[:, 0, :], in_=ybuf[b, c * 128:(c + 1) * 128, :]),
            reads=[P.slot(f"yb{b}_{c}")], writes=[yst_sl[0]], dma=True,
            qkey="l1f_yst", desc=f"{pr} ldy{c}")
        P.do("vector", lambda e: e.tensor_add(xst[:, 0, :], xst[:, 0, :],
                                              yst[:, 0, :]),
             reads=[xst_sl[0], yst_sl[0]], writes=[xst_sl[0]], desc=f"{pr} add{c}")
        P.do("sync", lambda e, c=c: e.dma_start(
            out=xout[b, c * 128:(c + 1) * 128, :], in_=xst[:, 0, :]),
            reads=[xst_sl[0]], writes=[P.slot(f"{pr}_xo{c}")], dma=True,
            qkey="l1f_xo", desc=f"{pr} stx{c}")
    P.barrier()


def _l1f_attn_alloc(cx):
    cx.phase_begin()
    t = {}
    t["W"] = cx.palloc("a_W", [128, 8, 3072], BF16).ap()
    t["wrk"] = cx.palloc("a_wrk", [128, 4, T], F32, at="a_W").ap()
    t["Wo"] = cx.palloc("a_Wo", [128, 8, 1024], BF16, at="a_W", off2=32768).ap()
    t["xsb"] = cx.palloc("a_xsb", [128, NCH, T], BF16).ap()
    t["vab"] = cx.palloc("a_vab", [128, 16, 16, 65], BF16).ap()
    t["qf"] = cx.palloc("a_qf", [128, NCH, T], BF16).ap()
    t["kf"] = cx.palloc("a_kf", [128, NCH, T], BF16).ap()
    t["rn2"] = cx.palloc("a_rn2", [2, T], F32).ap()
    t["pn"] = cx.palloc("a_pn", [1, T], F32, at="a_rn2").ap()
    t["yst"] = cx.palloc("a_yst", [128, T], F32).ap()
    t["Mbd"] = cx.palloc("a_Mbd", [1, 2, 8, 512], BF16, at="a_W",
                         off2=32768).ap()
    t["sqr"] = cx.palloc("a_sqr", [128, T], BF16, at="a_W", off2=32768).ap()
    t["scr"] = cx.palloc("a_scr", [128, T], F32).ap()
    t["sq"] = cx.palloc("a_sq", [128, T], BF16, at="a_scr").ap()
    t["et"] = cx.palloc("a_et", [128, 2, 512], BF16, at="a_scr", off2=4096).ap()
    t["asc"] = cx.palloc("a_asc", [128, 512], F32, at="a_scr", off2=6144).ap()
    t["kh"] = cx.palloc("a_kh", [128, 8, 2, 8], BF16).ap()
    t["bm"] = cx.palloc("a_bm", [128, 2, 8], F32).ap()
    return t


def emit_l1f_attn_weights(cx, t, l):
    P = cx.P
    for qq in range(4):
        P.do("gpsimd", lambda e, l=l, qq=qq: e.dma_start(
            out=t["W"][:, 2 * qq:2 * qq + 2, :],
            in_=cx.dram["L1f_Wqkv"][l].rearrange(
                "(c p) f -> p c f", p=128)[:, 2 * qq:2 * qq + 2, :]),
            reads=(), writes=[P.slot("a_W"), P.slot("a_wrk"), P.slot("a_Wo"),
                              P.slot("a_Mbd"), P.slot("a_sqr")],
            dma=True, qkey=f"l1f_W{qq}", desc=f"l1f ldW{l}.{qq}")


def emit_l1f_attn(cx, t, l, b, ybuf):
    nc, P = cx.nc, cx.P
    sl = P.slot
    pr = f"l1fa{l}_{b}"
    W, wrk, Wo = t["W"], t["wrk"], t["Wo"]
    xsb, vab, qf, kf = t["xsb"], t["vab"], t["qf"], t["kf"]
    rn2, pn, Mbd, sqr = t["rn2"], t["pn"], t["Mbd"], t["sqr"]
    scr, sq, et, asc = t["scr"], t["sq"], t["et"], t["asc"]
    yst, kh, bm = t["yst"], t["kh"], t["bm"]
    W_sl, wrk_sl, Wo_sl = sl("a_W"), sl("a_wrk"), sl("a_Wo")
    xsb_sl = [sl(f"{pr}_xs{c}") for c in range(NCH)]
    vab_sl = [sl(f"{pr}_v{tc}") for tc in range(16)]
    qf_sl = [sl(f"{pr}_qf{c}") for c in range(NCH)]
    kf_sl = [sl(f"{pr}_kf{c}") for c in range(NCH)]
    rn_sl = sl("a_rn2")
    pn_sl = rn_sl
    scr_sl = sl("a_scr")
    sq_sl = sl("a_sq")
    et_sl = sl("a_et")
    asc_sl = sl("a_asc")
    SCR_ALL = [scr_sl, sq_sl, et_sl, asc_sl]
    yst_sl = sl("a_yst")
    mbd_sl = sl("a_Mbd")
    sqr_sl = sl("a_sqr")
    kh_sl, bm_sl = sl("a_kh"), sl("a_bm")
    wv_sl = [sl(f"{pr}_wv{c}") for c in range(NCH)]
    y_sl = [sl(f"yb{b}_{c}") for c in range(NCH)]
    qsc = cx.t["l1f_qks"].ap()

    for qq in range(4):
        P.do("gpsimd", lambda e, l=l, qq=qq: e.dma_start(
            out=W[:, 2 * qq:2 * qq + 2, :],
            in_=cx.dram["L1f_Wqkv"][l].rearrange(
                "(c p) f -> p c f", p=128)[:, 2 * qq:2 * qq + 2, :]),
            reads=(), writes=[W_sl, wrk_sl, Wo_sl, mbd_sl, sqr_sl], dma=True,
            qkey=f"l1f_W{qq}", desc=f"{pr} ldW.{qq}")

    for c in range(NCH):
        if c < 4:
            P.do("gpsimd", lambda e, c=c: e.dma_start(
                out=xsb[:, c, :], in_=ybuf[b, c * 128:(c + 1) * 128, :]),
                reads=[y_sl[c]], writes=[xsb_sl[c]], dma=True, qkey=f"l1f_xsb{c}",
                desc=f"{pr} shc{c}")
        else:
            P.do("gpsimd", lambda e, c=c: e.dma_start(
                out=xsb[:, c, 1:T], in_=ybuf[b, c * 128:(c + 1) * 128, 0:T - 1]),
                reads=[y_sl[c]], writes=[xsb_sl[c]], dma=True, qkey=f"l1f_xsb{c}",
                desc=f"{pr} shc{c}")
            P.do("vector", lambda e, c=c: e.memset(xsb[:, c, 0:1], 0.0),
                 reads=(), writes=[xsb_sl[c]], desc=f"{pr} shz{c}")

    P.do("vector", lambda e: e.tensor_copy(
        vab[:, :, :, 64:65],
        bass.AP(cx.t["c_ones_col"], 0, [[1, 128], [0, 16], [0, 16], [0, 1]])),
        reads=[cx.csl["c_ones_col"]],
        writes=[sl(f"{pr}_v{tc}") for tc in range(16)],
        desc=f"{pr} vones")
    for tc in range(16):
        pb = (tc % 2) * 2
        for fh in range(2):
            for dd in range(NCH):
                P.do("tensor", lambda e, tc=tc, fh=fh, dd=dd, pb=pb: e.matmul(
                    cx.ps(pb + fh), xsb[:, dd, tc * 128:(tc + 1) * 128],
                    W[:, dd, 2048 + fh * 512: 2048 + (fh + 1) * 512],
                    start=(dd == 0), stop=(dd == NCH - 1)),
                    reads=[W_sl, xsb_sl[dd]], writes=cx.psl(pb + fh),
                    desc=f"{pr} vmm{tc}.{fh}.{dd}")
            P.do("vector", lambda e, tc=tc, fh=fh, pb=pb: e.tensor_copy(
                vab[:, tc, fh * 8:(fh + 1) * 8, 0:64], cx.ps(pb + fh)),
                reads=cx.psl(pb + fh), writes=[vab_sl[tc]],
                desc=f"{pr} vev{tc}.{fh}")

    for which, dstt, dst_sls, wof in (("q", qf, qf_sl, 0), ("k", kf, kf_sl, 1024)):
        for c in range(NCH):
            pb = 0 if c % 2 == 0 else 4
            for tt in range(4):
                for dd in range(NCH):
                    P.do("tensor", lambda e, c=c, tt=tt, dd=dd, wof=wof, pb=pb:
                         e.matmul(cx.ps(pb + tt),
                                  W[:, dd, wof + c * 128: wof + (c + 1) * 128],
                                  xsb[:, dd, tt * 512: tt * 512 + 512],
                                  start=(dd == 0), stop=(dd == NCH - 1)),
                         reads=[W_sl, xsb_sl[dd]], writes=cx.psl(pb + tt),
                         desc=f"{pr} {which}mm{c}.{tt}.{dd}")
            P.do("vector", lambda e, c=c, pb=pb, dstt=dstt: e.tensor_copy(
                dstt[:, c, :], cx.ps(pb, 0, 2048)),
                reads=cx.psl(pb, pb + 1, pb + 2, pb + 3), writes=[dst_sls[c]],
                desc=f"{pr} {which}ev{c}")

    for i in range(4):
        P.do("scalar", lambda e, i=i, l=l: e.dma_start(
            out=wrk[:, i, :], in_=cx.dram["L1f_rot"][i]),
            reads=(), writes=[wrk_sl, W_sl], dma=True, qkey="l1f_wrk",
            desc=f"{pr} ldrot{i}")
    for i in range(4):
        col = 4 * l + i
        P.do("vector", lambda e, i=i, col=col: e.tensor_scalar_mul(
            wrk[:, i, :], wrk[:, i, :], qsc[:, col:col + 1]),
            reads=[wrk_sl, cx.csl["l1f_qks"]], writes=[wrk_sl],
            desc=f"{pr} wrks{i}")

    steps = []
    for c in range(NCH):
        for which in ("k", "q"):
            steps.append((c, which))

    def rope_square(idx):
        c, which = steps[idx]
        dstt = kf if which == "k" else qf
        dst_sls = kf_sl if which == "k" else qf_sl
        P.do("scalar", lambda e, c=c, dstt=dstt: e.activation(
            sqr[:, :], dstt[:, c, :], AF.Square),
            reads=[dst_sls[c]], writes=[sqr_sl, mbd_sl, W_sl],
            desc=f"{pr} {which}sq{c}")

    rope_square(0)
    for idx, (c, which) in enumerate(steps):
        dstt = kf if which == "k" else qf
        dst_sls = kf_sl if which == "k" else qf_sl
        ti0, ti1 = (2, 3) if which == "k" else (0, 1)
        par = idx % 2
        pb = 0 if par == 0 else 4
        gb = 4 if par == 0 else 0
        for tt in range(4):
            P.do("tensor", lambda e, tt=tt, gb=gb: e.matmul(
                cx.ps(gb + tt, 0, 512, (0, 2)), cx.t["c_sel2"].ap(),
                sqr[:, tt * 512: tt * 512 + 512], start=True, stop=True),
                reads=[sqr_sl, cx.csl["c_sel2"]], writes=cx.psl(gb + tt),
                desc=f"{pr} {which}ss{c}.{tt}")
        for tt in range(4):
            P.do("tensor", lambda e, tt=tt, c=c, dstt=dstt, pb=pb: e.matmul(
                cx.ps(pb + tt), cx.t["c_pswap_l1"].ap(),
                dstt[:, c, tt * 512: tt * 512 + 512], start=True, stop=True),
                reads=[dst_sls[c], cx.csl["c_pswap_l1"]], writes=cx.psl(pb + tt),
                desc=f"{pr} {which}P{c}.{tt}")
        if idx + 1 < len(steps):
            rope_square(idx + 1)
        P.do("vector", lambda e, gb=gb: e.tensor_copy(
            rn2[0:2, :], cx.ps(gb, 0, 2048, (0, 2))),
            reads=cx.psl(gb, gb + 1, gb + 2, gb + 3), writes=[rn_sl],
            desc=f"{pr} {which}se{c}")
        P.do("scalar", lambda e: e.activation(rn2[0:2, :], rn2[0:2, :], AF.Ln,
                                              bias=cx.fcol(1e-30, 0, 2)),
             reads=[rn_sl, cx.csl["c_fconsts"]], writes=[rn_sl],
             desc=f"{pr} {which}ln{c}")
        P.do("scalar", lambda e: e.activation(rn2[0:2, :], rn2[0:2, :], AF.Exp,
                                              scale=-0.5),
             reads=[rn_sl], writes=[rn_sl], desc=f"{pr} {which}ex{c}")
        P.do("vector", lambda e: e.tensor_scalar_min(rn2[0:2, :], rn2[0:2, :],
                                                     1e12),
             reads=[rn_sl], writes=[rn_sl], desc=f"{pr} {which}mx{c}")
        for tt in range(4):
            P.do("tensor", lambda e, tt=tt, gb=gb: e.matmul(
                cx.ps(gb + tt), cx.t["c_sel2T32"].ap(),
                rn2[0:2, tt * 512: tt * 512 + 512], start=True, stop=True),
                reads=[rn_sl, cx.csl["c_sel2T32"]], writes=cx.psl(gb + tt),
                desc=f"{pr} {which}G{c}.{tt}")
        P.do("gpsimd", lambda e, c=c, dstt=dstt, ti0=ti0: e.tensor_mul(
            scr[:, :], dstt[:, c, :], wrk[:, ti0, :]),
            reads=[dst_sls[c], wrk_sl], writes=SCR_ALL,
            desc=f"{pr} {which}t1{c}")
        P.do("vector", lambda e, ti1=ti1, pb=pb: e.tensor_mul(
            yst[:, :], cx.ps(pb, 0, 2048), wrk[:, ti1, :]),
            reads=cx.psl(pb, pb + 1, pb + 2, pb + 3) + [wrk_sl],
            writes=[yst_sl], desc=f"{pr} {which}t2{c}")
        P.do("gpsimd", lambda e: e.tensor_add(scr[:, :], scr[:, :], yst[:, :]),
             reads=[scr_sl, yst_sl], writes=SCR_ALL, desc=f"{pr} {which}t3{c}")
        P.do("vector", lambda e, c=c, dstt=dstt, gb=gb: e.tensor_mul(
            dstt[:, c, :], scr[:, :], cx.ps(gb, 0, 2048)),
            reads=SCR_ALL + cx.psl(gb, gb + 1, gb + 2, gb + 3),
            writes=[dst_sls[c]], desc=f"{pr} {which}rot{c}")
        if which == "k":
            P.do("scalar", lambda e, c=c: e.activation(sq[:, :], kf[:, c, :],
                                                       AF.Abs),
                 reads=[kf_sl[c]], writes=[sq_sl], desc=f"{pr} kabs{c}")
            P.do("vector", lambda e: e.reduce_max(
                bm[:, 0, :], sq.rearrange("p (w u) -> p w u", u=256),
                axis=AX.X),
                reads=[sq_sl], writes=[bm_sl], desc=f"{pr} km{c}")
            P.do("vector", lambda e: e.tensor_max(
                bm[:, 1, 1:8], bm[:, 0, 1:8], bm[:, 0, 0:7]),
                reads=[bm_sl], writes=[bm_sl], desc=f"{pr} km2{c}")
            P.do("vector", lambda e: e.tensor_copy(bm[:, 1, 0:1], bm[:, 0, 0:1]),
                 reads=[bm_sl], writes=[bm_sl], desc=f"{pr} km3{c}")
            for hh in range(2):
                P.do("vector", lambda e, c=c, hh=hh: e.tensor_scalar_mul(
                    kh[:, c, hh, :], bm[:, 1, :],
                    cx.t["c_sel232"].ap()[:, hh:hh + 1]),
                    reads=[bm_sl, cx.csl["c_sel232"]], writes=[kh_sl],
                    desc=f"{pr} km4{c}.{hh}")

    P.do("gpsimd", lambda e: e.memset(asc[64:128, 0:512], 0.0),
         reads=(), writes=[asc_sl], desc=f"{pr} ascz")

    pend = [None]

    def rdst(wp):
        par = wp % 2
        if par == 0:
            return asc[0:1, 0:512], asc_sl
        return rn2[0:1, 0:512], rn_sl

    def emit_recip(c, hh, wp):
        par = wp % 2
        if hh == 0:
            numb = 6 + par
            P.do("vector", lambda e, numb=numb: e.reciprocal(
                asc[64:65, 0:512], cx.ps(numb, 0, 512, (64, 65))),
                reads=cx.psl(numb), writes=[asc_sl],
                desc=f"{pr} rc{c}.{hh}.{wp}")
        else:
            denb = 4 + par
            dst, dsl = rdst(wp)
            P.do("vector", lambda e, denb=denb, dst=dst: e.reciprocal(
                dst, cx.ps(denb, 0, 512, (0, 1))),
                reads=cx.psl(denb), writes=[dsl], desc=f"{pr} rc{c}.{hh}.{wp}")

    def flush_tail():
        if pend[0] is None:
            return
        c, hh, wp, q0 = pend[0]
        pend[0] = None
        par = wp % 2
        denb = 4 + par
        numb = 6 + par
        dst, dsl = rdst(wp)
        if hh == 0:
            P.do("tensor", lambda e, denb=denb: e.matmul(
                cx.ps(denb, 0, 512, (0, 64)),
                cx.t["c_e64"].ap()[64:128, 0:64],
                asc[64:128, 0:512], start=True, stop=True,
                tile_position=(64, 0)),
                reads=[asc_sl, cx.csl["c_e64"]], writes=cx.psl(denb),
                desc=f"{pr} rbA{c}.{wp}")
            P.do("vector", lambda e, denb=denb: e.tensor_copy(
                scr[0:64, 0:512], cx.ps(denb, 0, 512, (0, 64))),
                reads=cx.psl(denb), writes=[scr_sl, sq_sl],
                desc=f"{pr} reA{c}.{wp}")
            P.do("vector", lambda e, c=c, q0=q0, numb=numb: e.tensor_mul(
                xsb[0:64, c, q0:q0 + 512], cx.ps(numb, 0, 512, (0, 64)),
                scr[0:64, 0:512]),
                reads=cx.psl(numb) + [scr_sl, sq_sl],
                writes=[wv_sl[c], xsb_sl[c]], desc=f"{pr} wvA{c}.{wp}")
        else:
            P.do("tensor", lambda e, denb=denb, dst=dst: e.matmul(
                cx.ps(denb, 0, 512, (64, 128)),
                cx.t["c_ones_row"].ap()[0:1, 0:64],
                dst, start=True, stop=True,
                tile_position=(0, 64)),
                reads=[dsl, cx.csl["c_ones_row"]], writes=cx.psl(denb),
                desc=f"{pr} rbB{c}.{wp}")
            P.do("vector", lambda e, denb=denb: e.tensor_copy(
                scr[64:128, 0:512], cx.ps(denb, 0, 512, (64, 128))),
                reads=cx.psl(denb), writes=[scr_sl, sq_sl],
                desc=f"{pr} reB{c}.{wp}")
            P.do("vector", lambda e, c=c, q0=q0, numb=numb: e.tensor_mul(
                xsb[64:128, c, q0:q0 + 512], cx.ps(numb, 0, 512, (64, 128)),
                scr[64:128, 0:512]),
                reads=cx.psl(numb) + [scr_sl, sq_sl],
                writes=[wv_sl[c], xsb_sl[c]], desc=f"{pr} wvB{c}.{wp}")

    av_pend = [None]

    def emit_av():
        if av_pend[0] is None:
            return None
        c, hh, wp, wi, live, par, h = av_pend[0]
        av_pend[0] = None
        denb = 4 + par
        numb = 6 + par
        for ii, k2 in enumerate(live):
            bk, off = divmod(ii, 2)
            off *= 256
            st = (ii == 0)
            sp = (ii == len(live) - 1)
            if hh == 0:
                P.do("tensor", lambda e, k2=k2, bk=bk, off=off,
                     wi=wi, h=h, st=st, sp=sp, numb=numb: e.matmul(
                         cx.ps(numb, wi * 256, 256, (0, 65)),
                         vab[:, k2, h, 0:65],
                         et[:, bk, off:off + 256],
                         start=st, stop=sp, tile_position=(0, 0)),
                     reads=[et_sl, vab_sl[k2]], writes=cx.psl(numb),
                     desc=f"{pr} av{c}.{hh}.{2 * wp + wi}.{k2}")
            else:
                P.do("tensor", lambda e, k2=k2, bk=bk, off=off,
                     wi=wi, h=h, st=st, sp=sp, numb=numb: e.matmul(
                         cx.ps(numb, wi * 256, 256, (64, 128)),
                         vab[:, k2, h, 0:64],
                         et[:, bk, off:off + 256],
                         start=st, stop=sp, tile_position=(0, 64)),
                     reads=[et_sl, vab_sl[k2]], writes=cx.psl(numb),
                     desc=f"{pr} av{c}.{hh}.{2 * wp + wi}.{k2}")
                P.do("tensor", lambda e, k2=k2, bk=bk, off=off,
                     wi=wi, st=st, sp=sp, denb=denb: e.matmul(
                         cx.ps(denb, wi * 256, 256, (0, 1)),
                         cx.t["c_ones_col"].ap(),
                         et[:, bk, off:off + 256],
                         start=st, stop=sp),
                     reads=[et_sl, cx.csl["c_ones_col"]],
                     writes=cx.psl(denb),
                     desc=f"{pr} avd{c}.{hh}.{2 * wp + wi}.{k2}")
        if wi == 1:
            flush_tail()
            emit_recip(c, hh, wp)
            pend[0] = (c, hh, wp, wp * 512)
        return None

    for c in range(NCH):
        P.do("scalar", lambda e, c=c: e.activation(sq[:, 0:2048], qf[:, c, :],
                                                   AF.Abs),
             reads=[qf_sl[c]], writes=[sq_sl], desc=f"{pr} qabs{c}")
        for hh in range(2):
            for w in range(8):
                P.do("tensor", lambda e, c=c, hh=hh, w=w: e.matmul(
                    cx.ps(w // 2, (w % 2) * 256, 256, (0, 1)),
                    kh[:, c, hh, w:w + 1], sq[:, w * 256:(w + 1) * 256],
                    start=True, stop=True),
                    reads=[sq_sl, kh_sl], writes=cx.psl(w // 2),
                    desc=f"{pr} mb{c}.{hh}.{w}")
            for bb in range(4):
                P.do("vector", lambda e, hh=hh, bb=bb: e.tensor_scalar_mul(
                    Mbd[0:1, hh, 2 * bb:2 * bb + 2, :].rearrange(
                        "p w (d q) -> p w d q", d=2),
                    bass.AP(cx.PSt, bb * 512,
                            [[4096, 1], [256, 2], [0, 2], [1, 256]]),
                    -1.0),
                    reads=cx.psl(bb), writes=[mbd_sl, W_sl, sqr_sl],
                    desc=f"{pr} mbe{c}.{hh}.{bb}")
        for hh in range(2):
            h = 2 * c + hh
            p0 = hh * 64
            for wp in range(4):
                par = wp % 2
                for wi in range(2):
                    w = 2 * wp + wi
                    kcs = [2 * w - 2, 2 * w - 1, 2 * w, 2 * w + 1]
                    live = [k2 for k2 in kcs if k2 >= 0]
                    for ii, k2 in enumerate(live):
                        bk, off = divmod(ii, 2)
                        bk += 2 * par
                        off *= 256
                        P.do("tensor", lambda e, k2=k2, bk=bk, off=off, w=w,
                             p0=p0, c=c: e.matmul(
                                 cx.ps(bk, off, 256),
                                 kf[p0:p0 + 64, c, k2 * 128:(k2 + 1) * 128],
                                 qf[p0:p0 + 64, c, w * 256:(w + 1) * 256],
                                 start=True, stop=False,
                                 tile_position=(p0, 0)),
                             reads=[kf_sl[c], qf_sl[c]], writes=cx.psl(bk),
                             desc=f"{pr} s{c}.{hh}.{w}.{k2}")
                        pos = k2 - (2 * w - 2)
                        for qh in range(2):
                            if pos == 0:
                                kind = "tri_prev" if qh == 0 else "big"
                            elif pos == 1:
                                kind = None if qh == 0 else "tri_prev"
                            elif pos == 2:
                                kind = "tri_cur" if qh == 0 else None
                            else:
                                kind = "big" if qh == 0 else "tri_cur"
                            if kind is None:
                                continue
                            if kind == "big":
                                P.do("tensor", lambda e, bk=bk, off=off, qh=qh:
                                     e.matmul(cx.ps(bk, off + qh * 128, 128),
                                              cx.t["c_bigrow"].ap(),
                                              cx.t["c_onesbf"].ap(),
                                              start=False, stop=False),
                                     reads=[cx.csl["c_bigrow"],
                                            cx.csl["c_onesbf"]],
                                     writes=cx.psl(bk),
                                     desc=f"{pr} mkb{c}.{hh}.{w}.{k2}.{qh}")
                            else:
                                pat = ("c_mask_prevT" if kind == "tri_prev"
                                       else "c_mask_curT")
                                P.do("tensor", lambda e, bk=bk, off=off, qh=qh,
                                     pat=pat: e.matmul(
                                         cx.ps(bk, off + qh * 128, 128),
                                         cx.t[pat].ap(), cx.t["c_eye"].ap(),
                                         start=False, stop=False),
                                     reads=[cx.csl[pat], cx.csl["c_eye"]],
                                     writes=cx.psl(bk),
                                     desc=f"{pr} mkt{c}.{hh}.{w}.{k2}.{qh}")
                        P.do("tensor", lambda e, bk=bk, off=off, w=w, hh=hh:
                             e.matmul(cx.ps(bk, off, 256),
                                      cx.t["c_onesbf"].ap(),
                                      Mbd[0:1, hh, w, 0:256],
                                      start=False, stop=True),
                             reads=[mbd_sl, cx.csl["c_onesbf"]],
                             writes=cx.psl(bk),
                             desc=f"{pr} bias{c}.{hh}.{w}.{k2}")
                    emit_av()
                    nbk = (len(live) + 1) // 2
                    for bk in range(nbk):
                        wd = 512 if (len(live) - 2 * bk) >= 2 else 256
                        P.do("scalar", lambda e, bk=bk, wd=wd, par=par:
                             e.activation(et[:, bk, 0:wd],
                                          cx.ps(bk + 2 * par, 0, wd),
                                          AF.Exp, scale=8.0),
                            reads=cx.psl(bk + 2 * par), writes=[et_sl],
                            desc=f"{pr} exp{c}.{hh}.{w}.{bk}")
                    av_pend[0] = (c, hh, wp, wi, live, par, h)
    emit_av()
    flush_tail()

    for qq in range(2):
        P.do("gpsimd", lambda e, l=l, qq=qq: e.dma_start(
            out=Wo[:, 4 * qq:4 * qq + 4, :],
            in_=cx.dram["L1f_Wo"][l].rearrange(
                "(c p) f -> p c f", p=128)[:, 4 * qq:4 * qq + 4, :]),
            reads=(), writes=[Wo_sl, wrk_sl, W_sl, mbd_sl, sqr_sl], dma=True,
            qkey=f"l1f_Wo{qq}", desc=f"{pr} ldWo.{qq}")
    for f in range(NCH):
        pb = 0 if f % 2 == 0 else 4
        P.do("sync", lambda e, f=f: e.dma_start(
            out=yst[:, :], in_=ybuf[b, f * 128:(f + 1) * 128, :]),
            reads=[y_sl[f]], writes=[yst_sl], dma=True, qkey="l1f_yst",
            desc=f"{pr} ldy{f}")
        for tt in range(4):
            for dd in range(NCH):
                P.do("tensor", lambda e, f=f, tt=tt, dd=dd, pb=pb: e.matmul(
                    cx.ps(pb + tt), Wo[:, dd, f * 128:(f + 1) * 128],
                    xsb[:, dd, tt * 512: tt * 512 + 512],
                    start=(dd == 0), stop=(dd == NCH - 1)),
                    reads=[Wo_sl, wv_sl[dd]], writes=cx.psl(pb + tt),
                    desc=f"{pr} womm{f}.{tt}.{dd}")
        P.do("vector", lambda e, f=f, pb=pb: e.tensor_add(
            yst[:, :], yst[:, :], cx.ps(pb, 0, 2048)),
            reads=cx.psl(pb, pb + 1, pb + 2, pb + 3) + [yst_sl],
            writes=[yst_sl], desc=f"{pr} yadd{f}")
        P.do("sync", lambda e, f=f: e.dma_start(
            out=ybuf[b, f * 128:(f + 1) * 128, :], in_=yst[:, :]),
            reads=[yst_sl], writes=[y_sl[f]], dma=True, qkey=f"l1f_y{b}_{f}",
            desc=f"{pr} sty{f}")
    P.barrier()


def _l1f_ffn_alloc(cx):
    cx.phase_begin()
    t = {}
    t["rows"] = cx.palloc("rows", [1, 4 * 512], F32)
    t["W1"] = cx.palloc("f_W1", [128, 8, 5632], BF16).ap()
    t["W2"] = cx.palloc("f_W2", [128, 22, 1024], BF16).ap()
    t["c1"] = cx.palloc("f_c1", [1, 5632], BF16).ap()
    t["g"] = cx.palloc("f_g", [128, 22, 512], BF16).ap()
    t["xs"] = cx.palloc("f_xs", [128, NCH, 512], F32).ap()
    t["xn"] = cx.palloc("f_xn", [128, NCH, 512], BF16).ap()
    t["gel"] = cx.palloc("f_gel", [128, 512], F32, at="rows").ap()
    t["yst"] = cx.palloc("f_yst", [128, 1, 512], F32).ap()
    t["eb"] = cx.palloc("f_eb", [128, 44], F32).ap()
    t["cs"] = cx.palloc("f_cs", [128, 1], BF16).ap()
    t["mr"] = cx.palloc("f_mr", [1, 512], BF16).ap()
    return t


def emit_l1f_ffn_weights(cx, t, l):
    P = cx.P
    sl = P.slot
    pr = f"l1fw{l}"
    W1, W2, c1, eb, cs = t["W1"], t["W2"], t["c1"], t["eb"], t["cs"]
    W1_sl, W2_sl, c1_sl = sl("f_W1"), sl("f_W2"), sl("f_c1")
    eb_sl, cs_sl = sl("f_eb"), sl("f_cs")
    gcols = cx.t["l1f_lng"].ap()
    lnbb = cx.t["l1f_lnbb"].ap()
    bcols = cx.t["l1f_b1c"].ap()
    b2cols = cx.t["l1f_b2c"].ap()
    for qq in range(4):
        P.do("gpsimd", lambda e, l=l, qq=qq: e.dma_start(
            out=W1[:, 2 * qq:2 * qq + 2, :],
            in_=cx.dram["L1f_W1p"][l].rearrange(
                "(c p) f -> p c f", p=128)[:, 2 * qq:2 * qq + 2, :]),
            reads=(), writes=[W1_sl], dma=True, qkey=f"l1f_W1{qq}",
            desc=f"{pr} ldW1.{qq}")
    for qq in range(4):
        P.do("gpsimd", lambda e, l=l, qq=qq: e.dma_start(
            out=W2[:, qq::4, :],
            in_=cx.dram["L1f_W2p"][l].rearrange(
                "(c p) f -> p c f", p=128)[:, qq::4, :]),
            reads=(), writes=[W2_sl], dma=True, qkey=f"l1f_W2{qq}",
            desc=f"{pr} ldW2.{qq}")
    for j in range(44):
        for dd in range(NCH):
            P.do("tensor", lambda e, j=j, dd=dd, l=l: e.matmul(
                cx.ps(0, 0, 1, (0, 128)), W1[:, dd, j * 128:(j + 1) * 128],
                lnbb[:, l, dd:dd + 1],
                start=(dd == 0), stop=(dd == NCH - 1)),
                reads=[W1_sl, cx.csl["l1f_lnbb"]], writes=cx.psl(0),
                desc=f"{pr} c2mm{j}.{dd}")
        P.do("vector", lambda e, j=j, l=l: e.scalar_tensor_tensor(
            eb[:, j:j + 1], cx.ps(0, 0, 1, (0, 128)), 1.0,
            bcols[:, l, j:j + 1], ALU.mult, ALU.add),
            reads=cx.psl(0) + [cx.csl["l1f_b1c"]], writes=[eb_sl],
            desc=f"{pr} eb{j}")
    for dd in range(NCH):
        P.do("vector", lambda e, dd=dd, l=l: e.tensor_scalar_mul(
            W1[:, dd, :], W1[:, dd, :], gcols[:, l, dd:dd + 1]),
            reads=[W1_sl, cx.csl["l1f_lng"]], writes=[W1_sl],
            desc=f"{pr} gfold{dd}")
    for j in range(44):
        for dd in range(NCH):
            P.do("tensor", lambda e, j=j, dd=dd: e.matmul(
                cx.ps(1, 0, 1, (0, 128)), W1[:, dd, j * 128:(j + 1) * 128],
                cx.t["c_ones_col"].ap(), start=(dd == 0), stop=(dd == NCH - 1)),
                reads=[W1_sl, cx.csl["c_ones_col"]], writes=cx.psl(1),
                desc=f"{pr} c1mm{j}.{dd}")
        P.do("vector", lambda e: e.tensor_copy(cs[:, 0:1], cx.ps(1, 0, 1, (0, 128))),
             reads=cx.psl(1), writes=[cs_sl], desc=f"{pr} c1c{j}")
        P.do("tensor", lambda e: e.matmul(
            cx.ps(2, 0, 128, (0, 1)), cs[:, 0:1], cx.t["c_eye"].ap(),
            start=True, stop=True),
            reads=[cs_sl, cx.csl["c_eye"]], writes=cx.psl(2), desc=f"{pr} c1t{j}")
        P.do("vector", lambda e, j=j: e.tensor_copy(
            c1[0:1, j * 128:(j + 1) * 128], cx.ps(2, 0, 128, (0, 1))),
            reads=cx.psl(2), writes=[c1_sl], desc=f"{pr} c1r{j}")


def emit_l1f_ffn(cx, t, l, b, ybuf):
    nc, P = cx.nc, cx.P
    sl = P.slot
    pr = f"l1ff{l}_{b}"
    W1, W2, c1, g, xs, xn = t["W1"], t["W2"], t["c1"], t["g"], t["xs"], t["xn"]
    gel, yst, eb, mr = t["gel"], t["yst"], t["eb"], t["mr"]
    W1_sl, W2_sl, c1_sl = sl("f_W1"), sl("f_W2"), sl("f_c1")
    eb_sl, mr_sl = sl("f_eb"), sl("f_mr")
    g_sl = [sl(f"{pr}_g{j}") for j in range(22)]
    xs_sl = [sl(f"{pr}_xs{c}") for c in range(NCH)]
    xn_sl = [sl(f"{pr}_xn{c}") for c in range(NCH)]
    gel_sl, yst_sl = sl("f_gel"), sl("f_yst")
    y_sl = [sl(f"yb{b}_{c}") for c in range(NCH)]
    b2cols = cx.t["l1f_b2c"].ap()
    R_MEAN, R_MSQ, R_RSTD, R_TMP = 0, 1, 2, 3

    for tq in range(4):
        t0 = tq * 512
        for c in range(NCH):
            if c < 4:
                P.do("gpsimd", lambda e, c=c, t0=t0: e.dma_start(
                    out=xs[:, c, :], in_=ybuf[b, c * 128:(c + 1) * 128, t0:t0 + 512]),
                    reads=[y_sl[c]], writes=[xs_sl[c]], dma=True, qkey=f"l1f_fxs{c}",
                    desc=f"{pr} sh{tq}.{c}")
            elif t0 > 0:
                P.do("gpsimd", lambda e, c=c, t0=t0: e.dma_start(
                    out=xs[:, c, :],
                    in_=ybuf[b, c * 128:(c + 1) * 128, t0 - 1:t0 + 511]),
                    reads=[y_sl[c]], writes=[xs_sl[c]], dma=True, qkey=f"l1f_fxs{c}",
                    desc=f"{pr} sh{tq}.{c}")
            else:
                P.do("gpsimd", lambda e, c=c: e.dma_start(
                    out=xs[:, c, 1:512], in_=ybuf[b, c * 128:(c + 1) * 128, 0:511]),
                    reads=[y_sl[c]], writes=[xs_sl[c]], dma=True, qkey=f"l1f_fxs{c}",
                    desc=f"{pr} sh{tq}.{c}")
                P.do("vector", lambda e, c=c: e.memset(xs[:, c, 0:1], 0.0),
                     reads=(), writes=[xs_sl[c]], desc=f"{pr} shz{c}")
        for c in range(NCH):
            P.do("scalar", lambda e, c=c: e.activation(xn[:, c, :], xs[:, c, :],
                                                       AF.Square),
                 reads=[xs_sl[c]], writes=[xn_sl[c]], desc=f"{pr} sq{tq}.{c}")
            P.do("tensor", lambda e, c=c: e.matmul(
                cx.ps(6, 0, 512, (0, 1)), cx.t["c_ones_row"].ap()[:, 0:1],
                xs[:, c, :], start=(c == 0), stop=(c == NCH - 1)),
                reads=[xs_sl[c], cx.csl["c_ones_row"]], writes=cx.psl(6),
                desc=f"{pr} sum{tq}.{c}")
            P.do("tensor", lambda e, c=c: e.matmul(
                cx.ps(7, 0, 512, (0, 1)), cx.t["c_ones_col"].ap(), xn[:, c, :],
                start=(c == 0), stop=(c == NCH - 1)),
                reads=[xn_sl[c], cx.csl["c_ones_col"]], writes=cx.psl(7),
                desc=f"{pr} ssq{tq}.{c}")
        _rm, _rq, _rr, _rt = (cx.row5(R_MEAN), cx.row5(R_MSQ),
                              cx.row5(R_RSTD), cx.row5(R_TMP))
        P.do("vector", lambda e, _rm=_rm: e.tensor_scalar_mul(
            _rm, cx.ps(6, 0, 512, (0, 1)), 1.0 / D),
            reads=cx.psl(6), writes=[cx.rowsl(R_MEAN)], desc=f"{pr} mean{tq}")
        P.do("vector", lambda e, _rq=_rq: e.tensor_scalar_mul(
            _rq, cx.ps(7, 0, 512, (0, 1)), 1.0 / D),
            reads=cx.psl(7), writes=[cx.rowsl(R_MSQ)], desc=f"{pr} msq{tq}")
        P.do("vector", lambda e: e.tensor_mul(_rt, _rm, _rm),
             reads=[cx.rowsl(R_MEAN)], writes=[cx.rowsl(R_TMP)], desc=f"{pr} m2{tq}")
        P.do("vector", lambda e: e.tensor_sub(_rt, _rq, _rt),
             reads=[cx.rowsl(R_MSQ), cx.rowsl(R_TMP)], writes=[cx.rowsl(R_TMP)],
             desc=f"{pr} var{tq}")
        P.do("scalar", lambda e: e.activation(_rt, _rt, AF.Ln, bias=cx.fcol(EPS)),
             reads=[cx.rowsl(R_TMP), cx.csl["c_fconsts"]],
             writes=[cx.rowsl(R_TMP)], desc=f"{pr} ln{tq}")
        P.do("scalar", lambda e: e.activation(_rr, _rt, AF.Exp, scale=-0.5),
             reads=[cx.rowsl(R_TMP)], writes=[cx.rowsl(R_RSTD)], desc=f"{pr} rs{tq}")
        P.do("vector", lambda e: e.tensor_mul(_rt, _rm, _rr),
             reads=[cx.rowsl(R_MEAN), cx.rowsl(R_RSTD)], writes=[cx.rowsl(R_TMP)],
             desc=f"{pr} mr1{tq}")
        P.do("vector", lambda e: e.tensor_scalar_mul(mr[0:1, :], _rt, -1.0),
             reads=[cx.rowsl(R_TMP)], writes=[mr_sl], desc=f"{pr} mr2{tq}")
        P.do("tensor", lambda e, _rr=_rr: e.matmul(
            cx.ps(6), cx.t["c_ones_row"].ap()[0:1, 0:128], _rr,
            start=True, stop=True),
            reads=[cx.rowsl(R_RSTD), cx.csl["c_ones_row"]], writes=cx.psl(6),
            desc=f"{pr} rbc{tq}")
        for c in range(NCH):
            P.do("vector", lambda e, c=c: e.tensor_mul(xn[:, c, :], xs[:, c, :],
                                                       cx.ps(6)),
                 reads=[xs_sl[c]] + cx.psl(6), writes=[xn_sl[c]],
                 desc=f"{pr} xn{tq}.{c}")
        for jj in range(22):
            jb = (jj % 2) * 2
            for part, pb in ((0, 0), (1, 1)):
                pb += jb
                j = jj + 22 * part
                for dd in range(NCH):
                    P.do("tensor", lambda e, j=j, dd=dd, pb=pb: e.matmul(
                        cx.ps(pb), W1[:, dd, j * 128:(j + 1) * 128], xn[:, dd, :],
                        start=(dd == 0), stop=False),
                        reads=[W1_sl, xn_sl[dd]], writes=cx.psl(pb),
                        desc=f"{pr} w1mm{tq}.{j}.{dd}")
                P.do("tensor", lambda e, j=j, pb=pb: e.matmul(
                    cx.ps(pb), c1[0:1, j * 128:(j + 1) * 128], mr[0:1, :],
                    start=False, stop=True),
                    reads=[c1_sl, mr_sl], writes=cx.psl(pb),
                    desc=f"{pr} corr{tq}.{j}")
            P.do("scalar", lambda e, jj=jj, jb=jb: e.activation(
                gel[:, :], cx.ps(1 + jb), AF.Gelu, bias=eb[:, 22 + jj:23 + jj]),
                reads=cx.psl(1 + jb) + [eb_sl],
                writes=[gel_sl, cx.rowsl(0)],
                desc=f"{pr} gelu{tq}.{jj}")
            P.do("vector", lambda e, jj=jj, jb=jb: e.scalar_tensor_tensor(
                g[:, jj, :], cx.ps(0 + jb), eb[:, jj:jj + 1],
                gel[:, :], ALU.add, ALU.mult),
                reads=cx.psl(0 + jb) + [gel_sl, eb_sl], writes=[g_sl[jj]],
                desc=f"{pr} gmul{tq}.{jj}")
        for do_ in range(NCH):
            pb = 4 if do_ % 2 == 0 else 5
            P.do("sync", lambda e, do_=do_, t0=t0: e.dma_start(
                out=yst[:, 0, :],
                in_=ybuf[b, do_ * 128:(do_ + 1) * 128, t0:t0 + 512]),
                reads=[y_sl[do_]], writes=[yst_sl], dma=True, qkey="l1f_yst",
                desc=f"{pr} ldy{tq}.{do_}")
            for jj in range(22):
                P.do("tensor", lambda e, do_=do_, jj=jj, pb=pb: e.matmul(
                    cx.ps(pb), W2[:, jj, do_ * 128:(do_ + 1) * 128], g[:, jj, :],
                    start=(jj == 0), stop=(jj == 21)),
                    reads=[W2_sl, g_sl[jj]], writes=cx.psl(pb),
                    desc=f"{pr} w2mm{tq}.{do_}.{jj}")
            P.do("vector", lambda e, do_=do_, pb=pb, l=l: e.scalar_tensor_tensor(
                yst[:, 0, :], cx.ps(pb), b2cols[:, l, do_:do_ + 1],
                yst[:, 0, :], ALU.add, ALU.add),
                reads=cx.psl(pb) + [yst_sl, cx.csl["l1f_b2c"]], writes=[yst_sl],
                desc=f"{pr} yev{tq}.{do_}")
            P.do("sync", lambda e, do_=do_, t0=t0: e.dma_start(
                out=ybuf[b, do_ * 128:(do_ + 1) * 128, t0:t0 + 512],
                in_=yst[:, 0, :]),
                reads=[yst_sl], writes=[y_sl[do_]], dma=True,
                qkey=f"l1f_y{b}_{do_}", desc=f"{pr} sty{tq}.{do_}")
    P.barrier()



def l1f_dram(nc, cx):
    dr = {}
    dr["L1f_Wqkv"] = nc.dram_tensor("L1f_Wqkv", [6, D, 3072], F32,
                                    kind="ExternalInput").ap()
    dr["L1f_Wo"] = nc.dram_tensor("L1f_Wo", [6, D, D], F32,
                                  kind="ExternalInput").ap()
    dr["L1f_W1p"] = nc.dram_tensor("L1f_W1p", [6, D, 5632], F32,
                                   kind="ExternalInput").ap()
    dr["L1f_W2p"] = nc.dram_tensor("L1f_W2p", [6, 2816, D], F32,
                                   kind="ExternalInput").ap()
    dr["L1f_rot"] = nc.dram_tensor("L1f_rot", [4, 128, T], F32,
                                   kind="ExternalInput").ap()
    cx.dram.update(dr)
    for nm, sh, dt in [("l1f_qks", [128, 24], F32), ("l1f_lng", [128, 6, 8], F32),
                       ("l1f_lnbb", [128, 6, 8], BF16),
                       ("l1f_b1c", [128, 6, 44], F32),
                       ("l1f_b2c", [128, 6, 8], F32)]:
        t = nc.dram_tensor(nm + "_hbm", [sh[0], int(np.prod(sh[1:]))], F32,
                           kind="ExternalInput")
        cx.dram[nm + "_hbm"] = t.ap()
        cx.galloc(nm, sh, dt)
        cx.csl[nm] = cx.P.slot("const_" + nm)
        cx.P.qfinal["kparam"] = True
        q = "gpsimd" if dt == BF16 else "sync"
        qk = "kparamg" if dt == BF16 else "kparam"
        cx.P.qfinal[qk] = True
        dst = cx.t[nm].ap()
        if len(sh) == 3:
            dst = dst.rearrange("p a b -> p (a b)")
        cx.P.do(q, lambda e, nm=nm, dst=dst: e.dma_start(
            out=dst, in_=cx.dram[nm + "_hbm"]),
            reads=(), writes=[cx.csl[nm]], dma=True, qkey=qk, desc=f"ld {nm}")


def l1f_host_inputs(inputs):
    qks = np.zeros((128, 24), np.float32)
    for l in range(6):
        qs = np.asarray(inputs["L1_ffn_q_scale"][l], np.float32)
        ks = np.asarray(inputs["L1_ffn_k_scale"][l], np.float32)
        qsw = np.concatenate([qs[32:], qs[:32]])
        ksw = np.concatenate([ks[32:], ks[:32]])
        qks[:, 4 * l] = np.concatenate([qs, qs])
        qks[:, 4 * l + 1] = np.concatenate([qsw, qsw])
        qks[:, 4 * l + 2] = np.concatenate([ks, ks])
        qks[:, 4 * l + 3] = np.concatenate([ksw, ksw])
    lng = np.zeros((128, 48), np.float32)
    lnb = np.zeros((128, 48), np.float32)
    b1c = np.zeros((128, 6 * 44), np.float32)
    b2c = np.zeros((128, 48), np.float32)
    for l in range(6):
        lng[:, l * 8:(l + 1) * 8] = col128(inputs["L1_ffn_ln_g"][l])
        lnb[:, l * 8:(l + 1) * 8] = col128(inputs["L1_ffn_ln_b"][l])
        b1 = np.asarray(inputs["L1_ffn_b1"][l], np.float32)
        b1c[:, l * 44:l * 44 + 22] = col128(b1[0:2730], 22)
        b1c[:, l * 44 + 22:l * 44 + 44] = col128(b1[2730:5460], 22)
        b2c[:, l * 8:(l + 1) * 8] = col128(inputs["L1_ffn_b2"][l])
    W1 = np.asarray(inputs["L1_ffn_W1"], np.float32)
    W1p = np.zeros((6, D, 5632), np.float32)
    W1p[:, :, 0:2730] = W1[:, :, 0:2730]
    W1p[:, :, 2816:2816 + 2730] = W1[:, :, 2730:5460]
    W2 = np.asarray(inputs["L1_ffn_W2"], np.float32)
    W2p = np.zeros((6, 2816, D), np.float32)
    W2p[:, 0:2730, :] = W2
    rot = np.stack(host_l1f_rotary(), axis=0)
    return {
        "L1f_Wqkv": _f32(inputs["L1_ffn_Wqkv"]),
        "L1f_Wo": _f32(inputs["L1_ffn_Wo"]),
        "L1f_W1p": _f32(W1p),
        "L1f_W2p": _f32(W2p),
        "L1f_rot": _f32(rot),
        "l1f_qks_hbm": qks,
        "l1f_lng_hbm": lng,
        "l1f_lnbb_hbm": lnb,
        "l1f_b1c_hbm": b1c,
        "l1f_b2c_hbm": b2c,
    }


def emit_l1f_all(cx, xin, xout, layers=6):
    ybuf = cx.dram["ybuf"]
    for b in range(NB):
        emit_l1f_entry(cx, b, xin, ybuf)
    ta = _l1f_attn_alloc(cx)
    tf = _l1f_ffn_alloc(cx)
    for l in range(layers):
        if not L1F_SKIP_ATTN:
            for b in range(NB):
                emit_l1f_attn(cx, ta, l, b, ybuf)
        if not L1F_SKIP_FFN:
            cx.t["rows"] = tf["rows"]
            emit_l1f_ffn_weights(cx, tf, l)
            for b in range(NB):
                emit_l1f_ffn(cx, tf, l, b, ybuf)
    for b in range(NB):
        emit_l1f_exit(cx, b, xin, ybuf, xout)


def build_program_l1f():
    nc = bass.Bass("TRN2", target_bir_lowering=False)
    P = Prog()
    cx = build_common(nc, P)
    cx.dram["x_in"] = nc.dram_tensor("x_in", [NB, D, T], F32,
                                     kind="ExternalInput").ap()
    cx.dram["x_out"] = nc.dram_tensor("x_out", [NB, D, T], F32,
                                      kind="ExternalOutput").ap()
    cx.dram["ybuf"] = nc.dram_tensor("ybuf", [NB, D, T], F32,
                                     kind="Internal").ap()
    for nm, (sh, _dt) in CONST_SHAPES.items():
        cx.dram[nm] = nc.dram_tensor(nm, sh, F32, kind="ExternalInput").ap()
    emit_load_consts(cx)
    l1f_dram(nc, cx)
    emit_l1f_all(cx, cx.dram["x_in"], cx.dram["x_out"], layers=L1F_LAYERS)
    P.do("gpsimd", lambda e: e.nop(),
         reads=[P.slot(f"l1fx{b}_xo{c}") for b in range(NB) for c in range(NCH)],
         desc="final wait")
    finish_program(nc, P)
    return nc, P


def finish_program(nc, P):
    sems = {}
    stack = contextlib.ExitStack()
    for e in ENGINES:
        sems["c_" + e] = stack.enter_context(nc.semaphore("c_" + e))
    for k in P.qcum:
        sems["q_" + k] = stack.enter_context(nc.semaphore("q_" + k))
    with nc.Block() as block:
        P.replay(nc, block, sems)
    stack.close()
    return nc


def l0f_dram(nc, cx, standalone=True):
    dr = {}
    dr["L0_ffn_w1"] = nc.dram_tensor("L0_ffn_w1", [8, D, 4096], F32,
                                     kind="ExternalInput").ap()
    dr["L0_ffn_w2"] = nc.dram_tensor("L0_ffn_w2", [8, 4096, D], F32,
                                     kind="ExternalInput").ap()
    cx.dram.update(dr)
    for nm, sh in [("l0f_b1", [128, 32]), ("l0f_b2", [128, 8]),
                   ("l0f_gamma", [128, 8]), ("l0f_beta", [128, 8])]:
        t = nc.dram_tensor(nm + "_hbm", sh, F32, kind="ExternalInput")
        cx.dram[nm + "_hbm"] = t.ap()
        cx.galloc(nm, sh, F32)
        cx.csl[nm] = cx.P.slot("const_" + nm)
        cx.P.qfinal["kparam"] = True
        cx.P.do("sync", lambda e, nm=nm: e.dma_start(
            out=cx.t[nm].ap(), in_=cx.dram[nm + "_hbm"]),
            reads=(), writes=[cx.csl[nm]], dma=True, qkey="kparam", desc=f"ld {nm}")


def l0f_host_inputs(inputs):
    m = {
        "L0_ffn_w1": _f32(inputs["L0_ffn_w1"]),
        "L0_ffn_w2": _f32(inputs["L0_ffn_w2"]),
        "l0f_b1_hbm": col128(inputs["L0_ffn_b1"][0]),
        "l0f_b2_hbm": col128(inputs["L0_ffn_b2"][0]),
        "l0f_gamma_hbm": col128(inputs["L0_ffn_gamma"]),
        "l0f_beta_hbm": col128(inputs["L0_ffn_beta"]),
    }
    m.update(host_constants())
    return m


def build_program_l0f():
    nc = bass.Bass("TRN2", target_bir_lowering=False)
    P = Prog()
    cx = build_common(nc, P)
    cx.dram["x_in"] = nc.dram_tensor("x_in", [NB, D, T], F32,
                                     kind="ExternalInput").ap()
    cx.dram["x_out"] = nc.dram_tensor("x_out", [NB, D, T], F32,
                                      kind="ExternalOutput").ap()
    cx.dram["tgtf_hbm"] = nc.dram_tensor("tgtf_hbm", [NB, D, T], F32,
                                         kind="Internal").ap()
    for nm, (sh, _dt) in CONST_SHAPES.items():
        cx.dram[nm] = nc.dram_tensor(nm, sh, F32, kind="ExternalInput").ap()
    emit_load_consts(cx)
    l0f_dram(nc, cx)
    for b in range(NB):
        emit_l0f(cx, b, cx.dram["x_in"], cx.dram["x_out"])
    P.do("gpsimd", lambda e: e.nop(),
         reads=[P.slot(f"l0f{b}_xo{c}") for b in range(NB) for c in range(NCH)],
         desc="final wait")
    finish_program(nc, P)
    return nc, P



def build_program_full():
    nc = bass.Bass("TRN2", target_bir_lowering=False)
    P = Prog()
    cx = build_common(nc, P)
    cx.dram["x_in"] = nc.dram_tensor("x_in", [NB, D, T], F32,
                                     kind="ExternalInput").ap()
    cx.dram["x_out"] = nc.dram_tensor("x_out", [NB, D, T], F32,
                                      kind="ExternalOutput").ap()
    xbA = nc.dram_tensor("xbA", [NB, D, T], F32, kind="Internal").ap()
    xbB = nc.dram_tensor("xbB", [NB, D, T], F32, kind="Internal").ap()
    cx.dram["ybuf"] = nc.dram_tensor("ybuf", [NB, D, T], F32,
                                     kind="Internal").ap()
    cx.dram["tgtf_hbm"] = nc.dram_tensor("tgtf_hbm", [NB, D, T], F32,
                                         kind="Internal").ap()
    for nm, (sh, _dt) in CONST_SHAPES.items():
        cx.dram[nm] = nc.dram_tensor(nm, sh, F32, kind="ExternalInput").ap()
    emit_load_consts(cx)
    l0a_dram(nc, cx)
    l0f_dram(nc, cx)
    l1a_dram(nc, cx)
    l1f_dram(nc, cx)
    for b in range(NB):
        emit_l0a(cx, b, cx.dram["x_in"], xbA)
    for b in range(NB):
        emit_l0f(cx, b, xbA, xbB)
    for b in range(NB):
        emit_l1a(cx, b, xbB, xbA)
    emit_l1f_all(cx, xbA, cx.dram["x_out"], layers=L1F_LAYERS)
    P.do("gpsimd", lambda e: e.nop(),
         reads=[P.slot(f"l1fx{b}_xo{c}") for b in range(NB) for c in range(NCH)],
         desc="final wait")
    finish_program(nc, P)
    return nc, P


def host_all_inputs(inputs):
    m = {}
    m.update(l0a_host_inputs(inputs))
    m.update(l0f_host_inputs(inputs))
    m.update(l1a_host_inputs(inputs))
    m.update(l1f_host_inputs(inputs))
    m.update(host_constants())
    return m


def kernel(**inputs):
    x = np.asarray(inputs["L0_attn_x"], np.float32)
    nc_, _P = build_program_full()
    hbase = host_all_inputs(inputs)
    cur = x_to_fm(x)
    in_maps = [dict(hbase, x_in=cur[c]) for c in range(8)]
    res = run_bass_kernel_spmd(nc_, in_maps, core_ids=list(range(8)))
    return fm_to_x([np.asarray(res.results[c]["x_out"], np.float32)
                    for c in range(8)])


# version 18
# speedup vs baseline: 1.0052x; 1.0052x over previous
import contextlib
import numpy as np
import concourse.bass as bass
import concourse.mybir as mybir
from concourse.bass_utils import run_bass_kernel_spmd

F32 = mybir.dt.float32
BF16 = mybir.dt.bfloat16
AF = mybir.ActivationFunctionType
ALU = mybir.AluOpType
AX = mybir.AxisListType
AP = bass.AP

D = 1024
T = 2048
NB = 2
NCH = 8
EPS = 1e-5
BIGNEG = -3.0e38

L1F_LAYERS = 6
L1F_SKIP_FFN = False
L1F_SKIP_ATTN = False
L1F_SKIP_CORE = False

ENGINES = ("sync", "tensor", "scalar", "vector", "gpsimd")
DMA_ENGINES = ("sync", "scalar", "gpsimd")


class Slot:
    __slots__ = ("name", "writer", "readers")

    def __init__(self, name):
        self.name = name
        self.writer = None
        self.readers = []

    def __repr__(self):
        return f"<{self.name}>"


class _Op:
    __slots__ = ("fn", "deps", "is_dma", "desc", "qkey")

    def __init__(self, fn, deps, is_dma, desc):
        self.fn = fn
        self.deps = deps
        self.is_dma = is_dma
        self.desc = desc


class Prog:
    self_sync = True

    def __init__(self):
        self.ops = {e: [] for e in ENGINES}
        self.marks = {e: set() for e in ENGINES}
        self.qcum = {}
        self.qfinal = {}
        self._slots = {}
        self._barrier = {}

    def slot(self, name):
        s = self._slots.get(name)
        if s is None:
            s = self._slots[name] = Slot(name)
        return s

    def barrier(self):
        for e in ENGINES:
            last = getattr(self, "_last_compute", {}).get(e)
            if last is not None:
                self._barrier[("c", e)] = last
        for k, v in self.qcum.items():
            self._barrier[("q", k)] = v

    def do(self, eng, fn, reads=(), writes=(), dma=False, qkey=None, desc=""):
        assert eng in ENGINES, eng
        assert not dma or (eng in DMA_ENGINES and qkey is not None)
        idx = len(self.ops[eng])
        deps = {}

        def add_dep(kind, e2, v):
            if kind == "c" and e2 == eng and not self.self_sync:
                return
            key = (kind, e2)
            if deps.get(key, -1) < v:
                deps[key] = v

        for (kind, e2), v in self._barrier.items():
            add_dep(kind, e2, v)
        for s in reads:
            if s.writer:
                add_dep(*s.writer)
        for s in writes:
            if s.writer:
                add_dep(*s.writer)
            for r in s.readers:
                add_dep(*r)
        op = _Op(fn, deps, dma, desc)
        op.qkey = qkey
        self.ops[eng].append(op)
        if not hasattr(self, "_last_compute"):
            self._last_compute = {}
        if not dma:
            self._last_compute[eng] = idx
        if dma:
            self.qcum[qkey] = self.qcum.get(qkey, 0) + 1
            tag = ("q", qkey, self.qcum[qkey])
        else:
            tag = ("c", eng, idx)
        for key, v in deps.items():
            if key[0] == "c":
                self.marks[key[1]].add(v)
        for s in reads:
            s.readers.append(tag)
        for s in writes:
            s.writer = tag
            s.readers = []
        return idx

    def replay(self, nc, block, sems):
        markcum = {}
        for e in ENGINES:
            marked = sorted(self.marks[e])
            cum, c, j = [], 0, 0
            for i in range(len(self.ops[e])):
                if j < len(marked) and marked[j] == i:
                    c += 1
                    j += 1
                cum.append(c)
            markcum[e] = cum
        csem = {e: sems["c_" + e] for e in ENGINES}
        qsem = {k: sems["q_" + k] for k in self.qcum}

        def make_body(e):
            ops, marks = self.ops[e], self.marks[e]

            def body(eng):
                ensured = {}
                for i, op in enumerate(ops):
                    for key, v in sorted(op.deps.items(), key=str):
                        if key[0] == "c":
                            sem, val = csem[key[1]], markcum[key[1]][v]
                        else:
                            k = key[1]
                            sem = qsem[k]
                            val = 16 * (self.qcum[k] if self.qfinal.get(k) else v)
                        if ensured.get(sem.name, -1) < val:
                            eng.wait_ge(sem, val)
                            ensured[sem.name] = val
                    ins = op.fn(eng)
                    if op.is_dma:
                        ins.then_inc(qsem[op.qkey], 16)
                    elif i in marks:
                        ins.then_inc(csem[e], 1)
            return body

        for e in ENGINES:
            if self.ops[e]:
                getattr(block, e)(make_body(e))

    def nops(self):
        return {e: len(self.ops[e]) for e in ENGINES}


def _f32(x):
    return np.ascontiguousarray(x, dtype=np.float32)


def col128(v, pad_chunks=None):
    v = np.asarray(v, np.float32).ravel()
    n = v.shape[0]
    nch = (n + 127) // 128 if pad_chunks is None else pad_chunks
    out = np.zeros((nch * 128,), np.float32)
    out[:n] = v
    return _f32(out.reshape(nch, 128).T)


def host_constants():
    c = {}
    c["c_eye"] = _f32(np.eye(128))
    psw = np.zeros((128, 128), np.float32)
    for p in range(128):
        psw[p, p ^ 1] = 1.0
    c["c_pswap_l0"] = psw
    prh = np.zeros((128, 128), np.float32)
    for blk in range(2):
        b0 = blk * 64
        for i in range(32):
            prh[b0 + i, b0 + 32 + i] = 1.0
            prh[b0 + 32 + i, b0 + i] = 1.0
    c["c_pswap_l1"] = prh
    sel2 = np.zeros((128, 2), np.float32)
    sel2[0:64, 0] = 1.0
    sel2[64:128, 1] = 1.0
    c["c_sel2"] = sel2
    sel2T = np.zeros((2, 128), np.float32)
    sel2T[0, 0:64] = 1.0
    sel2T[1, 64:128] = 1.0
    c["c_sel2T"] = sel2T
    mcur = np.where(np.arange(128)[:, None] > np.arange(128)[None, :], BIGNEG, 0.0)
    c["c_mask_curT"] = _f32(mcur.T)
    mprev = np.where(np.arange(128)[:, None] < np.arange(128)[None, :], BIGNEG, 0.0)
    c["c_mask_prevT"] = _f32(mprev.T)
    c["c_ones_col"] = _f32(np.ones((128, 1)))
    c["c_ones_row"] = _f32(np.ones((128, 128)))
    c["c_ones512"] = _f32(np.ones((1, 512)))
    c["c_onesbf"] = _f32(np.ones((1, 128)))
    c["c_bigrow"] = _f32(np.full((1, 128), BIGNEG))
    e64 = np.zeros((128, 64), np.float32)
    e64[64, :] = 1.0
    c["c_e64"] = e64
    c["c_sel2T32"] = c["c_sel2T"]
    c["c_sel232"] = c["c_sel2"]
    fc = np.zeros((128, 16), np.float32)
    for i, v in enumerate(FCONST_VALS):
        fc[:, i] = v
    c["c_fconsts"] = fc
    return c


FCONST_VALS = [1e-5, 1.0, 2048.0, 10.0, -10.0, 0.125, 220.0, 1e-30]

CONST_SHAPES = {
    "c_fconsts": ([128, 16], "f32"),
    "c_eye": ([128, 128], "bf"), "c_pswap_l0": ([128, 128], "bf"),
    "c_pswap_l1": ([128, 128], "bf"), "c_sel2": ([128, 2], "bf"),
    "c_sel2T": ([2, 128], "bf"), "c_mask_curT": ([128, 128], "bf"),
    "c_mask_prevT": ([128, 128], "bf"), "c_ones_col": ([128, 1], "bf"),
    "c_ones_row": ([128, 128], "f32"),
    "c_ones512": ([1, 512], "bf"),
    "c_onesbf": ([1, 128], "bf"),
    "c_bigrow": ([1, 128], "bf"),
    "c_e64": ([128, 64], "f32"),
    "c_sel2T32": ([2, 128], "f32"),
    "c_sel232": ([128, 2], "f32"),
}


def host_l0_rope_consts():
    import jax
    import jax.numpy as jnp
    cpu = jax.devices("cpu")[0]
    with jax.default_device(cpu):
        return _host_l0_rope_consts_impl(jnp)


def _host_l0_rope_consts_impl(jnp):
    top = 2595.0 * jnp.log10(jnp.asarray(1.0 + 4000.0 / 200.0, dtype=jnp.float32))
    mel = jnp.power(10.0, jnp.linspace(0.0, top, 32, dtype=jnp.float32) / 2595.0) - 1.0
    fb = np.asarray(200.0 * mel / 1000.0, dtype=np.float32)
    col = np.zeros((128, 1), np.float32)
    sign = np.zeros((128, 1), np.float32)
    for p in range(128):
        col[p, 0] = fb[(p % 64) // 2]
        sign[p, 0] = -1.0 if (p % 2) == 0 else 1.0
    return col, sign


def host_l1f_rotary():
    import jax
    import jax.numpy as jnp
    cpu = jax.devices("cpu")[0]
    with jax.default_device(cpu):
        return _host_l1f_rotary_impl(jnp)


def _host_l1f_rotary_impl(jnp):
    DIM_HEAD, WINDOW, n = 64, 256, T
    half = jnp.arange(0, DIM_HEAD, 2, dtype=jnp.float32)
    inv_freq = 1.0 / 10000.0 ** (half / DIM_HEAD)
    t = jnp.arange(n, dtype=jnp.float32)
    freqs = t[:, None] * inv_freq[None, :]
    freqs = jnp.concatenate([freqs, freqs], -1)
    base_scale = (half + 0.4 * DIM_HEAD) / (1.4 * DIM_HEAD)
    power = (t - n // 2) / (WINDOW // 2)
    scale = base_scale[None, :] ** power[:, None]
    scale = jnp.concatenate([scale, scale], -1)
    cos, sin = jnp.cos(freqs), jnp.sin(freqs)
    inv = 1.0 / scale
    sgn = np.ones((64, 1), np.float32)
    sgn[0:32] = -1.0
    tabs = []
    for tbl in (np.asarray(cos * scale).T, np.asarray(sin * scale).T * sgn,
                np.asarray(cos * inv).T, np.asarray(sin * inv).T * sgn):
        tabs.append(_f32(np.concatenate([tbl, tbl], axis=0)))
    return tabs


def _cw_consts():
    c_rem = 2 * np.pi
    cs = []
    for _ in range(6):
        ci = np.float32(c_rem)
        m, e = np.frexp(ci)
        ci = np.float32(np.ldexp(np.round(float(m) * 64) / 64.0, int(e)))
        cs.append(float(ci))
        c_rem -= float(ci)
    return cs


CW = _cw_consts()
INV_2PI = float(np.float32(1.0 / (2 * np.pi)))
MAGIC = 12582912.0


def x_to_fm(x):
    out = []
    for core in range(8):
        xb = x[core * NB:(core + 1) * NB]
        out.append(_f32(np.transpose(xb, (0, 2, 1))))
    return out


def fm_to_x(parts):
    return _f32(np.concatenate([np.transpose(np.asarray(p, np.float32), (0, 2, 1))
                                for p in parts], axis=0))


class Ctx:
    SBUF_BASE = 16512
    SBUF_LIMIT = 212832

    def __init__(self, nc, P):
        self.nc = nc
        self.P = P
        self.dram = {}
        self.t = {}
        self._top = 0
        self._ptop = 0
        self._pofs = {}
        self._uid = 0

    def _alloc(self, name, shape, dtype, off):
        self._uid += 1
        h = self.nc.alloc_sbuf_tensor_at(f"{name}__{self._uid}", list(shape), dtype,
                                         offset=off + self.SBUF_BASE)
        self.t[name] = h
        return h

    @staticmethod
    def _sz(shape, dtype):
        sz = int(np.prod(shape[1:])) * (4 if dtype == F32 else 2)
        return (sz + 31) & ~31

    def galloc(self, name, shape, dtype):
        h = self._alloc(name, shape, dtype, self._top)
        self._top += self._sz(shape, dtype)
        assert self._top <= self.SBUF_LIMIT, (name, self._top)
        return h

    def phase_begin(self):
        self._ptop = (self._top + 255) & ~255
        self._pofs = {}

    def palloc(self, name, shape, dtype, at=None, off2=0):
        if at is not None:
            off = self._pofs[at] + off2
            assert off2 + self._sz(shape, dtype) <= self._pofs_sz[at], (name, at)
        else:
            off = self._ptop
            self._ptop += self._sz(shape, dtype)
            assert self._ptop <= self.SBUF_LIMIT, (name, self._ptop)
        if not hasattr(self, "_pofs_sz"):
            self._pofs_sz = {}
        self._pofs[name] = off
        self._pofs_sz[name] = self._sz(shape, dtype)
        return self._alloc(name, shape, dtype, off)

    def ps(self, bank, off=0, width=512, parts=(0, 128)):
        return self.PS[parts[0]:parts[1], bank * 512 + off: bank * 512 + off + width]

    def psl(self, *banks):
        return [self.P.slot(f"psum_b{b}") for b in banks]

    def row(self, i):
        return self.t["rows"].ap()[0:1, i * T:(i + 1) * T]

    def row5(self, i):
        return self.t["rows"].ap()[0:1, i * 512:(i + 1) * 512]

    def rowsl(self, i):
        return self.P.slot(f"row{i}")


def build_common(nc, P):
    cx = Ctx(nc, P)
    cx.PSt = nc.alloc_psum_tensor("PS", [128, 4096], F32)
    cx.PS = cx.PSt.ap()
    for nm, (shape, dt) in CONST_SHAPES.items():
        cx.galloc(nm, shape, BF16 if dt == "bf" else F32)
    cx.csl = {nm: P.slot("const_" + nm) for nm in CONST_SHAPES}
    cx.sl_const = P.slot("const")

    def fcol(v, p0=0, n=1):
        i = FCONST_VALS.index(v)
        return cx.t["c_fconsts"].ap()[p0:p0 + n, i:i + 1]
    cx.fcol = fcol
    return cx


def emit_load_consts(cx):
    P = cx.P
    P.qfinal["kconst"] = True
    for nm in CONST_SHAPES:
        P.do("gpsimd", lambda e, nm=nm: e.dma_start(
            out=cx.t[nm].ap(), in_=cx.dram[nm]),
            reads=(), writes=[cx.csl[nm]], dma=True, qkey="kconst", desc=f"ld {nm}")


def emit_stats_rows(cx, pr, src_t, src_slots, sq_t, sq_slots, i_mean, i_msq,
                    nch=NCH, sq_engine="scalar"):
    P = cx.P
    for c in range(nch):
        P.do(sq_engine, lambda e, c=c: e.activation(
            sq_t[:, c % sq_t.shape[1], :] if False else sq_t[:, c % sq_t.shape[1], :],
            src_t[:, c, :], AF.Square),
            reads=[src_slots[c]], writes=[sq_slots[c % len(sq_slots)]],
            desc=f"{pr} sq{c}")
        for ti in range(4):
            P.do("tensor", lambda e, c=c, ti=ti: e.matmul(
                cx.ps(ti, 0, 512, (0, 1)), cx.t["c_ones_col"].ap(),
                src_t[:, c, ti * 512: ti * 512 + 512],
                start=(c == 0), stop=(c == nch - 1)),
                reads=[src_slots[c], cx.csl["c_ones_col"]], writes=cx.psl(ti),
                desc=f"{pr} sum{c}.{ti}")
            P.do("tensor", lambda e, c=c, ti=ti: e.matmul(
                cx.ps(4 + ti, 0, 512, (0, 1)), cx.t["c_ones_col"].ap(),
                sq_t[:, c % sq_t.shape[1], ti * 512: ti * 512 + 512],
                start=(c == 0), stop=(c == nch - 1)),
                reads=[sq_slots[c % len(sq_slots)], cx.csl["c_ones_col"]],
                writes=cx.psl(4 + ti), desc=f"{pr} ssq{c}.{ti}")
    r_mean, r_msq = cx.row(i_mean), cx.row(i_msq)
    for ti in range(4):
        P.do("vector", lambda e, ti=ti, r_mean=r_mean: e.tensor_scalar_mul(
            r_mean[0:1, ti * 512: ti * 512 + 512],
            cx.ps(ti, 0, 512, (0, 1)), 1.0 / D),
            reads=cx.psl(ti), writes=[cx.rowsl(i_mean)], desc=f"{pr} mean{ti}")
        P.do("vector", lambda e, ti=ti, r_msq=r_msq: e.tensor_scalar_mul(
            r_msq[0:1, ti * 512: ti * 512 + 512],
            cx.ps(4 + ti, 0, 512, (0, 1)), 1.0 / D),
            reads=cx.psl(4 + ti), writes=[cx.rowsl(i_msq)], desc=f"{pr} msq{ti}")


def emit_rstd_row(cx, pr, i_mean, i_msq, i_rstd, i_tmp, eps=EPS):
    P = cx.P
    rt, rm, rq, rr = cx.row(i_tmp), cx.row(i_mean), cx.row(i_msq), cx.row(i_rstd)
    P.do("vector", lambda e: e.tensor_mul(rt, rm, rm),
         reads=[cx.rowsl(i_mean)], writes=[cx.rowsl(i_tmp)], desc=f"{pr} m2")
    P.do("vector", lambda e: e.tensor_sub(rt, rq, rt),
         reads=[cx.rowsl(i_msq), cx.rowsl(i_tmp)], writes=[cx.rowsl(i_tmp)],
         desc=f"{pr} var")
    P.do("scalar", lambda e: e.activation(rt, rt, AF.Ln, bias=cx.fcol(eps, 0)),
         reads=[cx.rowsl(i_tmp), cx.csl["c_fconsts"]], writes=[cx.rowsl(i_tmp)],
         desc=f"{pr} ln")
    P.do("scalar", lambda e: e.activation(rr, rt, AF.Exp, scale=-0.5),
         reads=[cx.rowsl(i_tmp)], writes=[cx.rowsl(i_rstd)], desc=f"{pr} exp")


def emit_bcast(cx, pr, i_row, banks):
    P = cx.P
    ones = cx.t["c_ones_row"].ap()[0:1, 0:128]
    rowap = cx.row(i_row)
    out = []
    for ti, bk in enumerate(banks):
        pap = cx.ps(bk)
        P.do("tensor", lambda e, pap=pap, ti=ti, ones=ones, rowap=rowap: e.matmul(
            pap, ones, rowap[0:1, ti * 512: ti * 512 + 512],
            start=True, stop=True),
            reads=[cx.rowsl(i_row), cx.csl["c_ones_row"]], writes=cx.psl(bk),
            desc=f"{pr} bc{ti}")
        out.append(pap)
    return out


def emit_l0f(cx, b, xin, xout):
    nc, P = cx.nc, cx.P
    sl = P.slot
    pr = f"l0f{b}"
    R_MEAN, R_MSQ, R_RSTD, R_TMP = 0, 1, 2, 3
    R_MEANU, R_MSQU, R_RSTDU = 0, 1, 2

    cx.phase_begin()
    cx.palloc("rows", [1, 4 * T], F32)
    tgtb = cx.palloc("tgtb", [128, NCH, T], BF16).ap()
    h1 = cx.palloc("h1", [128, 32, 1024], BF16).ap()
    ubf = cx.palloc("ubf", [128, NCH, T], BF16).ap()
    xst = cx.palloc("xst", [128, 1, T], F32).ap()
    fsc = cx.palloc("fsc", [128, 2, 1024], F32).ap()
    xsq = cx.palloc("xsq", [128, 1, T], BF16).ap()
    w1s = cx.palloc("w1s", [128, 2, 8, 128], BF16).ap()
    w2s = cx.palloc("w2s", [128, 2, 32, 128], BF16).ap()
    tfh = cx.dram["tgtf_hbm"]

    tb_sl = [sl(f"{pr}_tb{c}") for c in range(NCH)]
    sq_sl = [sl(f"{pr}_sq0")]
    u_sl = [sl(f"{pr}_u{c}") for c in range(NCH)]
    h1_sl = [sl(f"{pr}_h1_{j}") for j in range(32)]
    _xs = sl(f"{pr}_xst0")
    xst_sl = [_xs, _xs]
    fsc_sl = [sl(f"{pr}_fsc0"), sl(f"{pr}_fsc1")]
    tfh_sl = [sl(f"{pr}_tfh{c}") for c in range(NCH)]

    for c in range(NCH):
        db = c % 2
        P.do("sync", lambda e, c=c, db=db: e.dma_start(
            out=xst[:, 0, :], in_=xin[b, c * 128:(c + 1) * 128, :]),
            reads=(), writes=[xst_sl[db]], dma=True, qkey="l0f_xst", desc=f"{pr} ldx{c}")
        P.do("vector", lambda e, c=c, db=db: e.tensor_copy(tgtb[:, c, :], xst[:, 0, :]),
             reads=[xst_sl[db]], writes=[tb_sl[c]], desc=f"{pr} cast{c}")
    emit_stats_rows(cx, pr, tgtb, tb_sl, xsq, sq_sl, R_MEAN, R_MSQ)
    emit_rstd_row(cx, pr, R_MEAN, R_MSQ, R_RSTD, R_TMP)
    mb = emit_bcast(cx, pr + "m", R_MEAN, [0, 1, 2, 3])
    rb = emit_bcast(cx, pr + "r", R_RSTD, [4, 5, 6, 7])

    for c in range(NCH):
        db = c % 2
        P.do("scalar", lambda e, c=c, db=db: e.dma_start(
            out=xst[:, 0, :], in_=xin[b, c * 128:(c + 1) * 128, :]),
            reads=(), writes=[xst_sl[db]], dma=True, qkey="l0f_xst", desc=f"{pr} ldx2_{c}")
        for ti in range(4):
            P.do("vector", lambda e, c=c, db=db, ti=ti: e.tensor_sub(
                xst[:, 0, ti * 512: ti * 512 + 512],
                xst[:, 0, ti * 512: ti * 512 + 512], mb[ti]),
                reads=[xst_sl[db]] + cx.psl(ti), writes=[xst_sl[db]],
                desc=f"{pr} sub{c}.{ti}")
            P.do("vector", lambda e, c=c, db=db, ti=ti: e.tensor_mul(
                xst[:, 0, ti * 512: ti * 512 + 512],
                xst[:, 0, ti * 512: ti * 512 + 512], rb[ti]),
                reads=[xst_sl[db]] + cx.psl(4 + ti), writes=[xst_sl[db]],
                desc=f"{pr} mul{c}.{ti}")
        P.do("vector", lambda e, c=c, db=db: e.tensor_copy(tgtb[:, c, :], xst[:, 0, :]),
             reads=[xst_sl[db]], writes=[tb_sl[c]], desc=f"{pr} tcast{c}")
        P.do("sync", lambda e, c=c, db=db: e.dma_start(
            out=tfh[b, c * 128:(c + 1) * 128, :], in_=xst[:, 0, :]),
            reads=[xst_sl[db]], writes=[tfh_sl[c]], dma=True, qkey=f"l0f_tfh{c}",
            desc=f"{pr} sttf{c}")

    W1 = cx.dram["L0_ffn_w1"]
    W2 = cx.dram["L0_ffn_w2"]
    b1c = cx.t["l0f_b1"].ap()
    b2c = cx.t["l0f_b2"].ap()
    w1s_sl = [sl(f"{pr}_w1s0"), sl(f"{pr}_w1s1")]
    w2s_sl = [sl(f"{pr}_w2s0"), sl(f"{pr}_w2s1")]
    for th in range(2):
        t0 = th * 1024
        for j in range(32):
            db = j % 2
            pb = 0 if db == 0 else 2
            P.do("gpsimd", lambda e, j=j, db=db: e.dma_start(
                out=w1s[:, db, :, :],
                in_=W1[0, :, j * 128:(j + 1) * 128].rearrange("(c p) f -> p c f", p=128)),
                reads=(), writes=[w1s_sl[db]], dma=True, qkey=f"l0f_w1s{db}",
                desc=f"{pr} ldw1 {th}.{j}")
            for tt in range(2):
                for dd in range(NCH):
                    P.do("tensor", lambda e, j=j, db=db, tt=tt, dd=dd, pb=pb, t0=t0:
                         e.matmul(cx.ps(pb + tt), w1s[:, db, dd, :],
                                  tgtb[:, dd, t0 + tt * 512: t0 + tt * 512 + 512],
                                  start=(dd == 0), stop=(dd == NCH - 1)),
                         reads=[w1s_sl[db], tb_sl[dd]], writes=cx.psl(pb + tt),
                         desc=f"{pr} w1mm {th}.{j}.{tt}.{dd}")
            if db == 0:
                P.do("scalar", lambda e, j=j, pb=pb: e.activation(
                    h1[:, j, :], cx.ps(pb, 0, 1024), AF.Relu, bias=b1c[:, j:j + 1]),
                    reads=cx.psl(pb, pb + 1) + [cx.csl["l0f_b1"]],
                    writes=[h1_sl[j]], desc=f"{pr} relu {th}.{j}")
            else:
                P.do("vector", lambda e, j=j, pb=pb: e.tensor_scalar(
                    h1[:, j, :], cx.ps(pb, 0, 1024), b1c[:, j:j + 1], 0.0,
                    ALU.add, ALU.max),
                    reads=cx.psl(pb, pb + 1) + [cx.csl["l0f_b1"]],
                    writes=[h1_sl[j]], desc=f"{pr} reluv {th}.{j}")
        for do_ in range(NCH):
            db = do_ % 2
            pb = 4 if db == 0 else 6
            P.do("gpsimd", lambda e, do_=do_, db=db: e.dma_start(
                out=w2s[:, db, :, :],
                in_=W2[0, :, do_ * 128:(do_ + 1) * 128].rearrange("(c p) f -> p c f", p=128)),
                reads=(), writes=[w2s_sl[db]], dma=True, qkey=f"l0f_w2s{db}",
                desc=f"{pr} ldw2 {th}.{do_}")
            P.do("scalar", lambda e, do_=do_, db=db, t0=t0: e.dma_start(
                out=fsc[:, db, :], in_=tfh[b, do_ * 128:(do_ + 1) * 128, t0:t0 + 1024]),
                reads=[tfh_sl[do_]], writes=[fsc_sl[db]], dma=True,
                qkey=f"l0f_fsc{db}", desc=f"{pr} ldtf {th}.{do_}")
            for j in range(32):
                for tt in range(2):
                    P.do("tensor", lambda e, do_=do_, db=db, j=j, tt=tt, pb=pb:
                         e.matmul(cx.ps(pb + tt), w2s[:, db, j, :],
                                  h1[:, j, tt * 512: tt * 512 + 512],
                                  start=(j == 0), stop=(j == 31)),
                         reads=[w2s_sl[db], h1_sl[j]], writes=cx.psl(pb + tt),
                         desc=f"{pr} w2mm {th}.{do_}.{j}.{tt}")
            P.do("vector", lambda e, do_=do_, db=db, pb=pb, t0=t0: e.scalar_tensor_tensor(
                ubf[:, do_, t0: t0 + 1024], cx.ps(pb, 0, 1024), b2c[:, do_:do_ + 1],
                fsc[:, db, :], ALU.add, ALU.add),
                reads=cx.psl(pb, pb + 1) + [fsc_sl[db], cx.csl["l0f_b2"]],
                writes=[u_sl[do_]], desc=f"{pr} uev {th}.{do_}")

    emit_stats_rows(cx, pr + "u", ubf, u_sl, xsq, sq_sl, R_MEANU, R_MSQU)
    emit_rstd_row(cx, pr + "u", R_MEANU, R_MSQU, R_RSTDU, R_TMP)
    mbU = emit_bcast(cx, pr + "mu", R_MEANU, [0, 1, 2, 3])
    rbU = emit_bcast(cx, pr + "ru", R_RSTDU, [4, 5, 6, 7])
    gam = cx.t["l0f_gamma"].ap()
    bet = cx.t["l0f_beta"].ap()
    for c in range(NCH):
        db = c % 2
        P.do("sync", lambda e, c=c, db=db: e.dma_start(
            out=xst[:, 0, :], in_=xin[b, c * 128:(c + 1) * 128, :]),
            reads=(), writes=[xst_sl[db]], dma=True, qkey="l0f_xst", desc=f"{pr} ldx3_{c}")
        for hf in range(2):
            t0 = hf * 1024
            for ti in range(2):
                bi = hf * 2 + ti
                P.do("vector", lambda e, c=c, db=db, t0=t0, ti=ti, bi=bi: e.tensor_sub(
                    fsc[:, db, ti * 512: ti * 512 + 512],
                    ubf[:, c, t0 + ti * 512: t0 + ti * 512 + 512], mbU[bi]),
                    reads=[u_sl[c]] + cx.psl(bi), writes=[fsc_sl[db]],
                    desc=f"{pr} ap1 {c}.{bi}")
                P.do("vector", lambda e, c=c, db=db, ti=ti, bi=bi: e.tensor_mul(
                    fsc[:, db, ti * 512: ti * 512 + 512],
                    fsc[:, db, ti * 512: ti * 512 + 512], rbU[bi]),
                    reads=[fsc_sl[db]] + cx.psl(4 + bi), writes=[fsc_sl[db]],
                    desc=f"{pr} ap2 {c}.{bi}")
            P.do("vector", lambda e, c=c, db=db: e.tensor_scalar(
                fsc[:, db, :], fsc[:, db, :], gam[:, c:c + 1], bet[:, c:c + 1],
                ALU.mult, ALU.add),
                reads=[fsc_sl[db], cx.csl["l0f_gamma"], cx.csl["l0f_beta"]],
                writes=[fsc_sl[db]], desc=f"{pr} ap3 {c}.{hf}")
            P.do("vector", lambda e, c=c, db=db, t0=t0: e.tensor_add(
                xst[:, 0, t0:t0 + 1024], xst[:, 0, t0:t0 + 1024], fsc[:, db, :]),
                reads=[fsc_sl[db], xst_sl[db]], writes=[xst_sl[db]],
                desc=f"{pr} ap4 {c}.{hf}")
        P.do("sync", lambda e, c=c, db=db: e.dma_start(
            out=xout[b, c * 128:(c + 1) * 128, :], in_=xst[:, 0, :]),
            reads=[xst_sl[db]], writes=[sl(f"{pr}_xo{c}")], dma=True,
            qkey="l0f_xo", desc=f"{pr} stx{c}")
    P.barrier()



def emit_ln_spill(cx, pr, b, xin, hbf, hbf_sl, xst, xst_sl, xsq, sq_sl,
                  spill=None, spill_sl=None, spill_qk=None, xst_qk=None):
    P = cx.P
    R_MEAN, R_MSQ, R_RSTD, R_TMP = 0, 1, 2, 3
    for c in range(NCH):
        P.do("sync", lambda e, c=c: e.dma_start(
            out=xst[:, 0, :], in_=xin[b, c * 128:(c + 1) * 128, :]),
            reads=(), writes=[xst_sl[0]], dma=True, qkey=xst_qk, desc=f"{pr} ldx{c}")
        P.do("vector", lambda e, c=c: e.tensor_copy(hbf[:, c, :], xst[:, 0, :]),
             reads=[xst_sl[0]], writes=[hbf_sl[c]], desc=f"{pr} cast{c}")
    emit_stats_rows(cx, pr, hbf, hbf_sl, xsq, sq_sl, R_MEAN, R_MSQ)
    emit_rstd_row(cx, pr, R_MEAN, R_MSQ, R_RSTD, R_TMP)
    mb = emit_bcast(cx, pr + "m", R_MEAN, [0, 1, 2, 3])
    rb = emit_bcast(cx, pr + "r", R_RSTD, [4, 5, 6, 7])
    for c in range(NCH):
        P.do("scalar", lambda e, c=c: e.dma_start(
            out=xst[:, 0, :], in_=xin[b, c * 128:(c + 1) * 128, :]),
            reads=(), writes=[xst_sl[0]], dma=True, qkey=xst_qk, desc=f"{pr} ldx2_{c}")
        for ti in range(4):
            P.do("vector", lambda e, c=c, ti=ti: e.tensor_sub(
                xst[:, 0, ti * 512: ti * 512 + 512],
                xst[:, 0, ti * 512: ti * 512 + 512], mb[ti]),
                reads=[xst_sl[0]] + cx.psl(ti), writes=[xst_sl[0]],
                desc=f"{pr} sub{c}.{ti}")
            P.do("vector", lambda e, c=c, ti=ti: e.tensor_mul(
                xst[:, 0, ti * 512: ti * 512 + 512],
                xst[:, 0, ti * 512: ti * 512 + 512], rb[ti]),
                reads=[xst_sl[0]] + cx.psl(4 + ti), writes=[xst_sl[0]],
                desc=f"{pr} mul{c}.{ti}")
        P.do("vector", lambda e, c=c: e.tensor_copy(hbf[:, c, :], xst[:, 0, :]),
             reads=[xst_sl[0]], writes=[hbf_sl[c]], desc=f"{pr} hcast{c}")
        if spill is not None:
            P.do("sync", lambda e, c=c: e.dma_start(
                out=spill[b, c * 128:(c + 1) * 128, :], in_=xst[:, 0, :]),
                reads=[xst_sl[0]], writes=[spill_sl[c]], dma=True,
                qkey=spill_qk + str(c), desc=f"{pr} spill{c}")


def emit_l1a(cx, b, xin, xout):
    nc, P = cx.nc, cx.P
    sl = P.slot
    pr = f"l1a{b}"
    E, S = 2048, 16
    NE = 16

    cx.phase_begin()
    cx.palloc("rows", [1, 4 * T], F32)
    hbf = cx.palloc("hbf", [128, NCH, T], BF16).ap()
    xc = cx.palloc("xc", [128, NE, T], BF16).ap()
    xm = cx.palloc("xm", [128, T], F32).ap()
    cv = cx.palloc("cv", [128, T], F32).ap()
    spb = cx.palloc("spb", [128, 512], BF16).ap()
    dens = cx.palloc("dens", [16, 3, T], F32).ap()
    xst = cx.palloc("xst", [128, 1, T], F32).ap()
    xsq = cx.palloc("xsq", [128, 1, T], BF16, at="xc").ap()
    hfst = cx.palloc("hfst", [128, T], F32, at="xm").ap()
    iws = cx.palloc("iws", [128, 2, 8, 128], BF16).ap()
    dtw = cx.palloc("dtw", [128, 2, 16, 128], BF16).ap()
    bcd = cx.palloc("bcd", [128, 16, 48], BF16).ap()
    ows = cx.palloc("ows", [128, 2, 16, 128], BF16).ap()

    hbf_sl = [sl(f"{pr}_h{c}") for c in range(NCH)]
    xc_sl = [sl(f"{pr}_xc{c}") for c in range(NE)]
    sq_sl = [sl(f"{pr}_sq0")]
    xst_sl = [sl(f"{pr}_xst")]
    hf_sl = [sl(f"{pr}_hf{c}") for c in range(NCH)]
    xm_sl, cv_sl = sl(f"{pr}_xm"), sl(f"{pr}_cv")
    spb_sl = sl(f"{pr}_spb")
    hfst_sl = sl(f"{pr}_hfst")
    den_sl = [sl(f"{pr}_den{i}") for i in range(3)]
    iws_sl = [sl(f"{pr}_iw0"), sl(f"{pr}_iw1")]
    dtw_sl = [sl(f"{pr}_dtw0"), sl(f"{pr}_dtw1")]
    bcd_sl = sl(f"{pr}_bcd")
    ows_sl = [sl(f"{pr}_ow0"), sl(f"{pr}_ow1")]

    hfh = cx.dram["tgtf_hbm"]
    IW = cx.dram["L1_in_wT"]
    DTW = cx.dram["L1_dt_wT"]
    BCD = cx.dram["L1_bcd_wT"]
    OW = cx.dram["L1_out_wT"]
    pc = cx.t["l1a_pcols"].ap()
    bcdb = cx.t["l1a_bcdb"].ap()
    outb = cx.t["l1a_outb"].ap()

    emit_ln_spill(cx, pr, b, xin, hbf, hbf_sl, xst, xst_sl, xsq, sq_sl,
                  spill=hfh, spill_sl=hf_sl, spill_qk="l1a_hf", xst_qk="l1a_xst")

    for f in range(NE):
        db = f % 2
        pb = 0 if db == 0 else 4
        P.do("gpsimd", lambda e, f=f, db=db: e.dma_start(
            out=iws[:, db, :, :],
            in_=IW[:, f * 128:(f + 1) * 128].rearrange("(c p) o -> p c o", p=128)),
            reads=(), writes=[iws_sl[db]], dma=True, qkey=f"l1a_iw{db}",
            desc=f"{pr} ldiw{f}")
        for tt in range(4):
            for dd in range(NCH):
                P.do("tensor", lambda e, f=f, db=db, tt=tt, dd=dd, pb=pb: e.matmul(
                    cx.ps(pb + tt), iws[:, db, dd, :],
                    hbf[:, dd, tt * 512: tt * 512 + 512],
                    start=(dd == 0), stop=(dd == NCH - 1)),
                    reads=[iws_sl[db], hbf_sl[dd]], writes=cx.psl(pb + tt),
                    desc=f"{pr} iwmm{f}.{tt}.{dd}")
        P.do("vector", lambda e, f=f, pb=pb: e.tensor_scalar_add(
            xm[:, :], cx.ps(pb, 0, 2048), pc[:, f:f + 1]),
            reads=cx.psl(pb, pb + 1, pb + 2, pb + 3) + [cx.csl["l1a_pcols"]],
            writes=[xm_sl], desc=f"{pr} xmev{f}")
        P.do("vector", lambda e: e.memset(cv[:, 0:1], 0.0),
             reads=(), writes=[cv_sl], desc=f"{pr} cvz{f}")
        P.do("vector", lambda e, f=f: e.tensor_scalar_mul(
            cv[:, 1:T], xm[:, 0:T - 1], pc[:, 32 + f:32 + f + 1]),
            reads=[xm_sl, cx.csl["l1a_pcols"]], writes=[cv_sl], desc=f"{pr} cv0{f}")
        P.do("vector", lambda e, f=f: e.scalar_tensor_tensor(
            cv[:, :], xm[:, :], pc[:, 48 + f:48 + f + 1], cv[:, :],
            ALU.mult, ALU.add),
            reads=[xm_sl, cv_sl, cx.csl["l1a_pcols"]], writes=[cv_sl],
            desc=f"{pr} cv1{f}")
        P.do("vector", lambda e, f=f: e.scalar_tensor_tensor(
            cv[:, 0:T - 1], xm[:, 1:T], pc[:, 64 + f:64 + f + 1], cv[:, 0:T - 1],
            ALU.mult, ALU.add),
            reads=[xm_sl, cv_sl, cx.csl["l1a_pcols"]], writes=[cv_sl],
            desc=f"{pr} cv2{f}")
        P.do("scalar", lambda e, f=f: e.activation(
            xc[:, f, :], cv[:, :], AF.Silu, bias=pc[:, 80 + f:80 + f + 1]),
            reads=[cv_sl, cx.csl["l1a_pcols"]], writes=[xc_sl[f]],
            desc=f"{pr} silu{f}")

    mx = cv[:, 0:512]
    az = cv[:, 512:1024]
    for eo in range(NE):
        db = eo % 2
        P.do("gpsimd", lambda e, eo=eo, db=db: e.dma_start(
            out=dtw[:, db, :, :],
            in_=DTW[:, eo * 128:(eo + 1) * 128].rearrange("(c p) o -> p c o", p=128)),
            reads=(), writes=[dtw_sl[db]], dma=True, qkey=f"l1a_dtw{db}",
            desc=f"{pr} lddtw{eo}")
        for tt in range(4):
            pb = (tt % 2) + 2 * db
            for ee in range(NE):
                P.do("tensor", lambda e, eo=eo, db=db, tt=tt, ee=ee, pb=pb: e.matmul(
                    cx.ps(pb), dtw[:, db, ee, :],
                    xc[:, ee, tt * 512: tt * 512 + 512],
                    start=(ee == 0), stop=(ee == NE - 1)),
                    reads=[dtw_sl[db], xc_sl[ee]], writes=cx.psl(pb),
                    desc=f"{pr} dtmm{eo}.{tt}.{ee}")
            P.do("vector", lambda e, eo=eo, pb=pb: e.tensor_scalar_add(
                az[:, :], cx.ps(pb), pc[:, 96 + eo:96 + eo + 1]),
                reads=cx.psl(pb) + [cx.csl["l1a_pcols"]], writes=[cv_sl],
                desc=f"{pr} az{eo}.{tt}")
            P.do("vector", lambda e: e.tensor_scalar_max(mx[:, :], az[:, :], 0.0),
                 reads=[cv_sl], writes=[cv_sl], desc=f"{pr} mx{eo}.{tt}")
            P.do("vector", lambda e: e.scalar_tensor_tensor(
                az[:, :], mx[:, :], -2.0, az[:, :], ALU.mult, ALU.add),
                reads=[cv_sl], writes=[cv_sl], desc=f"{pr} zm{eo}.{tt}")
            P.do("scalar", lambda e: e.activation(az[:, :], az[:, :], AF.Exp),
                 reads=[cv_sl], writes=[cv_sl], desc=f"{pr} spe{eo}.{tt}")
            P.do("scalar", lambda e: e.activation(az[:, :], az[:, :], AF.Ln,
                                                  bias=cx.fcol(1.0, 0, 128)),
                 reads=[cv_sl, cx.csl["c_fconsts"]], writes=[cv_sl],
                 desc=f"{pr} spl{eo}.{tt}")
            P.do("vector", lambda e: e.tensor_add(spb[:, :], mx[:, :], az[:, :]),
                 reads=[cv_sl], writes=[spb_sl], desc=f"{pr} sp{eo}.{tt}")
            P.do("tensor", lambda e, eo=eo, tt=tt: e.matmul(
                cx.ps(4 + tt, 0, 512, (0, 1)), cx.t["c_ones_col"].ap(), spb[:, :],
                start=(eo == 0), stop=(eo == NE - 1)),
                reads=[spb_sl, cx.csl["c_ones_col"]], writes=cx.psl(4 + tt),
                desc=f"{pr} spsum{eo}.{tt}")
    R_DTS, R_DEC, R_Y = 0, 1, 2
    for tt in range(4):
        _rd = cx.row(R_DTS)
        P.do("vector", lambda e, tt=tt, _rd=_rd: e.tensor_scalar(
            _rd[0:1, tt * 512: tt * 512 + 512],
            cx.ps(4 + tt, 0, 512, (0, 1)), -1.0 / E, 10.0, ALU.mult, ALU.min),
            reads=cx.psl(4 + tt), writes=[cx.rowsl(R_DTS)], desc=f"{pr} dts{tt}")
    _rd, _re = cx.row(R_DTS), cx.row(R_DEC)
    P.do("vector", lambda e, _rd=_rd: e.tensor_scalar_max(_rd, _rd, -10.0),
         reads=[cx.rowsl(R_DTS)], writes=[cx.rowsl(R_DTS)], desc=f"{pr} clip")
    P.do("scalar", lambda e, _rd=_rd, _re=_re: e.activation(_re, _rd, AF.Exp),
         reads=[cx.rowsl(R_DTS)], writes=[cx.rowsl(R_DEC)], desc=f"{pr} dec")

    P.do("gpsimd", lambda e: e.dma_start(
        out=bcd[:, :, :], in_=BCD.rearrange("(c p) s -> p c s", p=128)),
        reads=(), writes=[bcd_sl], dma=True, qkey="l1a_bcd", desc=f"{pr} ldbcd")
    for k in range(3):
        for tt in range(4):
            for ee in range(NE):
                P.do("tensor", lambda e, k=k, tt=tt, ee=ee: e.matmul(
                    cx.ps(k, 0, 512, (0, 16)), bcd[:, ee, k * 16:(k + 1) * 16],
                    xc[:, ee, tt * 512: tt * 512 + 512],
                    start=(ee == 0), stop=(ee == NE - 1)),
                    reads=[bcd_sl, xc_sl[ee]], writes=cx.psl(k),
                    desc=f"{pr} bcd{k}.{tt}.{ee}")
            P.do("vector", lambda e, k=k, tt=tt: e.tensor_scalar_add(
                dens[0:16, min(k, 2), tt * 512: tt * 512 + 512],
                cx.ps(k, 0, 512, (0, 16)), bcdb[0:16, k:k + 1]),
                reads=cx.psl(k) + [cx.csl["l1a_bcdb"]], writes=[den_sl[min(k, 2)]],
                desc=f"{pr} bcdev{k}.{tt}")
    P.do("vector", lambda e: e.tensor_mul(dens[0:16, 0, :], dens[0:16, 0, :],
                                          dens[0:16, 2, :]),
         reads=[den_sl[0], den_sl[2]], writes=[den_sl[0]], desc=f"{pr} u")
    for tt in range(4):
        _re = cx.row(R_DEC)
        P.do("tensor", lambda e, tt=tt, _re=_re: e.matmul(
            cx.ps(3, 0, 512, (0, 16)), cx.t["c_ones_row"].ap()[0:1, 0:16],
            _re[0:1, tt * 512: tt * 512 + 512], start=True, stop=True),
            reads=[cx.rowsl(R_DEC), cx.csl["c_ones_row"]], writes=cx.psl(3),
            desc=f"{pr} decbc{tt}")
        P.do("vector", lambda e, tt=tt: e.tensor_copy(
            dens[0:16, 2, tt * 512: tt * 512 + 512], cx.ps(3, 0, 512, (0, 16))),
            reads=cx.psl(3), writes=[den_sl[2]], desc=f"{pr} decev{tt}")
    P.do("vector", lambda e: e.tensor_tensor_scan(
        dens[0:16, 0, :], dens[0:16, 2, :], dens[0:16, 0, :], 0.0,
        ALU.mult, ALU.add),
        reads=[den_sl[0], den_sl[2]], writes=[den_sl[0]], desc=f"{pr} scan")
    P.do("vector", lambda e: e.tensor_mul(dens[0:16, 1, :], dens[0:16, 1, :],
                                          dens[0:16, 0, :]),
         reads=[den_sl[0], den_sl[1]], writes=[den_sl[1]], desc=f"{pr} cs")
    for tt in range(4):
        P.do("tensor", lambda e, tt=tt: e.matmul(
            cx.ps(4, 0, 512, (0, 1)), cx.t["c_ones_row"].ap()[0:16, 0:1],
            dens[0:16, 1, tt * 512: tt * 512 + 512], start=True, stop=True),
            reads=[den_sl[1], cx.csl["c_ones_row"]], writes=cx.psl(4),
            desc=f"{pr} ymm{tt}")
        _ry = cx.row(R_Y)
        P.do("vector", lambda e, tt=tt, _ry=_ry: e.tensor_copy(
            _ry[0:1, tt * 512: tt * 512 + 512], cx.ps(4, 0, 512, (0, 1))),
            reads=cx.psl(4), writes=[cx.rowsl(R_Y)], desc=f"{pr} yev{tt}")

    ybc = emit_bcast(cx, pr + "y", R_Y, [4, 5, 6, 7])
    for f in range(NE):
        db = f % 2
        P.do("gpsimd", lambda e, f=f, db=db: e.dma_start(
            out=iws[:, db, :, :],
            in_=IW[:, 2048 + f * 128: 2048 + (f + 1) * 128].rearrange(
                "(c p) o -> p c o", p=128)),
            reads=(), writes=[iws_sl[db]], dma=True, qkey=f"l1a_iw{db}",
            desc=f"{pr} ldgw{f}")
        for tt in range(4):
            pb = (tt % 2) + 2 * db
            for dd in range(NCH):
                P.do("tensor", lambda e, f=f, db=db, tt=tt, dd=dd, pb=pb: e.matmul(
                    cx.ps(pb), iws[:, db, dd, :],
                    hbf[:, dd, tt * 512: tt * 512 + 512],
                    start=(dd == 0), stop=(dd == NCH - 1)),
                    reads=[iws_sl[db], hbf_sl[dd]], writes=cx.psl(pb),
                    desc=f"{pr} gwmm{f}.{tt}.{dd}")
            P.do("scalar", lambda e, f=f, tt=tt, pb=pb: e.activation(
                xm[:, tt * 512: tt * 512 + 512], cx.ps(pb), AF.Silu,
                bias=pc[:, 16 + f:16 + f + 1]),
                reads=cx.psl(pb) + [cx.csl["l1a_pcols"]], writes=[xm_sl],
                desc=f"{pr} gsilu{f}.{tt}")
            P.do("vector", lambda e, f=f, tt=tt: e.tensor_mul(
                xc[:, f, tt * 512: tt * 512 + 512],
                xm[:, tt * 512: tt * 512 + 512], ybc[tt]),
                reads=[xm_sl] + cx.psl(4 + tt), writes=[xc_sl[f]],
                desc=f"{pr} gmul{f}.{tt}")

    for do_ in range(NCH):
        db = do_ % 2
        pb = 0 if db == 0 else 4
        P.do("gpsimd", lambda e, do_=do_, db=db: e.dma_start(
            out=ows[:, db, :, :],
            in_=OW[:, do_ * 128:(do_ + 1) * 128].rearrange("(c p) o -> p c o", p=128)),
            reads=(), writes=[ows_sl[db]], dma=True, qkey=f"l1a_ow{db}",
            desc=f"{pr} ldow{do_}")
        P.do("scalar", lambda e, do_=do_: e.dma_start(
            out=hfst[:, :], in_=hfh[b, do_ * 128:(do_ + 1) * 128, :]),
            reads=[hf_sl[do_]], writes=[hfst_sl, xm_sl], dma=True, qkey="l1a_hfst",
            desc=f"{pr} ldhf{do_}")
        P.do("sync", lambda e, do_=do_: e.dma_start(
            out=xst[:, 0, :], in_=xin[b, do_ * 128:(do_ + 1) * 128, :]),
            reads=(), writes=[xst_sl[0]], dma=True, qkey="l1a_xst",
            desc=f"{pr} ldx4_{do_}")
        for tt in range(4):
            for ee in range(NE):
                P.do("tensor", lambda e, do_=do_, db=db, tt=tt, ee=ee, pb=pb: e.matmul(
                    cx.ps(pb + tt), ows[:, db, ee, :],
                    xc[:, ee, tt * 512: tt * 512 + 512],
                    start=(ee == 0), stop=(ee == NE - 1)),
                    reads=[ows_sl[db], xc_sl[ee]], writes=cx.psl(pb + tt),
                    desc=f"{pr} owmm{do_}.{tt}.{ee}")
        P.do("vector", lambda e, do_=do_, pb=pb: e.scalar_tensor_tensor(
            hfst[:, :], cx.ps(pb, 0, 2048), outb[:, do_:do_ + 1], hfst[:, :],
            ALU.add, ALU.add),
            reads=cx.psl(pb, pb + 1, pb + 2, pb + 3) + [hfst_sl, cx.csl["l1a_outb"]],
            writes=[hfst_sl], desc=f"{pr} oev{do_}")
        P.do("vector", lambda e, do_=do_: e.tensor_add(
            xst[:, 0, :], xst[:, 0, :], hfst[:, :]),
            reads=[hfst_sl, xst_sl[0]], writes=[xst_sl[0]], desc=f"{pr} oadd{do_}")
        P.do("sync", lambda e, do_=do_: e.dma_start(
            out=xout[b, do_ * 128:(do_ + 1) * 128, :], in_=xst[:, 0, :]),
            reads=[xst_sl[0]], writes=[sl(f"{pr}_xo{do_}")], dma=True,
            qkey="l1a_xo", desc=f"{pr} stx{do_}")
    P.barrier()


def l1a_dram(nc, cx):
    dr = {}
    dr["L1_in_wT"] = nc.dram_tensor("L1_in_wT", [D, 4096], F32,
                                    kind="ExternalInput").ap()
    dr["L1_dt_wT"] = nc.dram_tensor("L1_dt_wT", [2048, 2048], F32,
                                    kind="ExternalInput").ap()
    dr["L1_bcd_wT"] = nc.dram_tensor("L1_bcd_wT", [2048, 48], F32,
                                     kind="ExternalInput").ap()
    dr["L1_out_wT"] = nc.dram_tensor("L1_out_wT", [2048, D], F32,
                                     kind="ExternalInput").ap()
    cx.dram.update(dr)
    for nm, sh in [("l1a_pcols", [128, 112]), ("l1a_bcdb", [16, 3]),
                   ("l1a_outb", [128, 8])]:
        t = nc.dram_tensor(nm + "_hbm", sh, F32, kind="ExternalInput")
        cx.dram[nm + "_hbm"] = t.ap()
        cx.galloc(nm, sh, F32)
        cx.csl[nm] = cx.P.slot("const_" + nm)
        cx.P.qfinal["kparam"] = True
        cx.P.do("sync", lambda e, nm=nm: e.dma_start(
            out=cx.t[nm].ap(), in_=cx.dram[nm + "_hbm"]),
            reads=(), writes=[cx.csl[nm]], dma=True, qkey="kparam", desc=f"ld {nm}")


def l1a_host_inputs(inputs):
    in_w = np.asarray(inputs["L1_attn_in_w"], np.float32)
    dt_w = np.asarray(inputs["L1_attn_dt_w"], np.float32)
    conv_w = np.asarray(inputs["L1_attn_conv_w"], np.float32)
    pcols = np.zeros((128, 112), np.float32)
    pcols[:, 0:16] = col128(inputs["L1_attn_in_b"][:2048])
    pcols[:, 16:32] = col128(inputs["L1_attn_in_b"][2048:])
    pcols[:, 32:48] = col128(conv_w[:, 0, 0])
    pcols[:, 48:64] = col128(conv_w[:, 0, 1])
    pcols[:, 64:80] = col128(conv_w[:, 0, 2])
    pcols[:, 80:96] = col128(inputs["L1_attn_conv_b"])
    pcols[:, 96:112] = col128(inputs["L1_attn_dt_b"])
    bcdb = np.stack([np.asarray(inputs["L1_attn_B_b"], np.float32),
                     np.asarray(inputs["L1_attn_C_b"], np.float32),
                     np.asarray(inputs["L1_attn_D_b"], np.float32)], axis=1)
    bcdw = np.concatenate([np.asarray(inputs["L1_attn_B_w"], np.float32).T,
                           np.asarray(inputs["L1_attn_C_w"], np.float32).T,
                           np.asarray(inputs["L1_attn_D_w"], np.float32).T], axis=1)
    return {
        "L1_in_wT": _f32(in_w.T),
        "L1_dt_wT": _f32(dt_w.T),
        "L1_bcd_wT": _f32(bcdw),
        "L1_out_wT": _f32(np.asarray(inputs["L1_attn_out_w"], np.float32).T),
        "l1a_pcols_hbm": _f32(pcols),
        "l1a_bcdb_hbm": _f32(bcdb),
        "l1a_outb_hbm": col128(inputs["L1_attn_out_b"]),
    }


def build_program_l1a():
    nc = bass.Bass("TRN2", target_bir_lowering=False)
    P = Prog()
    cx = build_common(nc, P)
    cx.dram["x_in"] = nc.dram_tensor("x_in", [NB, D, T], F32,
                                     kind="ExternalInput").ap()
    cx.dram["x_out"] = nc.dram_tensor("x_out", [NB, D, T], F32,
                                      kind="ExternalOutput").ap()
    cx.dram["tgtf_hbm"] = nc.dram_tensor("tgtf_hbm", [NB, D, T], F32,
                                         kind="Internal").ap()
    for nm, (sh, _dt) in CONST_SHAPES.items():
        cx.dram[nm] = nc.dram_tensor(nm, sh, F32, kind="ExternalInput").ap()
    emit_load_consts(cx)
    l1a_dram(nc, cx)
    for b in range(NB):
        emit_l1a(cx, b, cx.dram["x_in"], cx.dram["x_out"])
    P.do("gpsimd", lambda e: e.nop(),
         reads=[P.slot(f"l1a{b}_xo{c}") for b in range(NB) for c in range(NCH)],
         desc="final wait")
    finish_program(nc, P)
    return nc, P



def emit_l0a(cx, b, xin, xout):
    nc, P = cx.nc, cx.P
    sl = P.slot
    pr = f"l0a{b}"
    HP = 8
    R_MEAN, R_MSQ, R_RSTD, R_TMP = 0, 1, 2, 3

    cx.phase_begin()
    cx.palloc("rows", [1, 4 * T], F32)
    hbf = cx.palloc("hbf", [128, NCH, T], BF16).ap()
    wvu = cx.palloc("wvu", [128, NCH, T], BF16).ap()
    vab = cx.palloc("vab", [128, 16, 16, 66], BF16).ap()
    tabC = cx.palloc("tabC", [128, T], F32).ap()
    tabS = cx.palloc("tabS", [128, T], F32).ap()
    rsc = cx.palloc("rsc", [128, 2, T], F32).ap()
    qcb = cx.palloc("qcb", [128, T], BF16).ap()
    qfm = cx.palloc("qfm", [128, T], BF16).ap()
    kfm = cx.palloc("kfm", [128, T], BF16).ap()
    tpA = cx.palloc("tpA", [128, 1, 1024], BF16).ap()
    tpB = cx.palloc("tpB", [128, 1, 1024], BF16).ap()
    asc = cx.palloc("asc", [128, 512], F32).ap()
    svc = cx.palloc("svc", [128, 1], BF16).ap()
    svr = cx.palloc("svr", [1, 16, 66], BF16).ap()
    wqs = cx.palloc("wqs", [128, 2, 8, 128], BF16).ap()
    wks = cx.palloc("wks", [128, 2, 8, 128], BF16).ap()
    wvs = cx.palloc("wvs", [128, 8, 1024], BF16, at="rsc").ap()
    xst = cx.palloc("xst", [128, 1, T], F32).ap()
    xsq = cx.palloc("xsq", [128, 1, T], BF16, at="wvu").ap()

    hbf_sl = [sl(f"{pr}_h{c}") for c in range(NCH)]
    wvu_sl = [sl(f"{pr}_wv{c}") for c in range(NCH)]
    vab_sl = [sl(f"{pr}_v{tc}") for tc in range(16)]
    sq_sl = [sl(f"{pr}_sq0")]
    xst_sl = [sl(f"{pr}_xst")]
    tab_sl = sl(f"{pr}_tab")
    rsc_sl = sl(f"{pr}_rsc")
    qcb_sl = sl(f"{pr}_qcb")
    qfm_sl, kfm_sl = sl(f"{pr}_qfm"), sl(f"{pr}_kfm")
    tpA_sl = [sl(f"{pr}_tpA0"), sl(f"{pr}_tpA1")]
    tpB_sl = [sl(f"{pr}_tpB0"), sl(f"{pr}_tpB1")]
    asc_sl = sl(f"{pr}_asc")
    sv_sl = sl(f"{pr}_sv")
    wqs_sl = [sl(f"{pr}_wq0"), sl(f"{pr}_wq1")]
    wks_sl = [sl(f"{pr}_wk0"), sl(f"{pr}_wk1")]
    wvs_sl = sl(f"{pr}_wvs")

    WQ = cx.dram["L0_attn_w_q"]
    WKV = cx.dram["L0_attn_w_kv"]
    WO = cx.dram["L0_attn_w_o"]
    lnwb = cx.t["l0a_lnwb"].ap()
    fbc = cx.t["l0a_fbsgn"].ap()
    theta_t = cx.t["l0a_theta"].ap()

    ramp = rsc[:, 0, :]
    P.do("gpsimd", lambda e: e.iota(ramp, [[1, T]], channel_multiplier=0,
                                    allow_small_or_imprecise_dtypes=True),
         reads=(), writes=[rsc_sl], desc=f"{pr} iota")
    def _r(i):
        return asc[0:1, 16 + i:17 + i]
    P.do("vector", lambda e: e.reciprocal(_r(0), cx.fcol(220.0, 0, 1)),
         reads=[cx.csl["c_fconsts"]], writes=[asc_sl], desc=f"{pr} r220")
    P.do("vector", lambda e: e.tensor_mul(_r(1), theta_t, _r(0)),
         reads=[cx.csl["l0a_theta"], sv_sl], writes=[asc_sl], desc=f"{pr} q0")
    P.do("vector", lambda e: e.tensor_scalar_mul(_r(2), _r(1), 4097.0),
         reads=[asc_sl], writes=[asc_sl], desc=f"{pr} vc")
    P.do("vector", lambda e: e.tensor_sub(_r(3), _r(2), _r(1)),
         reads=[asc_sl], writes=[asc_sl], desc=f"{pr} vd")
    P.do("vector", lambda e: e.tensor_sub(_r(3), _r(2), _r(3)),
         reads=[asc_sl], writes=[asc_sl], desc=f"{pr} qh")
    P.do("vector", lambda e: e.tensor_sub(_r(4), _r(1), _r(3)),
         reads=[asc_sl], writes=[asc_sl], desc=f"{pr} ql")
    P.do("vector", lambda e: e.scalar_tensor_tensor(
        _r(5), _r(3), -220.0, theta_t, ALU.mult, ALU.add),
        reads=[asc_sl, cx.csl["l0a_theta"]], writes=[asc_sl], desc=f"{pr} e1")
    P.do("vector", lambda e: e.scalar_tensor_tensor(
        _r(5), _r(4), -220.0, _r(5), ALU.mult, ALU.add),
        reads=[asc_sl], writes=[asc_sl], desc=f"{pr} e2")
    P.do("vector", lambda e: e.tensor_mul(_r(5), _r(5), _r(0)),
        reads=[asc_sl], writes=[asc_sl], desc=f"{pr} corr")
    P.do("vector", lambda e: e.tensor_add(asc[0:1, 0:1], _r(1), _r(5)),
        reads=[asc_sl], writes=[asc_sl], desc=f"{pr} th220")
    P.do("tensor", lambda e: e.matmul(
        cx.ps(0, 0, 1, (0, 128)), cx.t["c_ones_row"].ap()[0:1, 0:128],
        asc[0:1, 0:1], start=True, stop=True),
        reads=[asc_sl, cx.csl["c_ones_row"]], writes=cx.psl(0), desc=f"{pr} thbc")
    P.do("vector", lambda e: e.tensor_mul(asc[:, 0:1], fbc[:, 0:1],
                                          cx.ps(0, 0, 1, (0, 128))),
         reads=cx.psl(0) + [cx.csl["l0a_fbsgn"]], writes=[asc_sl],
         desc=f"{pr} freqs")
    P.do("vector", lambda e: e.tensor_scalar_mul(ramp, ramp, asc[:, 0:1]),
         reads=[asc_sl, rsc_sl], writes=[rsc_sl], desc=f"{pr} ang")
    kt = rsc[:, 1, :]
    P.do("vector", lambda e: e.tensor_scalar(kt, ramp, INV_2PI, MAGIC,
                                             ALU.mult, ALU.add),
         reads=[rsc_sl], writes=[rsc_sl], desc=f"{pr} kmag")
    P.do("vector", lambda e: e.tensor_scalar_add(kt, kt, -MAGIC),
         reads=[rsc_sl], writes=[rsc_sl], desc=f"{pr} kmag2")
    rt = tabS[:, :]
    P.do("vector", lambda e: e.scalar_tensor_tensor(rt, kt, -CW[0], ramp,
                                                    ALU.mult, ALU.add),
         reads=[rsc_sl], writes=[tab_sl], desc=f"{pr} cw0")
    for i in range(1, 6):
        P.do("vector", lambda e, i=i: e.scalar_tensor_tensor(
            rt, kt, -CW[i], rt, ALU.mult, ALU.add),
            reads=[rsc_sl, tab_sl], writes=[tab_sl], desc=f"{pr} cw{i}")
    mt0 = rsc[:, 0, :]
    P.do("vector", lambda e: e.tensor_single_scalar(mt0, rt, float(np.pi), ALU.is_gt),
         reads=[tab_sl], writes=[rsc_sl], desc=f"{pr} rgt")
    P.do("vector", lambda e: e.scalar_tensor_tensor(
        rt, mt0, float(-2 * np.pi), rt, ALU.mult, ALU.add),
        reads=[rsc_sl, tab_sl], writes=[tab_sl], desc=f"{pr} rwrap1")
    P.do("vector", lambda e: e.tensor_single_scalar(mt0, rt, float(-np.pi), ALU.is_lt),
         reads=[tab_sl], writes=[rsc_sl], desc=f"{pr} rlt")
    P.do("vector", lambda e: e.scalar_tensor_tensor(
        rt, mt0, float(2 * np.pi), rt, ALU.mult, ALU.add),
        reads=[rsc_sl, tab_sl], writes=[tab_sl], desc=f"{pr} rwrap2")
    P.do("vector", lambda e: e.tensor_scalar(
        rt, rt, float(np.pi), float(-np.pi), ALU.min, ALU.max),
        reads=[tab_sl], writes=[tab_sl], desc=f"{pr} rclamp")
    wt = tabC[:, :]
    P.do("vector", lambda e: e.tensor_scalar_add(wt, rt, float(np.pi / 2)),
         reads=[tab_sl], writes=[tab_sl], desc=f"{pr} wshift")
    mt = rsc[:, 0, :]
    P.do("vector", lambda e: e.tensor_single_scalar(mt, wt, float(np.pi), ALU.is_gt),
         reads=[tab_sl], writes=[rsc_sl], desc=f"{pr} wgt")
    P.do("vector", lambda e: e.scalar_tensor_tensor(
        wt, mt, float(-2 * np.pi), wt, ALU.mult, ALU.add),
        reads=[rsc_sl, tab_sl], writes=[tab_sl], desc=f"{pr} wrap")
    P.do("vector", lambda e: e.tensor_scalar(
        wt, wt, float(np.pi), float(-np.pi), ALU.min, ALU.max),
        reads=[tab_sl], writes=[tab_sl], desc=f"{pr} wclamp")
    P.do("scalar", lambda e: e.activation(tabC[:, :], wt, AF.Sin),
         reads=[tab_sl], writes=[tab_sl], desc=f"{pr} cos")
    P.do("scalar", lambda e: e.activation(tabS[:, :], rt, AF.Sin),
         reads=[tab_sl], writes=[tab_sl], desc=f"{pr} sin")
    P.do("vector", lambda e: e.tensor_scalar_mul(tabS[:, :], tabS[:, :], fbc[:, 1:2]),
         reads=[tab_sl, cx.csl["l0a_fbsgn"]], writes=[tab_sl], desc=f"{pr} sgn")

    for c in range(NCH):
        P.do("sync", lambda e, c=c: e.dma_start(
            out=xst[:, 0, :], in_=xin[b, c * 128:(c + 1) * 128, :]),
            reads=(), writes=[xst_sl[0]], dma=True, qkey="l0a_xst",
            desc=f"{pr} ldx{c}")
        P.do("vector", lambda e, c=c: e.tensor_copy(hbf[:, c, :], xst[:, 0, :]),
             reads=[xst_sl[0]], writes=[hbf_sl[c]], desc=f"{pr} cast{c}")
    emit_stats_rows(cx, pr, hbf, hbf_sl, xsq, sq_sl, R_MEAN, R_MSQ)
    _rm, _rq, _rr, _rt = (cx.row(R_MEAN), cx.row(R_MSQ), cx.row(R_RSTD),
                          cx.row(R_TMP))
    P.do("vector", lambda e: e.tensor_mul(_rt, _rm, _rm),
         reads=[cx.rowsl(R_MEAN)], writes=[cx.rowsl(R_TMP)], desc=f"{pr} m2")
    P.do("vector", lambda e: e.tensor_sub(_rq, _rq, _rt),
         reads=[cx.rowsl(R_MSQ), cx.rowsl(R_TMP)], writes=[cx.rowsl(R_MSQ)],
         desc=f"{pr} var")
    P.do("scalar", lambda e: e.activation(_rt, _rq, AF.Ln, bias=cx.fcol(EPS)),
         reads=[cx.rowsl(R_MSQ), cx.csl["c_fconsts"]], writes=[cx.rowsl(R_TMP)],
         desc=f"{pr} lnv")
    P.do("scalar", lambda e: e.activation(_rr, _rt, AF.Exp, scale=-0.5),
         reads=[cx.rowsl(R_TMP)], writes=[cx.rowsl(R_RSTD)], desc=f"{pr} rstd")
    P.do("vector", lambda e: e.tensor_mul(_rt, _rq, _rr),
         reads=[cx.rowsl(R_MSQ), cx.rowsl(R_RSTD)], writes=[cx.rowsl(R_TMP)],
         desc=f"{pr} vy1")
    P.do("vector", lambda e: e.tensor_mul(_rt, _rt, _rr),
         reads=[cx.rowsl(R_TMP), cx.rowsl(R_RSTD)], writes=[cx.rowsl(R_TMP)],
         desc=f"{pr} vy2")
    P.do("scalar", lambda e: e.activation(_rt, _rt, AF.Ln, bias=cx.fcol(EPS)),
         reads=[cx.rowsl(R_TMP), cx.csl["c_fconsts"]], writes=[cx.rowsl(R_TMP)],
         desc=f"{pr} lnvy")
    P.do("scalar", lambda e: e.activation(_rt, _rt, AF.Exp, scale=-0.5),
         reads=[cx.rowsl(R_TMP)], writes=[cx.rowsl(R_TMP)], desc=f"{pr} ri")
    P.do("vector", lambda e: e.tensor_mul(_rr, _rr, _rt),
         reads=[cx.rowsl(R_RSTD), cx.rowsl(R_TMP)], writes=[cx.rowsl(R_RSTD)],
         desc=f"{pr} a2")
    P.do("vector", lambda e: e.tensor_scalar_mul(_rm, _rm, -1.0),
         reads=[cx.rowsl(R_MEAN)], writes=[cx.rowsl(R_MEAN)], desc=f"{pr} negm")
    P.do("vector", lambda e: e.tensor_mul(_rm, _rm, _rr),
         reads=[cx.rowsl(R_MEAN), cx.rowsl(R_RSTD)], writes=[cx.rowsl(R_MEAN)],
         desc=f"{pr} m2a")
    ab = emit_bcast(cx, pr + "a", R_RSTD, [0, 1, 2, 3])
    mb = emit_bcast(cx, pr + "m", R_MEAN, [4, 5, 6, 7])
    for c in range(NCH):
        P.do("scalar", lambda e, c=c: e.dma_start(
            out=xst[:, 0, :], in_=xin[b, c * 128:(c + 1) * 128, :]),
            reads=(), writes=[xst_sl[0]], dma=True, qkey="l0a_xst",
            desc=f"{pr} ldx2_{c}")
        for ti in range(4):
            P.do("vector", lambda e, c=c, ti=ti: e.tensor_mul(
                xst[:, 0, ti * 512: ti * 512 + 512],
                xst[:, 0, ti * 512: ti * 512 + 512], ab[ti]),
                reads=[xst_sl[0]] + cx.psl(ti), writes=[xst_sl[0]],
                desc=f"{pr} ha{c}.{ti}")
            P.do("vector", lambda e, c=c, ti=ti: e.tensor_add(
                xst[:, 0, ti * 512: ti * 512 + 512],
                xst[:, 0, ti * 512: ti * 512 + 512], mb[ti]),
                reads=[xst_sl[0]] + cx.psl(4 + ti), writes=[xst_sl[0]],
                desc=f"{pr} hm{c}.{ti}")
        P.do("vector", lambda e, c=c: e.tensor_scalar(
            hbf[:, c, :], xst[:, 0, :], lnwb[:, c:c + 1], lnwb[:, 8 + c:9 + c],
            ALU.mult, ALU.add),
            reads=[xst_sl[0], cx.csl["l0a_lnwb"]], writes=[hbf_sl[c]],
            desc=f"{pr} hw{c}")

    P.do("gpsimd", lambda e: e.dma_start(
        out=wvs[:, :, :],
        in_=WKV[:, 1024:2048].rearrange("(c p) f -> p c f", p=128)),
        reads=(), writes=[wvs_sl, rsc_sl], dma=True, qkey="l0a_wvs",
        desc=f"{pr} ldwv")
    P.do("vector", lambda e: e.memset(vab[:, :, :, 64:65], 1.0),
         reads=(), writes=vab_sl, desc=f"{pr} vones")
    for tc in range(16):
        pb = (tc % 2) * 2
        for fh in range(2):
            for dd in range(NCH):
                P.do("tensor", lambda e, tc=tc, fh=fh, dd=dd, pb=pb: e.matmul(
                    cx.ps(pb + fh), hbf[:, dd, tc * 128:(tc + 1) * 128],
                    wvs[:, dd, fh * 512:(fh + 1) * 512],
                    start=(dd == 0), stop=(dd == NCH - 1)),
                    reads=[wvs_sl, rsc_sl, hbf_sl[dd]], writes=cx.psl(pb + fh),
                    desc=f"{pr} vmm{tc}.{fh}.{dd}")
            P.do("vector", lambda e, tc=tc, fh=fh, pb=pb: e.tensor_copy(
                vab[:, tc, fh * 8:(fh + 1) * 8, 0:64], cx.ps(pb + fh)),
                reads=cx.psl(pb + fh), writes=[vab_sl[tc]],
                desc=f"{pr} vev{tc}.{fh}")

    for c in range(HP):
        db = c % 2
        P.do("gpsimd", lambda e, c=c, db=db: e.dma_start(
            out=wqs[:, db, :, :],
            in_=WQ[:, c * 128:(c + 1) * 128].rearrange("(cc p) f -> p cc f", p=128)),
            reads=(), writes=[wqs_sl[db]], dma=True, qkey=f"l0a_wq{db}",
            desc=f"{pr} ldwq{c}")
        P.do("gpsimd", lambda e, c=c, db=db: e.dma_start(
            out=wks[:, db, :, :],
            in_=WKV[:, c * 128:(c + 1) * 128].rearrange("(cc p) f -> p cc f", p=128)),
            reads=(), writes=[wks_sl[db]], dma=True, qkey=f"l0a_wk{db}",
            desc=f"{pr} ldwk{c}")
        for which, wsl, wbuf, dst, dst_sl in (
                ("q", wqs_sl[db], wqs, qfm, qfm_sl),
                ("k", wks_sl[db], wks, kfm, kfm_sl)):
            for tt in range(4):
                for dd in range(NCH):
                    P.do("tensor", lambda e, c=c, db=db, tt=tt, dd=dd, wbuf=wbuf,
                         wsl=wsl: e.matmul(
                             cx.ps(tt), wbuf[:, db, dd, :],
                             hbf[:, dd, tt * 512: tt * 512 + 512],
                             start=(dd == 0), stop=(dd == NCH - 1)),
                         reads=[wsl, hbf_sl[dd]], writes=cx.psl(tt),
                         desc=f"{pr} {which}mm{c}.{tt}.{dd}")
            for hh2 in range(2):
                P.do("scalar", lambda e, hh2=hh2: e.activation(
                    rsc[:, 0, hh2 * 1024:(hh2 + 1) * 1024],
                    cx.ps(2 * hh2, 0, 1024), AF.Copy),
                    reads=cx.psl(2 * hh2, 2 * hh2 + 1), writes=[rsc_sl],
                    desc=f"{pr} {which}ef{c}.{hh2}")
            P.do("vector", lambda e: e.tensor_copy(qcb[:, :], cx.ps(0, 0, 2048)),
                 reads=cx.psl(0, 1, 2, 3), writes=[qcb_sl], desc=f"{pr} {which}eb{c}")
            for tt in range(4):
                P.do("tensor", lambda e, tt=tt: e.matmul(
                    cx.ps(4 + tt), cx.t["c_pswap_l0"].ap(),
                    qcb[:, tt * 512: tt * 512 + 512], start=True, stop=True),
                    reads=[qcb_sl, cx.csl["c_pswap_l0"]], writes=cx.psl(4 + tt),
                    desc=f"{pr} {which}pp{c}.{tt}")
            P.do("vector", lambda e: e.tensor_mul(rsc[:, 0, :], rsc[:, 0, :],
                                                  tabC[:, :]),
                 reads=[rsc_sl, tab_sl], writes=[rsc_sl], desc=f"{pr} {which}tc{c}")
            for tt in range(4):
                P.do("vector", lambda e, tt=tt: e.tensor_mul(
                    rsc[:, 1, tt * 512: tt * 512 + 512], cx.ps(4 + tt),
                    tabS[:, tt * 512: tt * 512 + 512]),
                    reads=cx.psl(4 + tt) + [tab_sl], writes=[rsc_sl],
                    desc=f"{pr} {which}ts{c}.{tt}")
            P.do("vector", lambda e, dst=dst: e.tensor_add(
                dst[:, :], rsc[:, 0, :], rsc[:, 1, :]),
                reads=[rsc_sl], writes=[dst_sl], desc=f"{pr} {which}rot{c}")

        for hh in range(2):
            h = 2 * c + hh
            tc16 = h
            for kc in range(16):
                P.do("tensor", lambda e, kc=kc, h=h: e.matmul(
                    cx.ps(6, 0, 1, (0, 65)), vab[:, kc, h, 0:65],
                    cx.t["c_ones_col"].ap(), start=(kc == 0), stop=(kc == 15)),
                    reads=[vab_sl[kc], cx.csl["c_ones_col"]], writes=cx.psl(6),
                    desc=f"{pr} sv{h}.{kc}")
            P.do("vector", lambda e: e.tensor_copy(svc[0:65, 0:1],
                                                   cx.ps(6, 0, 1, (0, 65))),
                 reads=cx.psl(6), writes=[sv_sl], desc=f"{pr} svc{h}")
            P.do("tensor", lambda e: e.matmul(
                cx.ps(7, 0, 65, (0, 1)), svc[0:65, 0:1],
                cx.t["c_eye"].ap()[0:65, 0:65], start=True, stop=True),
                reads=[sv_sl, cx.csl["c_eye"]], writes=cx.psl(7),
                desc=f"{pr} svt{h}")
            P.do("vector", lambda e, h=h: e.tensor_copy(
                svr[0:1, h, 0:65], cx.ps(7, 0, 65, (0, 1))),
                reads=cx.psl(7), writes=[sv_sl], desc=f"{pr} svr{h}")

        for qt in range(4):
            q0 = qt * 512
            hA, hB = 2 * c, 2 * c + 1
            P.do("tensor", lambda e, hA=hA: e.matmul(
                cx.ps(6, 0, 512, (0, 65)), svr[0:1, hA, 0:65],
                cx.t["c_ones512"].ap(), start=True, stop=False),
                reads=[sv_sl, cx.csl["c_ones512"]], writes=cx.psl(6),
                desc=f"{pr} wsvA{c}.{qt}")
            P.do("tensor", lambda e, hB=hB: e.matmul(
                cx.ps(7, 0, 512, (64, 128)), svr[0:1, hB, 0:64],
                cx.t["c_ones512"].ap(), start=True, stop=False,
                tile_position=(0, 64)),
                reads=[sv_sl, cx.csl["c_ones512"]], writes=cx.psl(7),
                desc=f"{pr} wsvB{c}.{qt}")
            P.do("tensor", lambda e, hB=hB: e.matmul(
                cx.ps(5, 0, 512, (0, 1)), svr[0:1, hB, 64:65],
                cx.t["c_ones512"].ap(), start=True, stop=False),
                reads=[sv_sl, cx.csl["c_ones512"]], writes=cx.psl(5),
                desc=f"{pr} wsvBd{c}.{qt}")
            for kc in range(16):
                kb = kc % 2
                P.do("tensor", lambda e, c=c, kc=kc, kb=kb, q0=q0: e.matmul(
                    cx.ps(kb), kfm[0:64, kc * 128:(kc + 1) * 128],
                    qfm[0:64, q0:q0 + 512], start=True, stop=True),
                    reads=[kfm_sl, qfm_sl], writes=cx.psl(kb),
                    desc=f"{pr} sA{c}.{qt}.{kc}")
                P.do("tensor", lambda e, c=c, kc=kc, kb=kb, q0=q0: e.matmul(
                    cx.ps(2 + kb), kfm[64:128, kc * 128:(kc + 1) * 128],
                    qfm[64:128, q0:q0 + 512], start=True, stop=True,
                    tile_position=(64, 0)),
                    reads=[kfm_sl, qfm_sl], writes=cx.psl(2 + kb),
                    desc=f"{pr} sB{c}.{qt}.{kc}")
                if kb == 1:
                    g = 0
                    P.do("scalar", lambda e, g=g: e.activation(
                        tpA[:, g, :], cx.ps(0, 0, 1024), AF.Square,
                        bias=cx.fcol(1.0, 0, 128), scale=0.125),
                        reads=cx.psl(0, 1) + [cx.csl["c_fconsts"]],
                        writes=[tpA_sl[g]], desc=f"{pr} tA{c}.{qt}.{kc}")
                    P.do("vector", lambda e, g=g: e.tensor_scalar(
                        rsc[:, 0, 0:1024], cx.ps(2, 0, 1024), 0.125, 1.0,
                        ALU.mult, ALU.add),
                        reads=cx.psl(2, 3), writes=[rsc_sl],
                        desc=f"{pr} tB1{c}.{qt}.{kc}")
                    P.do("vector", lambda e, g=g: e.tensor_mul(
                        tpB[:, g, :], rsc[:, 0, 0:1024], rsc[:, 0, 0:1024]),
                        reads=[rsc_sl], writes=[tpB_sl[g]],
                        desc=f"{pr} tB2{c}.{qt}.{kc}")
                    for k2 in (kc - 1, kc):
                        gg = (k2 % 2) * 512
                        P.do("tensor", lambda e, k2=k2, gg=gg, g=g, hA=hA: e.matmul(
                            cx.ps(6, 0, 512, (0, 65)), vab[:, k2, hA, 0:65],
                            tpA[:, g, gg:gg + 512], start=False,
                            stop=(k2 == 15)),
                            reads=[tpA_sl[g], vab_sl[k2]], writes=cx.psl(6),
                            desc=f"{pr} avA{c}.{qt}.{k2}")
                        P.do("tensor", lambda e, k2=k2, gg=gg, g=g, hB=hB: e.matmul(
                            cx.ps(7, 0, 512, (64, 128)), vab[:, k2, hB, 0:64],
                            tpB[:, g, gg:gg + 512], start=False,
                            stop=(k2 == 15), tile_position=(0, 64)),
                            reads=[tpB_sl[g], vab_sl[k2]], writes=cx.psl(7),
                            desc=f"{pr} avB{c}.{qt}.{k2}")
                        P.do("tensor", lambda e, k2=k2, gg=gg, g=g, hB=hB: e.matmul(
                            cx.ps(5, 0, 512, (0, 1)), vab[:, k2, hB, 64:65],
                            tpB[:, g, gg:gg + 512], start=False,
                            stop=(k2 == 15)),
                            reads=[tpB_sl[g], vab_sl[k2]], writes=cx.psl(5),
                            desc=f"{pr} avBd{c}.{qt}.{k2}")
            P.do("vector", lambda e: e.reciprocal(
                asc[64:65, 0:512], cx.ps(6, 0, 512, (64, 65))),
                reads=cx.psl(6), writes=[asc_sl], desc=f"{pr} rcA{c}.{qt}")
            P.do("tensor", lambda e: e.matmul(
                cx.ps(4, 0, 512, (0, 64)), cx.t["c_ones_row"].ap()[64:65, 0:64],
                asc[64:65, 0:512], start=True, stop=True),
                reads=[asc_sl, cx.csl["c_ones_row"]], writes=cx.psl(4),
                desc=f"{pr} rbA{c}.{qt}")
            P.do("vector", lambda e: e.tensor_copy(asc[0:64, 0:512],
                                                   cx.ps(4, 0, 512, (0, 64))),
                 reads=cx.psl(4), writes=[asc_sl], desc=f"{pr} reA{c}.{qt}")
            P.do("vector", lambda e, c=c, q0=q0: e.tensor_mul(
                wvu[0:64, c, q0:q0 + 512], cx.ps(6, 0, 512, (0, 64)),
                asc[0:64, 0:512]),
                reads=cx.psl(6) + [asc_sl], writes=[wvu_sl[c]],
                desc=f"{pr} wvA{c}.{qt}")
            P.do("vector", lambda e: e.reciprocal(
                asc[0:1, 0:512], cx.ps(5, 0, 512, (0, 1))),
                reads=cx.psl(5), writes=[asc_sl], desc=f"{pr} rcB{c}.{qt}")
            P.do("tensor", lambda e: e.matmul(
                cx.ps(4, 0, 512, (64, 128)), cx.t["c_ones_row"].ap()[0:1, 0:64],
                asc[0:1, 0:512], start=True, stop=True, tile_position=(0, 64)),
                reads=[asc_sl, cx.csl["c_ones_row"]], writes=cx.psl(4),
                desc=f"{pr} rbB{c}.{qt}")
            P.do("vector", lambda e: e.tensor_copy(asc[64:128, 0:512],
                                                   cx.ps(4, 0, 512, (64, 128))),
                 reads=cx.psl(4), writes=[asc_sl], desc=f"{pr} reB{c}.{qt}")
            P.do("vector", lambda e, c=c, q0=q0: e.tensor_mul(
                wvu[64:128, c, q0:q0 + 512], cx.ps(7, 0, 512, (64, 128)),
                asc[64:128, 0:512]),
                reads=cx.psl(7) + [asc_sl], writes=[wvu_sl[c]],
                desc=f"{pr} wvB{c}.{qt}")

    for f in range(NCH):
        db = f % 2
        pb = 0 if db == 0 else 4
        P.do("gpsimd", lambda e, f=f, db=db: e.dma_start(
            out=wqs[:, db, :, :],
            in_=WO[:, f * 128:(f + 1) * 128].rearrange("(cc p) o -> p cc o", p=128)),
            reads=(), writes=[wqs_sl[db]], dma=True, qkey=f"l0a_wq{db}",
            desc=f"{pr} ldwo{f}")
        P.do("sync", lambda e, f=f: e.dma_start(
            out=xst[:, 0, :], in_=xin[b, f * 128:(f + 1) * 128, :]),
            reads=(), writes=[xst_sl[0]], dma=True, qkey="l0a_xst",
            desc=f"{pr} ldx3_{f}")
        for tt in range(4):
            for dd in range(NCH):
                P.do("tensor", lambda e, f=f, db=db, tt=tt, dd=dd, pb=pb: e.matmul(
                    cx.ps(pb + tt), wqs[:, db, dd, :],
                    wvu[:, dd, tt * 512: tt * 512 + 512],
                    start=(dd == 0), stop=(dd == NCH - 1)),
                    reads=[wqs_sl[db], wvu_sl[dd]], writes=cx.psl(pb + tt),
                    desc=f"{pr} owmm{f}.{tt}.{dd}")
        P.do("vector", lambda e, f=f, pb=pb: e.tensor_add(
            xst[:, 0, :], xst[:, 0, :], cx.ps(pb, 0, 2048)),
            reads=cx.psl(pb, pb + 1, pb + 2, pb + 3) + [xst_sl[0]],
            writes=[xst_sl[0]], desc=f"{pr} oadd{f}")
        P.do("sync", lambda e, f=f: e.dma_start(
            out=xout[b, f * 128:(f + 1) * 128, :], in_=xst[:, 0, :]),
            reads=[xst_sl[0]], writes=[sl(f"{pr}_xo{f}")], dma=True,
            qkey="l0a_xo", desc=f"{pr} stx{f}")
    P.barrier()


def l0a_dram(nc, cx):
    dr = {}
    dr["L0_attn_w_q"] = nc.dram_tensor("L0_attn_w_q", [D, D], F32,
                                       kind="ExternalInput").ap()
    dr["L0_attn_w_kv"] = nc.dram_tensor("L0_attn_w_kv", [D, 2 * D], F32,
                                        kind="ExternalInput").ap()
    dr["L0_attn_w_o"] = nc.dram_tensor("L0_attn_w_o", [D, D], F32,
                                       kind="ExternalInput").ap()
    cx.dram.update(dr)
    for nm, sh in [("l0a_lnwb", [128, 16]), ("l0a_fbsgn", [128, 2]),
                   ("l0a_theta", [1, 1])]:
        t = nc.dram_tensor(nm + "_hbm", sh, F32, kind="ExternalInput")
        cx.dram[nm + "_hbm"] = t.ap()
        cx.galloc(nm, sh, F32)
        cx.csl[nm] = cx.P.slot("const_" + nm)
        cx.P.qfinal["kparam"] = True
        cx.P.do("sync", lambda e, nm=nm: e.dma_start(
            out=cx.t[nm].ap(), in_=cx.dram[nm + "_hbm"]),
            reads=(), writes=[cx.csl[nm]], dma=True, qkey="kparam", desc=f"ld {nm}")


def l0a_host_inputs(inputs):
    fb, sgn = host_l0_rope_consts()
    lnwb = np.zeros((128, 16), np.float32)
    lnwb[:, 0:8] = col128(inputs["L0_attn_ln_w"])
    lnwb[:, 8:16] = col128(inputs["L0_attn_ln_b"])
    fbsgn = np.concatenate([fb, sgn], axis=1)
    return {
        "L0_attn_w_q": _f32(inputs["L0_attn_w_q"]),
        "L0_attn_w_kv": _f32(inputs["L0_attn_w_kv"]),
        "L0_attn_w_o": _f32(inputs["L0_attn_w_o"]),
        "l0a_lnwb_hbm": lnwb,
        "l0a_fbsgn_hbm": _f32(fbsgn),
        "l0a_theta_hbm": _f32(np.asarray(inputs["L0_attn_theta"]).reshape(1, 1)),
    }


def build_program_l0a():
    nc = bass.Bass("TRN2", target_bir_lowering=False)
    P = Prog()
    cx = build_common(nc, P)
    cx.dram["x_in"] = nc.dram_tensor("x_in", [NB, D, T], F32,
                                     kind="ExternalInput").ap()
    cx.dram["x_out"] = nc.dram_tensor("x_out", [NB, D, T], F32,
                                      kind="ExternalOutput").ap()
    for nm, (sh, _dt) in CONST_SHAPES.items():
        cx.dram[nm] = nc.dram_tensor(nm, sh, F32, kind="ExternalInput").ap()
    emit_load_consts(cx)
    l0a_dram(nc, cx)
    for b in range(NB):
        emit_l0a(cx, b, cx.dram["x_in"], cx.dram["x_out"])
    P.do("gpsimd", lambda e: e.nop(),
         reads=[P.slot(f"l0a{b}_xo{c}") for b in range(NB) for c in range(NCH)],
         desc="final wait")
    finish_program(nc, P)
    return nc, P



def emit_l1f_entry(cx, b, xin, ybuf):
    cx.phase_begin()
    cx.palloc("rows", [1, 4 * T], F32)
    hbf = cx.palloc("e_hbf", [128, NCH, T], BF16).ap()
    xst = cx.palloc("e_xst", [128, 1, T], F32).ap()
    xsq = cx.palloc("e_xsq", [128, 1, T], BF16).ap()
    P = cx.P
    pr = f"l1fe{b}"
    hbf_sl = [P.slot(f"{pr}_h{c}") for c in range(NCH)]
    sq_sl = [P.slot(f"{pr}_sq0")]
    xst_sl = [P.slot(f"{pr}_xst")]
    y_sl = [P.slot(f"yb{b}_{c}") for c in range(NCH)]
    emit_ln_spill(cx, pr, b, xin, hbf, hbf_sl, xst, xst_sl, xsq, sq_sl,
                  spill=ybuf, spill_sl=y_sl, spill_qk=f"l1f_y{b}_",
                  xst_qk="l1f_xst")
    P.barrier()


def emit_l1f_exit(cx, b, xin, ybuf, xout):
    cx.phase_begin()
    xst = cx.palloc("x_xst", [128, 1, T], F32).ap()
    yst = cx.palloc("x_yst", [128, 1, T], F32).ap()
    P = cx.P
    pr = f"l1fx{b}"
    xst_sl = [P.slot(f"{pr}_xst")]
    yst_sl = [P.slot(f"{pr}_yst")]
    for c in range(NCH):
        P.do("sync", lambda e, c=c: e.dma_start(
            out=xst[:, 0, :], in_=xin[b, c * 128:(c + 1) * 128, :]),
            reads=(), writes=[xst_sl[0]], dma=True, qkey="l1f_xst",
            desc=f"{pr} ldx{c}")
        P.do("scalar", lambda e, c=c: e.dma_start(
            out=yst[:, 0, :], in_=ybuf[b, c * 128:(c + 1) * 128, :]),
            reads=[P.slot(f"yb{b}_{c}")], writes=[yst_sl[0]], dma=True,
            qkey="l1f_yst", desc=f"{pr} ldy{c}")
        P.do("vector", lambda e: e.tensor_add(xst[:, 0, :], xst[:, 0, :],
                                              yst[:, 0, :]),
             reads=[xst_sl[0], yst_sl[0]], writes=[xst_sl[0]], desc=f"{pr} add{c}")
        P.do("sync", lambda e, c=c: e.dma_start(
            out=xout[b, c * 128:(c + 1) * 128, :], in_=xst[:, 0, :]),
            reads=[xst_sl[0]], writes=[P.slot(f"{pr}_xo{c}")], dma=True,
            qkey="l1f_xo", desc=f"{pr} stx{c}")
    P.barrier()


def _l1f_attn_alloc(cx):
    cx.phase_begin()
    t = {}
    t["W"] = cx.palloc("a_W", [128, 8, 3072], BF16).ap()
    t["wrk"] = cx.palloc("a_wrk", [128, 4, T], F32, at="a_W").ap()
    t["Wo"] = cx.palloc("a_Wo", [128, 8, 1024], BF16, at="a_W", off2=32768).ap()
    t["xsb"] = cx.palloc("a_xsb", [128, NCH, T], BF16).ap()
    t["vab"] = cx.palloc("a_vab", [128, 16, 16, 65], BF16).ap()
    t["qf"] = cx.palloc("a_qf", [128, NCH, T], BF16).ap()
    t["kf"] = cx.palloc("a_kf", [128, NCH, T], BF16).ap()
    t["rn2"] = cx.palloc("a_rn2", [2, T], F32).ap()
    t["pn"] = cx.palloc("a_pn", [1, T], F32, at="a_rn2").ap()
    t["yst"] = cx.palloc("a_yst", [128, T], F32).ap()
    t["Mbd"] = cx.palloc("a_Mbd", [1, 2, 8, 512], BF16, at="a_W",
                         off2=32768).ap()
    t["sqr"] = cx.palloc("a_sqr", [128, T], BF16, at="a_W", off2=32768).ap()
    t["scr"] = cx.palloc("a_scr", [128, T], F32).ap()
    t["sq"] = cx.palloc("a_sq", [128, T], BF16, at="a_scr").ap()
    t["et"] = cx.palloc("a_et", [128, 2, 512], BF16, at="a_scr", off2=4096).ap()
    t["asc"] = cx.palloc("a_asc", [128, 512], F32, at="a_scr", off2=6144).ap()
    t["kh"] = cx.palloc("a_kh", [128, 8, 2, 8], BF16).ap()
    t["bm"] = cx.palloc("a_bm", [128, 2, 8], F32).ap()
    return t


def emit_l1f_attn_weights(cx, t, l):
    P = cx.P
    for qq in range(4):
        P.do("gpsimd", lambda e, l=l, qq=qq: e.dma_start(
            out=t["W"][:, 2 * qq:2 * qq + 2, :],
            in_=cx.dram["L1f_Wqkv"][l].rearrange(
                "(c p) f -> p c f", p=128)[:, 2 * qq:2 * qq + 2, :]),
            reads=(), writes=[P.slot("a_W"), P.slot("a_wrk"), P.slot("a_Wo"),
                              P.slot("a_Mbd"), P.slot("a_sqr")],
            dma=True, qkey=f"l1f_W{qq}", desc=f"l1f ldW{l}.{qq}")


def emit_l1f_attn(cx, t, l, b, ybuf):
    nc, P = cx.nc, cx.P
    sl = P.slot
    pr = f"l1fa{l}_{b}"
    W, wrk, Wo = t["W"], t["wrk"], t["Wo"]
    xsb, vab, qf, kf = t["xsb"], t["vab"], t["qf"], t["kf"]
    rn2, pn, Mbd, sqr = t["rn2"], t["pn"], t["Mbd"], t["sqr"]
    scr, sq, et, asc = t["scr"], t["sq"], t["et"], t["asc"]
    yst, kh, bm = t["yst"], t["kh"], t["bm"]
    W_sl, wrk_sl, Wo_sl = sl("a_W"), sl("a_wrk"), sl("a_Wo")
    xsb_sl = [sl(f"{pr}_xs{c}") for c in range(NCH)]
    vab_sl = [sl(f"{pr}_v{tc}") for tc in range(16)]
    qf_sl = [sl(f"{pr}_qf{c}") for c in range(NCH)]
    kf_sl = [sl(f"{pr}_kf{c}") for c in range(NCH)]
    rn_sl = sl("a_rn2")
    pn_sl = rn_sl
    scr_sl = sl("a_scr")
    sq_sl = sl("a_sq")
    et_sl = sl("a_et")
    asc_sl = sl("a_asc")
    SCR_ALL = [scr_sl, sq_sl, et_sl, asc_sl]
    yst_sl = sl("a_yst")
    mbd_sl = sl("a_Mbd")
    sqr_sl = sl("a_sqr")
    kh_sl, bm_sl = sl("a_kh"), sl("a_bm")
    wv_sl = [sl(f"{pr}_wv{c}") for c in range(NCH)]
    y_sl = [sl(f"yb{b}_{c}") for c in range(NCH)]
    qsc = cx.t["l1f_qks"].ap()

    for qq in range(4):
        P.do("gpsimd", lambda e, l=l, qq=qq: e.dma_start(
            out=W[:, 2 * qq:2 * qq + 2, :],
            in_=cx.dram["L1f_Wqkv"][l].rearrange(
                "(c p) f -> p c f", p=128)[:, 2 * qq:2 * qq + 2, :]),
            reads=(), writes=[W_sl, wrk_sl, Wo_sl, mbd_sl, sqr_sl], dma=True,
            qkey=f"l1f_W{qq}", desc=f"{pr} ldW.{qq}")

    for c in range(NCH):
        if c < 4:
            P.do("gpsimd", lambda e, c=c: e.dma_start(
                out=xsb[:, c, :], in_=ybuf[b, c * 128:(c + 1) * 128, :]),
                reads=[y_sl[c]], writes=[xsb_sl[c]], dma=True, qkey=f"l1f_xsb{c}",
                desc=f"{pr} shc{c}")
        else:
            P.do("gpsimd", lambda e, c=c: e.dma_start(
                out=xsb[:, c, 1:T], in_=ybuf[b, c * 128:(c + 1) * 128, 0:T - 1]),
                reads=[y_sl[c]], writes=[xsb_sl[c]], dma=True, qkey=f"l1f_xsb{c}",
                desc=f"{pr} shc{c}")
            P.do("vector", lambda e, c=c: e.memset(xsb[:, c, 0:1], 0.0),
                 reads=(), writes=[xsb_sl[c]], desc=f"{pr} shz{c}")

    P.do("vector", lambda e: e.tensor_copy(
        vab[:, :, :, 64:65],
        bass.AP(cx.t["c_ones_col"], 0, [[1, 128], [0, 16], [0, 16], [0, 1]])),
        reads=[cx.csl["c_ones_col"]],
        writes=[sl(f"{pr}_v{tc}") for tc in range(16)],
        desc=f"{pr} vones")
    for tc in range(16):
        pb = (tc % 2) * 2
        for fh in range(2):
            for dd in range(NCH):
                P.do("tensor", lambda e, tc=tc, fh=fh, dd=dd, pb=pb: e.matmul(
                    cx.ps(pb + fh), xsb[:, dd, tc * 128:(tc + 1) * 128],
                    W[:, dd, 2048 + fh * 512: 2048 + (fh + 1) * 512],
                    start=(dd == 0), stop=(dd == NCH - 1)),
                    reads=[W_sl, xsb_sl[dd]], writes=cx.psl(pb + fh),
                    desc=f"{pr} vmm{tc}.{fh}.{dd}")
            P.do("vector", lambda e, tc=tc, fh=fh, pb=pb: e.tensor_copy(
                vab[:, tc, fh * 8:(fh + 1) * 8, 0:64], cx.ps(pb + fh)),
                reads=cx.psl(pb + fh), writes=[vab_sl[tc]],
                desc=f"{pr} vev{tc}.{fh}")

    for which, dstt, dst_sls, wof in (("q", qf, qf_sl, 0), ("k", kf, kf_sl, 1024)):
        for c in range(NCH):
            pb = 0 if c % 2 == 0 else 4
            for tt in range(4):
                for dd in range(NCH):
                    P.do("tensor", lambda e, c=c, tt=tt, dd=dd, wof=wof, pb=pb:
                         e.matmul(cx.ps(pb + tt),
                                  W[:, dd, wof + c * 128: wof + (c + 1) * 128],
                                  xsb[:, dd, tt * 512: tt * 512 + 512],
                                  start=(dd == 0), stop=(dd == NCH - 1)),
                         reads=[W_sl, xsb_sl[dd]], writes=cx.psl(pb + tt),
                         desc=f"{pr} {which}mm{c}.{tt}.{dd}")
            P.do("vector", lambda e, c=c, pb=pb, dstt=dstt: e.tensor_copy(
                dstt[:, c, :], cx.ps(pb, 0, 2048)),
                reads=cx.psl(pb, pb + 1, pb + 2, pb + 3), writes=[dst_sls[c]],
                desc=f"{pr} {which}ev{c}")

    for i in range(4):
        P.do("scalar", lambda e, i=i, l=l: e.dma_start(
            out=wrk[:, i, :], in_=cx.dram["L1f_rot"][i]),
            reads=(), writes=[wrk_sl, W_sl], dma=True, qkey="l1f_wrk",
            desc=f"{pr} ldrot{i}")
    for i in range(4):
        col = 4 * l + i
        P.do("vector", lambda e, i=i, col=col: e.tensor_scalar_mul(
            wrk[:, i, :], wrk[:, i, :], qsc[:, col:col + 1]),
            reads=[wrk_sl, cx.csl["l1f_qks"]], writes=[wrk_sl],
            desc=f"{pr} wrks{i}")

    steps = []
    for c in range(NCH):
        for which in ("k", "q"):
            steps.append((c, which))

    def rope_square(idx):
        c, which = steps[idx]
        dstt = kf if which == "k" else qf
        dst_sls = kf_sl if which == "k" else qf_sl
        P.do("scalar", lambda e, c=c, dstt=dstt: e.activation(
            sqr[:, :], dstt[:, c, :], AF.Square),
            reads=[dst_sls[c]], writes=[sqr_sl, mbd_sl, W_sl],
            desc=f"{pr} {which}sq{c}")

    rope_square(0)
    for idx, (c, which) in enumerate(steps):
        dstt = kf if which == "k" else qf
        dst_sls = kf_sl if which == "k" else qf_sl
        ti0, ti1 = (2, 3) if which == "k" else (0, 1)
        par = idx % 2
        pb = 0 if par == 0 else 4
        gb = 4 if par == 0 else 0
        for tt in range(4):
            P.do("tensor", lambda e, tt=tt, gb=gb: e.matmul(
                cx.ps(gb + tt, 0, 512, (0, 2)), cx.t["c_sel2"].ap(),
                sqr[:, tt * 512: tt * 512 + 512], start=True, stop=True),
                reads=[sqr_sl, cx.csl["c_sel2"]], writes=cx.psl(gb + tt),
                desc=f"{pr} {which}ss{c}.{tt}")
        for tt in range(4):
            P.do("tensor", lambda e, tt=tt, c=c, dstt=dstt, pb=pb: e.matmul(
                cx.ps(pb + tt), cx.t["c_pswap_l1"].ap(),
                dstt[:, c, tt * 512: tt * 512 + 512], start=True, stop=True),
                reads=[dst_sls[c], cx.csl["c_pswap_l1"]], writes=cx.psl(pb + tt),
                desc=f"{pr} {which}P{c}.{tt}")
        if idx + 1 < len(steps):
            rope_square(idx + 1)
        for lh in range(2):
            P.do("scalar", lambda e, gb=gb, lh=lh: e.activation(
                rn2[0:2, lh * 1024:(lh + 1) * 1024],
                cx.ps(gb + 2 * lh, 0, 1024, (0, 2)), AF.Ln,
                bias=cx.fcol(1e-30, 0, 2)),
                reads=cx.psl(gb + 2 * lh, gb + 2 * lh + 1)
                + [cx.csl["c_fconsts"]],
                writes=[rn_sl], desc=f"{pr} {which}ln{c}.{lh}")
        P.do("scalar", lambda e: e.activation(rn2[0:2, :], rn2[0:2, :], AF.Exp,
                                              scale=-0.5),
             reads=[rn_sl], writes=[rn_sl], desc=f"{pr} {which}ex{c}")
        P.do("vector", lambda e: e.tensor_scalar_min(rn2[0:2, :], rn2[0:2, :],
                                                     1e12),
             reads=[rn_sl], writes=[rn_sl], desc=f"{pr} {which}mx{c}")
        for tt in range(4):
            P.do("tensor", lambda e, tt=tt, gb=gb: e.matmul(
                cx.ps(gb + tt), cx.t["c_sel2T32"].ap(),
                rn2[0:2, tt * 512: tt * 512 + 512], start=True, stop=True),
                reads=[rn_sl, cx.csl["c_sel2T32"]], writes=cx.psl(gb + tt),
                desc=f"{pr} {which}G{c}.{tt}")
        P.do("gpsimd", lambda e, c=c, dstt=dstt, ti0=ti0: e.tensor_mul(
            scr[:, :], dstt[:, c, :], wrk[:, ti0, :]),
            reads=[dst_sls[c], wrk_sl], writes=SCR_ALL,
            desc=f"{pr} {which}t1{c}")
        P.do("vector", lambda e, ti1=ti1, pb=pb: e.tensor_mul(
            yst[:, :], cx.ps(pb, 0, 2048), wrk[:, ti1, :]),
            reads=cx.psl(pb, pb + 1, pb + 2, pb + 3) + [wrk_sl],
            writes=[yst_sl], desc=f"{pr} {which}t2{c}")
        P.do("gpsimd", lambda e: e.tensor_add(scr[:, :], scr[:, :], yst[:, :]),
             reads=[scr_sl, yst_sl], writes=SCR_ALL, desc=f"{pr} {which}t3{c}")
        P.do("vector", lambda e, c=c, dstt=dstt, gb=gb: e.tensor_mul(
            dstt[:, c, :], scr[:, :], cx.ps(gb, 0, 2048)),
            reads=SCR_ALL + cx.psl(gb, gb + 1, gb + 2, gb + 3),
            writes=[dst_sls[c]], desc=f"{pr} {which}rot{c}")
        if which == "k":
            P.do("scalar", lambda e, c=c: e.activation(sq[:, :], kf[:, c, :],
                                                       AF.Abs),
                 reads=[kf_sl[c]], writes=[sq_sl], desc=f"{pr} kabs{c}")
            P.do("vector", lambda e: e.reduce_max(
                bm[:, 0, :], sq.rearrange("p (w u) -> p w u", u=256),
                axis=AX.X),
                reads=[sq_sl], writes=[bm_sl], desc=f"{pr} km{c}")
            P.do("vector", lambda e: e.tensor_max(
                bm[:, 1, 1:8], bm[:, 0, 1:8], bm[:, 0, 0:7]),
                reads=[bm_sl], writes=[bm_sl], desc=f"{pr} km2{c}")
            P.do("vector", lambda e: e.tensor_copy(bm[:, 1, 0:1], bm[:, 0, 0:1]),
                 reads=[bm_sl], writes=[bm_sl], desc=f"{pr} km3{c}")
            for hh in range(2):
                P.do("vector", lambda e, c=c, hh=hh: e.tensor_scalar_mul(
                    kh[:, c, hh, :], bm[:, 1, :],
                    cx.t["c_sel232"].ap()[:, hh:hh + 1]),
                    reads=[bm_sl, cx.csl["c_sel232"]], writes=[kh_sl],
                    desc=f"{pr} km4{c}.{hh}")

    P.do("gpsimd", lambda e: e.memset(asc[64:128, 0:512], 0.0),
         reads=(), writes=[asc_sl], desc=f"{pr} ascz")

    pend = [None]

    def rdst(wp):
        par = wp % 2
        if par == 0:
            return asc[0:1, 0:512], asc_sl
        return rn2[0:1, 0:512], rn_sl

    def emit_recip(c, hh, wp):
        par = wp % 2
        if hh == 0:
            numb = 6 + par
            P.do("vector", lambda e, numb=numb: e.reciprocal(
                asc[64:65, 0:512], cx.ps(numb, 0, 512, (64, 65))),
                reads=cx.psl(numb), writes=[asc_sl],
                desc=f"{pr} rc{c}.{hh}.{wp}")
        else:
            denb = 4 + par
            dst, dsl = rdst(wp)
            P.do("vector", lambda e, denb=denb, dst=dst: e.reciprocal(
                dst, cx.ps(denb, 0, 512, (0, 1))),
                reads=cx.psl(denb), writes=[dsl], desc=f"{pr} rc{c}.{hh}.{wp}")

    def flush_tail():
        if pend[0] is None:
            return
        c, hh, wp, q0 = pend[0]
        pend[0] = None
        par = wp % 2
        denb = 4 + par
        numb = 6 + par
        dst, dsl = rdst(wp)
        if hh == 0:
            P.do("tensor", lambda e, denb=denb: e.matmul(
                cx.ps(denb, 0, 512, (0, 64)),
                cx.t["c_e64"].ap()[64:128, 0:64],
                asc[64:128, 0:512], start=True, stop=True,
                tile_position=(64, 0)),
                reads=[asc_sl, cx.csl["c_e64"]], writes=cx.psl(denb),
                desc=f"{pr} rbA{c}.{wp}")
            P.do("vector", lambda e, denb=denb: e.tensor_copy(
                scr[0:64, 0:512], cx.ps(denb, 0, 512, (0, 64))),
                reads=cx.psl(denb), writes=[scr_sl, sq_sl],
                desc=f"{pr} reA{c}.{wp}")
            P.do("vector", lambda e, c=c, q0=q0, numb=numb: e.tensor_mul(
                xsb[0:64, c, q0:q0 + 512], cx.ps(numb, 0, 512, (0, 64)),
                scr[0:64, 0:512]),
                reads=cx.psl(numb) + [scr_sl, sq_sl],
                writes=[wv_sl[c], xsb_sl[c]], desc=f"{pr} wvA{c}.{wp}")
        else:
            P.do("tensor", lambda e, denb=denb, dst=dst: e.matmul(
                cx.ps(denb, 0, 512, (64, 128)),
                cx.t["c_ones_row"].ap()[0:1, 0:64],
                dst, start=True, stop=True,
                tile_position=(0, 64)),
                reads=[dsl, cx.csl["c_ones_row"]], writes=cx.psl(denb),
                desc=f"{pr} rbB{c}.{wp}")
            P.do("vector", lambda e, denb=denb: e.tensor_copy(
                scr[64:128, 0:512], cx.ps(denb, 0, 512, (64, 128))),
                reads=cx.psl(denb), writes=[scr_sl, sq_sl],
                desc=f"{pr} reB{c}.{wp}")
            P.do("vector", lambda e, c=c, q0=q0, numb=numb: e.tensor_mul(
                xsb[64:128, c, q0:q0 + 512], cx.ps(numb, 0, 512, (64, 128)),
                scr[64:128, 0:512]),
                reads=cx.psl(numb) + [scr_sl, sq_sl],
                writes=[wv_sl[c], xsb_sl[c]], desc=f"{pr} wvB{c}.{wp}")

    av_pend = [None]

    def emit_av():
        if av_pend[0] is None:
            return None
        c, hh, wp, wi, live, par, h = av_pend[0]
        av_pend[0] = None
        denb = 4 + par
        numb = 6 + par
        for ii, k2 in enumerate(live):
            bk, off = divmod(ii, 2)
            off *= 256
            st = (ii == 0)
            sp = (ii == len(live) - 1)
            if hh == 0:
                P.do("tensor", lambda e, k2=k2, bk=bk, off=off,
                     wi=wi, h=h, st=st, sp=sp, numb=numb: e.matmul(
                         cx.ps(numb, wi * 256, 256, (0, 65)),
                         vab[:, k2, h, 0:65],
                         et[:, bk, off:off + 256],
                         start=st, stop=sp, tile_position=(0, 0)),
                     reads=[et_sl, vab_sl[k2]], writes=cx.psl(numb),
                     desc=f"{pr} av{c}.{hh}.{2 * wp + wi}.{k2}")
            else:
                P.do("tensor", lambda e, k2=k2, bk=bk, off=off,
                     wi=wi, h=h, st=st, sp=sp, numb=numb: e.matmul(
                         cx.ps(numb, wi * 256, 256, (64, 128)),
                         vab[:, k2, h, 0:64],
                         et[:, bk, off:off + 256],
                         start=st, stop=sp, tile_position=(0, 64)),
                     reads=[et_sl, vab_sl[k2]], writes=cx.psl(numb),
                     desc=f"{pr} av{c}.{hh}.{2 * wp + wi}.{k2}")
                P.do("tensor", lambda e, k2=k2, bk=bk, off=off,
                     wi=wi, st=st, sp=sp, denb=denb: e.matmul(
                         cx.ps(denb, wi * 256, 256, (0, 1)),
                         cx.t["c_ones_col"].ap(),
                         et[:, bk, off:off + 256],
                         start=st, stop=sp),
                     reads=[et_sl, cx.csl["c_ones_col"]],
                     writes=cx.psl(denb),
                     desc=f"{pr} avd{c}.{hh}.{2 * wp + wi}.{k2}")
        if wi == 1:
            flush_tail()
            emit_recip(c, hh, wp)
            pend[0] = (c, hh, wp, wp * 512)
        return None

    for c in range(NCH):
        P.do("scalar", lambda e, c=c: e.activation(sq[:, 0:2048], qf[:, c, :],
                                                   AF.Abs),
             reads=[qf_sl[c]], writes=[sq_sl], desc=f"{pr} qabs{c}")
        for hh in range(2):
            for w in range(8):
                P.do("tensor", lambda e, c=c, hh=hh, w=w: e.matmul(
                    cx.ps(w // 2, (w % 2) * 256, 256, (0, 1)),
                    kh[:, c, hh, w:w + 1], sq[:, w * 256:(w + 1) * 256],
                    start=True, stop=True),
                    reads=[sq_sl, kh_sl], writes=cx.psl(w // 2),
                    desc=f"{pr} mb{c}.{hh}.{w}")
            for bb in range(4):
                P.do("vector", lambda e, hh=hh, bb=bb: e.tensor_scalar_mul(
                    Mbd[0:1, hh, 2 * bb:2 * bb + 2, :].rearrange(
                        "p w (d q) -> p w d q", d=2),
                    bass.AP(cx.PSt, bb * 512,
                            [[4096, 1], [256, 2], [0, 2], [1, 256]]),
                    -1.0),
                    reads=cx.psl(bb), writes=[mbd_sl, W_sl, sqr_sl],
                    desc=f"{pr} mbe{c}.{hh}.{bb}")
        for hh in range(2):
            h = 2 * c + hh
            p0 = hh * 64
            for wp in range(4):
                par = wp % 2
                for wi in range(2):
                    w = 2 * wp + wi
                    kcs = [2 * w - 2, 2 * w - 1, 2 * w, 2 * w + 1]
                    live = [k2 for k2 in kcs if k2 >= 0]
                    for ii, k2 in enumerate(live):
                        bk, off = divmod(ii, 2)
                        bk += 2 * par
                        off *= 256
                        P.do("tensor", lambda e, k2=k2, bk=bk, off=off, w=w,
                             p0=p0, c=c: e.matmul(
                                 cx.ps(bk, off, 256),
                                 kf[p0:p0 + 64, c, k2 * 128:(k2 + 1) * 128],
                                 qf[p0:p0 + 64, c, w * 256:(w + 1) * 256],
                                 start=True, stop=False,
                                 tile_position=(p0, 0)),
                             reads=[kf_sl[c], qf_sl[c]], writes=cx.psl(bk),
                             desc=f"{pr} s{c}.{hh}.{w}.{k2}")
                        pos = k2 - (2 * w - 2)
                        for qh in range(2):
                            if pos == 0:
                                kind = "tri_prev" if qh == 0 else "big"
                            elif pos == 1:
                                kind = None if qh == 0 else "tri_prev"
                            elif pos == 2:
                                kind = "tri_cur" if qh == 0 else None
                            else:
                                kind = "big" if qh == 0 else "tri_cur"
                            if kind is None:
                                continue
                            if kind == "big":
                                P.do("tensor", lambda e, bk=bk, off=off, qh=qh:
                                     e.matmul(cx.ps(bk, off + qh * 128, 128),
                                              cx.t["c_bigrow"].ap(),
                                              cx.t["c_onesbf"].ap(),
                                              start=False, stop=False),
                                     reads=[cx.csl["c_bigrow"],
                                            cx.csl["c_onesbf"]],
                                     writes=cx.psl(bk),
                                     desc=f"{pr} mkb{c}.{hh}.{w}.{k2}.{qh}")
                            else:
                                pat = ("c_mask_prevT" if kind == "tri_prev"
                                       else "c_mask_curT")
                                P.do("tensor", lambda e, bk=bk, off=off, qh=qh,
                                     pat=pat: e.matmul(
                                         cx.ps(bk, off + qh * 128, 128),
                                         cx.t[pat].ap(), cx.t["c_eye"].ap(),
                                         start=False, stop=False),
                                     reads=[cx.csl[pat], cx.csl["c_eye"]],
                                     writes=cx.psl(bk),
                                     desc=f"{pr} mkt{c}.{hh}.{w}.{k2}.{qh}")
                        P.do("tensor", lambda e, bk=bk, off=off, w=w, hh=hh:
                             e.matmul(cx.ps(bk, off, 256),
                                      cx.t["c_onesbf"].ap(),
                                      Mbd[0:1, hh, w, 0:256],
                                      start=False, stop=True),
                             reads=[mbd_sl, cx.csl["c_onesbf"]],
                             writes=cx.psl(bk),
                             desc=f"{pr} bias{c}.{hh}.{w}.{k2}")
                    emit_av()
                    nbk = (len(live) + 1) // 2
                    for bk in range(nbk):
                        wd = 512 if (len(live) - 2 * bk) >= 2 else 256
                        P.do("scalar", lambda e, bk=bk, wd=wd, par=par:
                             e.activation(et[:, bk, 0:wd],
                                          cx.ps(bk + 2 * par, 0, wd),
                                          AF.Exp, scale=8.0),
                            reads=cx.psl(bk + 2 * par), writes=[et_sl],
                            desc=f"{pr} exp{c}.{hh}.{w}.{bk}")
                    av_pend[0] = (c, hh, wp, wi, live, par, h)
    emit_av()
    flush_tail()

    for qq in range(2):
        P.do("gpsimd", lambda e, l=l, qq=qq: e.dma_start(
            out=Wo[:, 4 * qq:4 * qq + 4, :],
            in_=cx.dram["L1f_Wo"][l].rearrange(
                "(c p) f -> p c f", p=128)[:, 4 * qq:4 * qq + 4, :]),
            reads=(), writes=[Wo_sl, wrk_sl, W_sl, mbd_sl, sqr_sl], dma=True,
            qkey=f"l1f_Wo{qq}", desc=f"{pr} ldWo.{qq}")
    for f in range(NCH):
        pb = 0 if f % 2 == 0 else 4
        P.do("sync", lambda e, f=f: e.dma_start(
            out=yst[:, :], in_=ybuf[b, f * 128:(f + 1) * 128, :]),
            reads=[y_sl[f]], writes=[yst_sl], dma=True, qkey="l1f_yst",
            desc=f"{pr} ldy{f}")
        for tt in range(4):
            for dd in range(NCH):
                P.do("tensor", lambda e, f=f, tt=tt, dd=dd, pb=pb: e.matmul(
                    cx.ps(pb + tt), Wo[:, dd, f * 128:(f + 1) * 128],
                    xsb[:, dd, tt * 512: tt * 512 + 512],
                    start=(dd == 0), stop=(dd == NCH - 1)),
                    reads=[Wo_sl, wv_sl[dd]], writes=cx.psl(pb + tt),
                    desc=f"{pr} womm{f}.{tt}.{dd}")
        P.do("vector", lambda e, f=f, pb=pb: e.tensor_add(
            yst[:, :], yst[:, :], cx.ps(pb, 0, 2048)),
            reads=cx.psl(pb, pb + 1, pb + 2, pb + 3) + [yst_sl],
            writes=[yst_sl], desc=f"{pr} yadd{f}")
        P.do("sync", lambda e, f=f: e.dma_start(
            out=ybuf[b, f * 128:(f + 1) * 128, :], in_=yst[:, :]),
            reads=[yst_sl], writes=[y_sl[f]], dma=True, qkey=f"l1f_y{b}_{f}",
            desc=f"{pr} sty{f}")
    P.barrier()


def _l1f_ffn_alloc(cx):
    cx.phase_begin()
    t = {}
    t["rows"] = cx.palloc("rows", [1, 4 * 512], F32)
    t["W1"] = cx.palloc("f_W1", [128, 8, 5632], BF16).ap()
    t["W2"] = cx.palloc("f_W2", [128, 22, 1024], BF16).ap()
    t["c1"] = cx.palloc("f_c1", [1, 5632], BF16).ap()
    t["g"] = cx.palloc("f_g", [128, 22, 512], BF16).ap()
    t["xs"] = cx.palloc("f_xs", [128, NCH, 512], F32).ap()
    t["xn"] = cx.palloc("f_xn", [128, NCH, 512], BF16).ap()
    t["gel"] = cx.palloc("f_gel", [128, 512], F32, at="rows").ap()
    t["yst"] = cx.palloc("f_yst", [128, 1, 512], F32).ap()
    t["eb"] = cx.palloc("f_eb", [128, 44], F32).ap()
    t["cs"] = cx.palloc("f_cs", [128, 1], BF16).ap()
    t["mr"] = cx.palloc("f_mr", [1, 512], BF16).ap()
    return t


def emit_l1f_ffn_weights(cx, t, l):
    P = cx.P
    sl = P.slot
    pr = f"l1fw{l}"
    W1, W2, c1, eb, cs = t["W1"], t["W2"], t["c1"], t["eb"], t["cs"]
    W1_sl, W2_sl, c1_sl = sl("f_W1"), sl("f_W2"), sl("f_c1")
    eb_sl, cs_sl = sl("f_eb"), sl("f_cs")
    gcols = cx.t["l1f_lng"].ap()
    lnbb = cx.t["l1f_lnbb"].ap()
    bcols = cx.t["l1f_b1c"].ap()
    b2cols = cx.t["l1f_b2c"].ap()
    for qq in range(4):
        P.do("gpsimd", lambda e, l=l, qq=qq: e.dma_start(
            out=W1[:, 2 * qq:2 * qq + 2, :],
            in_=cx.dram["L1f_W1p"][l].rearrange(
                "(c p) f -> p c f", p=128)[:, 2 * qq:2 * qq + 2, :]),
            reads=(), writes=[W1_sl], dma=True, qkey=f"l1f_W1{qq}",
            desc=f"{pr} ldW1.{qq}")
    for qq in range(4):
        P.do("gpsimd", lambda e, l=l, qq=qq: e.dma_start(
            out=W2[:, qq::4, :],
            in_=cx.dram["L1f_W2p"][l].rearrange(
                "(c p) f -> p c f", p=128)[:, qq::4, :]),
            reads=(), writes=[W2_sl], dma=True, qkey=f"l1f_W2{qq}",
            desc=f"{pr} ldW2.{qq}")
    for j in range(44):
        for dd in range(NCH):
            P.do("tensor", lambda e, j=j, dd=dd, l=l: e.matmul(
                cx.ps(0, 0, 1, (0, 128)), W1[:, dd, j * 128:(j + 1) * 128],
                lnbb[:, l, dd:dd + 1],
                start=(dd == 0), stop=(dd == NCH - 1)),
                reads=[W1_sl, cx.csl["l1f_lnbb"]], writes=cx.psl(0),
                desc=f"{pr} c2mm{j}.{dd}")
        P.do("vector", lambda e, j=j, l=l: e.scalar_tensor_tensor(
            eb[:, j:j + 1], cx.ps(0, 0, 1, (0, 128)), 1.0,
            bcols[:, l, j:j + 1], ALU.mult, ALU.add),
            reads=cx.psl(0) + [cx.csl["l1f_b1c"]], writes=[eb_sl],
            desc=f"{pr} eb{j}")
    for dd in range(NCH):
        P.do("vector", lambda e, dd=dd, l=l: e.tensor_scalar_mul(
            W1[:, dd, :], W1[:, dd, :], gcols[:, l, dd:dd + 1]),
            reads=[W1_sl, cx.csl["l1f_lng"]], writes=[W1_sl],
            desc=f"{pr} gfold{dd}")
    for j in range(44):
        for dd in range(NCH):
            P.do("tensor", lambda e, j=j, dd=dd: e.matmul(
                cx.ps(1, 0, 1, (0, 128)), W1[:, dd, j * 128:(j + 1) * 128],
                cx.t["c_ones_col"].ap(), start=(dd == 0), stop=(dd == NCH - 1)),
                reads=[W1_sl, cx.csl["c_ones_col"]], writes=cx.psl(1),
                desc=f"{pr} c1mm{j}.{dd}")
        P.do("vector", lambda e: e.tensor_copy(cs[:, 0:1], cx.ps(1, 0, 1, (0, 128))),
             reads=cx.psl(1), writes=[cs_sl], desc=f"{pr} c1c{j}")
        P.do("tensor", lambda e: e.matmul(
            cx.ps(2, 0, 128, (0, 1)), cs[:, 0:1], cx.t["c_eye"].ap(),
            start=True, stop=True),
            reads=[cs_sl, cx.csl["c_eye"]], writes=cx.psl(2), desc=f"{pr} c1t{j}")
        P.do("vector", lambda e, j=j: e.tensor_copy(
            c1[0:1, j * 128:(j + 1) * 128], cx.ps(2, 0, 128, (0, 1))),
            reads=cx.psl(2), writes=[c1_sl], desc=f"{pr} c1r{j}")


def emit_l1f_ffn(cx, t, l, b, ybuf):
    nc, P = cx.nc, cx.P
    sl = P.slot
    pr = f"l1ff{l}_{b}"
    W1, W2, c1, g, xs, xn = t["W1"], t["W2"], t["c1"], t["g"], t["xs"], t["xn"]
    gel, yst, eb, mr = t["gel"], t["yst"], t["eb"], t["mr"]
    W1_sl, W2_sl, c1_sl = sl("f_W1"), sl("f_W2"), sl("f_c1")
    eb_sl, mr_sl = sl("f_eb"), sl("f_mr")
    g_sl = [sl(f"{pr}_g{j}") for j in range(22)]
    xs_sl = [sl(f"{pr}_xs{c}") for c in range(NCH)]
    xn_sl = [sl(f"{pr}_xn{c}") for c in range(NCH)]
    gel_sl, yst_sl = sl("f_gel"), sl("f_yst")
    y_sl = [sl(f"yb{b}_{c}") for c in range(NCH)]
    b2cols = cx.t["l1f_b2c"].ap()
    R_MEAN, R_MSQ, R_RSTD, R_TMP = 0, 1, 2, 3

    for tq in range(4):
        t0 = tq * 512
        for c in range(NCH):
            if c < 4:
                P.do("gpsimd", lambda e, c=c, t0=t0: e.dma_start(
                    out=xs[:, c, :], in_=ybuf[b, c * 128:(c + 1) * 128, t0:t0 + 512]),
                    reads=[y_sl[c]], writes=[xs_sl[c]], dma=True, qkey=f"l1f_fxs{c}",
                    desc=f"{pr} sh{tq}.{c}")
            elif t0 > 0:
                P.do("gpsimd", lambda e, c=c, t0=t0: e.dma_start(
                    out=xs[:, c, :],
                    in_=ybuf[b, c * 128:(c + 1) * 128, t0 - 1:t0 + 511]),
                    reads=[y_sl[c]], writes=[xs_sl[c]], dma=True, qkey=f"l1f_fxs{c}",
                    desc=f"{pr} sh{tq}.{c}")
            else:
                P.do("gpsimd", lambda e, c=c: e.dma_start(
                    out=xs[:, c, 1:512], in_=ybuf[b, c * 128:(c + 1) * 128, 0:511]),
                    reads=[y_sl[c]], writes=[xs_sl[c]], dma=True, qkey=f"l1f_fxs{c}",
                    desc=f"{pr} sh{tq}.{c}")
                P.do("vector", lambda e, c=c: e.memset(xs[:, c, 0:1], 0.0),
                     reads=(), writes=[xs_sl[c]], desc=f"{pr} shz{c}")
        for c in range(NCH):
            P.do("scalar", lambda e, c=c: e.activation(xn[:, c, :], xs[:, c, :],
                                                       AF.Square),
                 reads=[xs_sl[c]], writes=[xn_sl[c]], desc=f"{pr} sq{tq}.{c}")
            P.do("tensor", lambda e, c=c: e.matmul(
                cx.ps(6, 0, 512, (0, 1)), cx.t["c_ones_row"].ap()[:, 0:1],
                xs[:, c, :], start=(c == 0), stop=(c == NCH - 1)),
                reads=[xs_sl[c], cx.csl["c_ones_row"]], writes=cx.psl(6),
                desc=f"{pr} sum{tq}.{c}")
            P.do("tensor", lambda e, c=c: e.matmul(
                cx.ps(7, 0, 512, (0, 1)), cx.t["c_ones_col"].ap(), xn[:, c, :],
                start=(c == 0), stop=(c == NCH - 1)),
                reads=[xn_sl[c], cx.csl["c_ones_col"]], writes=cx.psl(7),
                desc=f"{pr} ssq{tq}.{c}")
        _rm, _rq, _rr, _rt = (cx.row5(R_MEAN), cx.row5(R_MSQ),
                              cx.row5(R_RSTD), cx.row5(R_TMP))
        P.do("vector", lambda e, _rm=_rm: e.tensor_scalar_mul(
            _rm, cx.ps(6, 0, 512, (0, 1)), 1.0 / D),
            reads=cx.psl(6), writes=[cx.rowsl(R_MEAN)], desc=f"{pr} mean{tq}")
        P.do("vector", lambda e, _rq=_rq: e.tensor_scalar_mul(
            _rq, cx.ps(7, 0, 512, (0, 1)), 1.0 / D),
            reads=cx.psl(7), writes=[cx.rowsl(R_MSQ)], desc=f"{pr} msq{tq}")
        P.do("vector", lambda e: e.tensor_mul(_rt, _rm, _rm),
             reads=[cx.rowsl(R_MEAN)], writes=[cx.rowsl(R_TMP)], desc=f"{pr} m2{tq}")
        P.do("vector", lambda e: e.tensor_sub(_rt, _rq, _rt),
             reads=[cx.rowsl(R_MSQ), cx.rowsl(R_TMP)], writes=[cx.rowsl(R_TMP)],
             desc=f"{pr} var{tq}")
        P.do("scalar", lambda e: e.activation(_rt, _rt, AF.Ln, bias=cx.fcol(EPS)),
             reads=[cx.rowsl(R_TMP), cx.csl["c_fconsts"]],
             writes=[cx.rowsl(R_TMP)], desc=f"{pr} ln{tq}")
        P.do("scalar", lambda e: e.activation(_rr, _rt, AF.Exp, scale=-0.5),
             reads=[cx.rowsl(R_TMP)], writes=[cx.rowsl(R_RSTD)], desc=f"{pr} rs{tq}")
        P.do("vector", lambda e: e.tensor_mul(_rt, _rm, _rr),
             reads=[cx.rowsl(R_MEAN), cx.rowsl(R_RSTD)], writes=[cx.rowsl(R_TMP)],
             desc=f"{pr} mr1{tq}")
        P.do("vector", lambda e: e.tensor_scalar_mul(mr[0:1, :], _rt, -1.0),
             reads=[cx.rowsl(R_TMP)], writes=[mr_sl], desc=f"{pr} mr2{tq}")
        P.do("tensor", lambda e, _rr=_rr: e.matmul(
            cx.ps(6), cx.t["c_ones_row"].ap()[0:1, 0:128], _rr,
            start=True, stop=True),
            reads=[cx.rowsl(R_RSTD), cx.csl["c_ones_row"]], writes=cx.psl(6),
            desc=f"{pr} rbc{tq}")
        for c in range(NCH):
            P.do("vector", lambda e, c=c: e.tensor_mul(xn[:, c, :], xs[:, c, :],
                                                       cx.ps(6)),
                 reads=[xs_sl[c]] + cx.psl(6), writes=[xn_sl[c]],
                 desc=f"{pr} xn{tq}.{c}")
        for jj in range(22):
            jb = (jj % 2) * 2
            for part, pb in ((0, 0), (1, 1)):
                pb += jb
                j = jj + 22 * part
                for dd in range(NCH):
                    P.do("tensor", lambda e, j=j, dd=dd, pb=pb: e.matmul(
                        cx.ps(pb), W1[:, dd, j * 128:(j + 1) * 128], xn[:, dd, :],
                        start=(dd == 0), stop=False),
                        reads=[W1_sl, xn_sl[dd]], writes=cx.psl(pb),
                        desc=f"{pr} w1mm{tq}.{j}.{dd}")
                P.do("tensor", lambda e, j=j, pb=pb: e.matmul(
                    cx.ps(pb), c1[0:1, j * 128:(j + 1) * 128], mr[0:1, :],
                    start=False, stop=True),
                    reads=[c1_sl, mr_sl], writes=cx.psl(pb),
                    desc=f"{pr} corr{tq}.{j}")
            P.do("scalar", lambda e, jj=jj, jb=jb: e.activation(
                gel[:, :], cx.ps(1 + jb), AF.Gelu, bias=eb[:, 22 + jj:23 + jj]),
                reads=cx.psl(1 + jb) + [eb_sl],
                writes=[gel_sl, cx.rowsl(0)],
                desc=f"{pr} gelu{tq}.{jj}")
            P.do("vector", lambda e, jj=jj, jb=jb: e.scalar_tensor_tensor(
                g[:, jj, :], cx.ps(0 + jb), eb[:, jj:jj + 1],
                gel[:, :], ALU.add, ALU.mult),
                reads=cx.psl(0 + jb) + [gel_sl, eb_sl], writes=[g_sl[jj]],
                desc=f"{pr} gmul{tq}.{jj}")
        for do_ in range(NCH):
            pb = 4 if do_ % 2 == 0 else 5
            P.do("sync", lambda e, do_=do_, t0=t0: e.dma_start(
                out=yst[:, 0, :],
                in_=ybuf[b, do_ * 128:(do_ + 1) * 128, t0:t0 + 512]),
                reads=[y_sl[do_]], writes=[yst_sl], dma=True, qkey="l1f_yst",
                desc=f"{pr} ldy{tq}.{do_}")
            for jj in range(22):
                P.do("tensor", lambda e, do_=do_, jj=jj, pb=pb: e.matmul(
                    cx.ps(pb), W2[:, jj, do_ * 128:(do_ + 1) * 128], g[:, jj, :],
                    start=(jj == 0), stop=(jj == 21)),
                    reads=[W2_sl, g_sl[jj]], writes=cx.psl(pb),
                    desc=f"{pr} w2mm{tq}.{do_}.{jj}")
            P.do("vector", lambda e, do_=do_, pb=pb, l=l: e.scalar_tensor_tensor(
                yst[:, 0, :], cx.ps(pb), b2cols[:, l, do_:do_ + 1],
                yst[:, 0, :], ALU.add, ALU.add),
                reads=cx.psl(pb) + [yst_sl, cx.csl["l1f_b2c"]], writes=[yst_sl],
                desc=f"{pr} yev{tq}.{do_}")
            P.do("sync", lambda e, do_=do_, t0=t0: e.dma_start(
                out=ybuf[b, do_ * 128:(do_ + 1) * 128, t0:t0 + 512],
                in_=yst[:, 0, :]),
                reads=[yst_sl], writes=[y_sl[do_]], dma=True,
                qkey=f"l1f_y{b}_{do_}", desc=f"{pr} sty{tq}.{do_}")
    P.barrier()



def l1f_dram(nc, cx):
    dr = {}
    dr["L1f_Wqkv"] = nc.dram_tensor("L1f_Wqkv", [6, D, 3072], F32,
                                    kind="ExternalInput").ap()
    dr["L1f_Wo"] = nc.dram_tensor("L1f_Wo", [6, D, D], F32,
                                  kind="ExternalInput").ap()
    dr["L1f_W1p"] = nc.dram_tensor("L1f_W1p", [6, D, 5632], F32,
                                   kind="ExternalInput").ap()
    dr["L1f_W2p"] = nc.dram_tensor("L1f_W2p", [6, 2816, D], F32,
                                   kind="ExternalInput").ap()
    dr["L1f_rot"] = nc.dram_tensor("L1f_rot", [4, 128, T], F32,
                                   kind="ExternalInput").ap()
    cx.dram.update(dr)
    for nm, sh, dt in [("l1f_qks", [128, 24], F32), ("l1f_lng", [128, 6, 8], F32),
                       ("l1f_lnbb", [128, 6, 8], BF16),
                       ("l1f_b1c", [128, 6, 44], F32),
                       ("l1f_b2c", [128, 6, 8], F32)]:
        t = nc.dram_tensor(nm + "_hbm", [sh[0], int(np.prod(sh[1:]))], F32,
                           kind="ExternalInput")
        cx.dram[nm + "_hbm"] = t.ap()
        cx.galloc(nm, sh, dt)
        cx.csl[nm] = cx.P.slot("const_" + nm)
        cx.P.qfinal["kparam"] = True
        q = "gpsimd" if dt == BF16 else "sync"
        qk = "kparamg" if dt == BF16 else "kparam"
        cx.P.qfinal[qk] = True
        dst = cx.t[nm].ap()
        if len(sh) == 3:
            dst = dst.rearrange("p a b -> p (a b)")
        cx.P.do(q, lambda e, nm=nm, dst=dst: e.dma_start(
            out=dst, in_=cx.dram[nm + "_hbm"]),
            reads=(), writes=[cx.csl[nm]], dma=True, qkey=qk, desc=f"ld {nm}")


def l1f_host_inputs(inputs):
    qks = np.zeros((128, 24), np.float32)
    for l in range(6):
        qs = np.asarray(inputs["L1_ffn_q_scale"][l], np.float32)
        ks = np.asarray(inputs["L1_ffn_k_scale"][l], np.float32)
        qsw = np.concatenate([qs[32:], qs[:32]])
        ksw = np.concatenate([ks[32:], ks[:32]])
        qks[:, 4 * l] = np.concatenate([qs, qs])
        qks[:, 4 * l + 1] = np.concatenate([qsw, qsw])
        qks[:, 4 * l + 2] = np.concatenate([ks, ks])
        qks[:, 4 * l + 3] = np.concatenate([ksw, ksw])
    lng = np.zeros((128, 48), np.float32)
    lnb = np.zeros((128, 48), np.float32)
    b1c = np.zeros((128, 6 * 44), np.float32)
    b2c = np.zeros((128, 48), np.float32)
    for l in range(6):
        lng[:, l * 8:(l + 1) * 8] = col128(inputs["L1_ffn_ln_g"][l])
        lnb[:, l * 8:(l + 1) * 8] = col128(inputs["L1_ffn_ln_b"][l])
        b1 = np.asarray(inputs["L1_ffn_b1"][l], np.float32)
        b1c[:, l * 44:l * 44 + 22] = col128(b1[0:2730], 22)
        b1c[:, l * 44 + 22:l * 44 + 44] = col128(b1[2730:5460], 22)
        b2c[:, l * 8:(l + 1) * 8] = col128(inputs["L1_ffn_b2"][l])
    W1 = np.asarray(inputs["L1_ffn_W1"], np.float32)
    W1p = np.zeros((6, D, 5632), np.float32)
    W1p[:, :, 0:2730] = W1[:, :, 0:2730]
    W1p[:, :, 2816:2816 + 2730] = W1[:, :, 2730:5460]
    W2 = np.asarray(inputs["L1_ffn_W2"], np.float32)
    W2p = np.zeros((6, 2816, D), np.float32)
    W2p[:, 0:2730, :] = W2
    rot = np.stack(host_l1f_rotary(), axis=0)
    return {
        "L1f_Wqkv": _f32(inputs["L1_ffn_Wqkv"]),
        "L1f_Wo": _f32(inputs["L1_ffn_Wo"]),
        "L1f_W1p": _f32(W1p),
        "L1f_W2p": _f32(W2p),
        "L1f_rot": _f32(rot),
        "l1f_qks_hbm": qks,
        "l1f_lng_hbm": lng,
        "l1f_lnbb_hbm": lnb,
        "l1f_b1c_hbm": b1c,
        "l1f_b2c_hbm": b2c,
    }


def emit_l1f_all(cx, xin, xout, layers=6):
    ybuf = cx.dram["ybuf"]
    for b in range(NB):
        emit_l1f_entry(cx, b, xin, ybuf)
    ta = _l1f_attn_alloc(cx)
    tf = _l1f_ffn_alloc(cx)
    for l in range(layers):
        if not L1F_SKIP_ATTN:
            for b in range(NB):
                emit_l1f_attn(cx, ta, l, b, ybuf)
        if not L1F_SKIP_FFN:
            cx.t["rows"] = tf["rows"]
            emit_l1f_ffn_weights(cx, tf, l)
            for b in range(NB):
                emit_l1f_ffn(cx, tf, l, b, ybuf)
    for b in range(NB):
        emit_l1f_exit(cx, b, xin, ybuf, xout)


def build_program_l1f():
    nc = bass.Bass("TRN2", target_bir_lowering=False)
    P = Prog()
    cx = build_common(nc, P)
    cx.dram["x_in"] = nc.dram_tensor("x_in", [NB, D, T], F32,
                                     kind="ExternalInput").ap()
    cx.dram["x_out"] = nc.dram_tensor("x_out", [NB, D, T], F32,
                                      kind="ExternalOutput").ap()
    cx.dram["ybuf"] = nc.dram_tensor("ybuf", [NB, D, T], F32,
                                     kind="Internal").ap()
    for nm, (sh, _dt) in CONST_SHAPES.items():
        cx.dram[nm] = nc.dram_tensor(nm, sh, F32, kind="ExternalInput").ap()
    emit_load_consts(cx)
    l1f_dram(nc, cx)
    emit_l1f_all(cx, cx.dram["x_in"], cx.dram["x_out"], layers=L1F_LAYERS)
    P.do("gpsimd", lambda e: e.nop(),
         reads=[P.slot(f"l1fx{b}_xo{c}") for b in range(NB) for c in range(NCH)],
         desc="final wait")
    finish_program(nc, P)
    return nc, P


def finish_program(nc, P):
    sems = {}
    stack = contextlib.ExitStack()
    for e in ENGINES:
        sems["c_" + e] = stack.enter_context(nc.semaphore("c_" + e))
    for k in P.qcum:
        sems["q_" + k] = stack.enter_context(nc.semaphore("q_" + k))
    with nc.Block() as block:
        P.replay(nc, block, sems)
    stack.close()
    return nc


def l0f_dram(nc, cx, standalone=True):
    dr = {}
    dr["L0_ffn_w1"] = nc.dram_tensor("L0_ffn_w1", [8, D, 4096], F32,
                                     kind="ExternalInput").ap()
    dr["L0_ffn_w2"] = nc.dram_tensor("L0_ffn_w2", [8, 4096, D], F32,
                                     kind="ExternalInput").ap()
    cx.dram.update(dr)
    for nm, sh in [("l0f_b1", [128, 32]), ("l0f_b2", [128, 8]),
                   ("l0f_gamma", [128, 8]), ("l0f_beta", [128, 8])]:
        t = nc.dram_tensor(nm + "_hbm", sh, F32, kind="ExternalInput")
        cx.dram[nm + "_hbm"] = t.ap()
        cx.galloc(nm, sh, F32)
        cx.csl[nm] = cx.P.slot("const_" + nm)
        cx.P.qfinal["kparam"] = True
        cx.P.do("sync", lambda e, nm=nm: e.dma_start(
            out=cx.t[nm].ap(), in_=cx.dram[nm + "_hbm"]),
            reads=(), writes=[cx.csl[nm]], dma=True, qkey="kparam", desc=f"ld {nm}")


def l0f_host_inputs(inputs):
    m = {
        "L0_ffn_w1": _f32(inputs["L0_ffn_w1"]),
        "L0_ffn_w2": _f32(inputs["L0_ffn_w2"]),
        "l0f_b1_hbm": col128(inputs["L0_ffn_b1"][0]),
        "l0f_b2_hbm": col128(inputs["L0_ffn_b2"][0]),
        "l0f_gamma_hbm": col128(inputs["L0_ffn_gamma"]),
        "l0f_beta_hbm": col128(inputs["L0_ffn_beta"]),
    }
    m.update(host_constants())
    return m


def build_program_l0f():
    nc = bass.Bass("TRN2", target_bir_lowering=False)
    P = Prog()
    cx = build_common(nc, P)
    cx.dram["x_in"] = nc.dram_tensor("x_in", [NB, D, T], F32,
                                     kind="ExternalInput").ap()
    cx.dram["x_out"] = nc.dram_tensor("x_out", [NB, D, T], F32,
                                      kind="ExternalOutput").ap()
    cx.dram["tgtf_hbm"] = nc.dram_tensor("tgtf_hbm", [NB, D, T], F32,
                                         kind="Internal").ap()
    for nm, (sh, _dt) in CONST_SHAPES.items():
        cx.dram[nm] = nc.dram_tensor(nm, sh, F32, kind="ExternalInput").ap()
    emit_load_consts(cx)
    l0f_dram(nc, cx)
    for b in range(NB):
        emit_l0f(cx, b, cx.dram["x_in"], cx.dram["x_out"])
    P.do("gpsimd", lambda e: e.nop(),
         reads=[P.slot(f"l0f{b}_xo{c}") for b in range(NB) for c in range(NCH)],
         desc="final wait")
    finish_program(nc, P)
    return nc, P



def build_program_full():
    nc = bass.Bass("TRN2", target_bir_lowering=False)
    P = Prog()
    cx = build_common(nc, P)
    cx.dram["x_in"] = nc.dram_tensor("x_in", [NB, D, T], F32,
                                     kind="ExternalInput").ap()
    cx.dram["x_out"] = nc.dram_tensor("x_out", [NB, D, T], F32,
                                      kind="ExternalOutput").ap()
    xbA = nc.dram_tensor("xbA", [NB, D, T], F32, kind="Internal").ap()
    xbB = nc.dram_tensor("xbB", [NB, D, T], F32, kind="Internal").ap()
    cx.dram["ybuf"] = nc.dram_tensor("ybuf", [NB, D, T], F32,
                                     kind="Internal").ap()
    cx.dram["tgtf_hbm"] = nc.dram_tensor("tgtf_hbm", [NB, D, T], F32,
                                         kind="Internal").ap()
    for nm, (sh, _dt) in CONST_SHAPES.items():
        cx.dram[nm] = nc.dram_tensor(nm, sh, F32, kind="ExternalInput").ap()
    emit_load_consts(cx)
    l0a_dram(nc, cx)
    l0f_dram(nc, cx)
    l1a_dram(nc, cx)
    l1f_dram(nc, cx)
    for b in range(NB):
        emit_l0a(cx, b, cx.dram["x_in"], xbA)
    for b in range(NB):
        emit_l0f(cx, b, xbA, xbB)
    for b in range(NB):
        emit_l1a(cx, b, xbB, xbA)
    emit_l1f_all(cx, xbA, cx.dram["x_out"], layers=L1F_LAYERS)
    P.do("gpsimd", lambda e: e.nop(),
         reads=[P.slot(f"l1fx{b}_xo{c}") for b in range(NB) for c in range(NCH)],
         desc="final wait")
    finish_program(nc, P)
    return nc, P


def host_all_inputs(inputs):
    m = {}
    m.update(l0a_host_inputs(inputs))
    m.update(l0f_host_inputs(inputs))
    m.update(l1a_host_inputs(inputs))
    m.update(l1f_host_inputs(inputs))
    m.update(host_constants())
    return m


def kernel(**inputs):
    x = np.asarray(inputs["L0_attn_x"], np.float32)
    nc_, _P = build_program_full()
    hbase = host_all_inputs(inputs)
    cur = x_to_fm(x)
    in_maps = [dict(hbase, x_in=cur[c]) for c in range(8)]
    res = run_bass_kernel_spmd(nc_, in_maps, core_ids=list(range(8)))
    return fm_to_x([np.asarray(res.results[c]["x_out"], np.float32)
                    for c in range(8)])
